# version 08
# baseline (speedup 1.0000x reference)
import numpy as np
import ml_dtypes
from contextlib import ExitStack

import concourse.bass as bass
import concourse.mybir as mybir
from concourse.bass_utils import run_bass_kernel_spmd

F32 = mybir.dt.float32
BF16 = mybir.dt.bfloat16
I16 = mybir.dt.int16
AF = mybir.ActivationFunctionType
ALU = mybir.AluOpType
AX = mybir.AxisListType

NTOK = 2050
NCH = 17
D = 1024
HID = 4096
NHEAD, HD, NF = 16, 64, 256
TILES = [(0, 512), (512, 512), (1024, 512), (1536, 514)]
SILU_MIN = 0.27846455
LN16 = float(np.log(16.0))
EPS_PHI = float(1e-6 / 16.0)
DN = float(64.0 ** -0.25)

DEBUG = []


def CH(i):
    return 128 if i < 16 else 2


def chunks_of(t0, L):
    out = []
    j = 0
    while 128 * j < L:
        out.append((t0 // 128 + j, min(128, L - 128 * j)))
        j += 1
    return out


class Plan:
    def __init__(self, nc, st):
        self.nc = nc
        self.st = st
        self.q = {e: [] for e in ("T", "V", "S", "G", "Y")}
        self.sems = {}
        self.cnt = {}

    def sem(self, name):
        self.sems[name] = self.st.enter_context(self.nc.semaphore(name))
        self.cnt[name] = 0

    def add(self, e, fn, waits=(), inc=None):
        sems = self.sems
        waits = tuple(w for w in waits if w is not None)
        if e in self.cnt and self.cnt[e] > 0:
            waits = ((e, self.cnt[e]),) + waits
        if inc is not None:
            self.cnt[inc[0]] += inc[1]
        tok = None if inc is None else (inc[0], self.cnt[inc[0]])

        def run(eng, _w=waits, _fn=fn, _inc=inc):
            best = {}
            for s, v in _w:
                if v > best.get(s, -1):
                    best[s] = v
            for s, v in best.items():
                eng.wait_ge(sems[s], v)
            ins = _fn(eng)
            if _inc is not None:
                ins.then_inc(sems[_inc[0]], _inc[1])
        self.q[e].append(run)
        return tok

    def step(self, e, fn, waits=()):
        return self.add(e, fn, waits, inc=(e, 1))

    def dma(self, e, fn, sem, waits=()):
        return self.add(e, fn, waits, inc=(sem, 16))

    def barrier(self):
        toks = []
        for e in ("T", "V", "S", "G"):
            toks.append(self.step(e, lambda eng: eng.nop(),
                                  waits=[(e, self.cnt[e])]))
        dma_waits = [(s, self.cnt[s]) for s in self.sems
                     if s not in ("T", "V", "S", "G", "Y") and self.cnt[s] > 0]
        ytok = self.step("Y", lambda eng: eng.nop(), waits=toks + dma_waits)
        for e in ("T", "V", "S", "G"):
            self.add(e, lambda eng: eng.nop(), waits=[ytok])
        return ytok


def build_program(debug=()):
    nc = bass.Bass("TRN2", target_bir_lowering=False, debug=False)

    x_d = nc.dram_tensor("x0", [NTOK, D], F32, kind="ExternalInput").ap()
    w1_d = nc.dram_tensor("w1", [HID, D], F32, kind="ExternalInput").ap()
    w2_d = nc.dram_tensor("w2", [D, HID], F32, kind="ExternalInput").ap()
    cw0_d = nc.dram_tensor("cw0", [128, 32], F32, kind="ExternalInput").ap()
    cw1_d = nc.dram_tensor("cw1", [128, 32], F32, kind="ExternalInput").ap()
    cw2_d = nc.dram_tensor("cw2", [128, 32], F32, kind="ExternalInput").ap()
    cb_d = nc.dram_tensor("cb", [128, 32], F32, kind="ExternalInput").ap()
    wqkv_d = nc.dram_tensor("wqkv", [D, 3 * D], F32, kind="ExternalInput").ap()
    bqkv_d = nc.dram_tensor("bqkv", [128, 24], F32, kind="ExternalInput").ap()
    bv_d = nc.dram_tensor("bv", [128, 512], F32, kind="ExternalInput").ap()
    wout_d = nc.dram_tensor("wout", [D, D], F32, kind="ExternalInput").ap()
    bout_d = nc.dram_tensor("bout", [128, 512], F32, kind="ExternalInput").ap()
    proj_d = nc.dram_tensor("proj", [NHEAD, NF, HD], F32, kind="ExternalInput").ap()
    emf_d = nc.dram_tensor("emaskf", [128, 2], F32, kind="ExternalInput").ap()
    eml_d = nc.dram_tensor("emaskl", [128, 2], F32, kind="ExternalInput").ap()
    out_d = nc.dram_tensor("xout", [NTOK, D], F32, kind="ExternalOutput").ap()
    dbg = {}
    for name in debug:
        dbg[name] = nc.dram_tensor("dbg_" + name, [NTOK, D], F32,
                                   kind="ExternalOutput").ap()

    wq1s_d = nc.dram_tensor("wq1s", [32, 128, 1024], BF16, kind="Internal").ap()
    wq2s_d = nc.dram_tensor("wq2s", [32, 128, 1024], BF16, kind="Internal").ap()
    wqks_d = nc.dram_tensor("wqks", [16, 128, 1024], BF16, kind="Internal").ap()
    wvs_d = nc.dram_tensor("wvs", [8, 128, 1024], BF16, kind="Internal").ap()
    wos_d = nc.dram_tensor("wos", [8, 128, 1024], BF16, kind="Internal").ap()
    xqs_d = nc.dram_tensor("xqs", [8, 128, NTOK], BF16, kind="Internal").ap()
    ccin_d = nc.dram_tensor("ccin", [65, 16 * NF], F32, kind="Internal").ap()
    ccout_d = nc.dram_tensor("ccout", [65, 16 * NF], F32, kind="Internal").ap()

    identb_d = nc.inline_tensor(np.eye(128).astype(ml_dtypes.bfloat16), name="idb")
    identf_d = nc.inline_tensor(np.eye(128, dtype=np.float32), name="idf")

    st = ExitStack()
    with st:
        def sb(name, shape, dtype=F32):
            return st.enter_context(nc.sbuf_tensor(name, shape, dtype))

        p = Plan(nc, st)
        for s in ("T", "V", "S", "G", "Y",
                  "da", "db", "dc", "dd", "dx", "dg", "ds", "dk", "dq", "dp", "dw"):
            p.sem(s)

        xb = sb("xb", [128, NCH * D])
        ybuf = sb("ybuf", [128, 32 * 514])
        kvloc = ybuf[0:65, 0:16 * NF]
        zr = sb("zr", [128, 2 * 516])
        i16r = sb("i16r", [128, 2 * 516], I16)
        arb = sb("arb", [128, 2 * 516], BF16)
        zrA = sb("zrA", [128, 2 * 512])
        i16A = sb("i16A", [128, 2 * 512], I16)
        arbA = sb("arbA", [128, 2 * 512], BF16)
        sqv = zrA[:, 0:1024]
        accr = zr[:, 0:516]
        mxv = sb("mxv", [128, 2 * 516])
        s2b = sb("s2b", [128, 516])
        srep = sb("sreps", [128, 128])
        bcsrc = srep[:, 0:1]
        identf = sb("identfs", [128, 128], F32)
        ones5 = sb("ones5", [128, 512], F32)
        allones = ones5[:, 0:128]
        onesw = ones5
        bvrow = sb("bvrows", [128, 512])
        PRM_N = ("ssum ssq smax smin pm pe2 pmm prstd pmabs pmyc psa pA pB "
                 "pr1 aB").split()
        prmt = sb("prmt", [128, len(PRM_N) * NCH])
        prm = {n: prmt[:, j * NCH:(j + 1) * NCH] for j, n in enumerate(PRM_N)}
        sml = sb("sml", [128, 64])
        s2m = sml[:, 0:5]; s2M = sml[:, 5:10]; s2v = sml[:, 10:15]
        s2vr = sml[:, 15:20]; hr2 = sml[:, 20:25]
        mw1o127 = sml[:, 25:26]; mw2o127h = sml[:, 26:27]
        sw1 = sml[:, 27:28]; sw2 = sml[:, 28:29]; c256 = sml[:, 29:30]
        negmx = sml[:, 30:32]; negmx2 = sml[:, 32:34]
        dent = sml[:, 34:35]; recr = sml[:, 35:36]
        emf = sml[:, 36:38]; eml = sml[:, 38:40]
        bqkv = sml[:, 40:64]
        smlb = sb("smlb", [128, 32])
        wsum = smlb[:, 0:32]
        bqkvDN = sb("bqkvDN", [128, 16])

        xqT = sb("xqT", [128, 2 * 8 * 516], BF16)
        xqT3 = [xqT[:, 0:8 * 516].rearrange("p (k c) -> p k c", k=8),
                xqT[:, 8 * 516:16 * 516].rearrange("p (k c) -> p k c", k=8)]
        hqr = sb("hqr", [128, 2 * 514], BF16)
        projTc = sb("projTc", [128, 8 * 256], BF16)
        identb = sb("identb", [128, 128], BF16)
        w1r = sb("w1r", [128, 2 * 1024], BF16)
        w2r = sb("w2r", [128, 2 * 1024], BF16)
        diagr = sb("diagr", [128, 2 * 128], BF16)
        hqmini = diagr[:, 0:64]
        xqrr = sb("xqrr", [128, 2 * 512], BF16)
        vtm = sb("vtm", [128, 5 * 1040], BF16)
        frb = sb("frb", [128, 1024], BF16)
        wqkr = w1r
        wvr = w1r
        xstg = arb
        xqld = xqT[:, 4128:6176]
        xkr = w2r[:, 0:520]
        kpr = w2r[:, 1024:1536]
        kvfm0 = xqT[:, 6176:7216]
        kvfm1 = xqT[:, 7216:8256]
        kvb = hqr[:, 0:256]
        qpr = w2r[:, 0:512]
        qfm = w2r[:, 512:1024]
        atile = xqT[:, 0:1024]
        aTl = xqT[:, 1040:2064]
        woutp = [vtm[:, 0:1024], vtm[:, 1040:2064], vtm[:, 2080:3104],
                 vtm[:, 3120:4144], vtm[:, 4160:5184],
                 w2r[:, 1024:2048], hqr[:, 0:1024], xqT[:, 2080:3104]]
        vstg = hqr[:, 0:520]

        psA = st.enter_context(nc.psum_tensor("psA", [128, 1024], F32))
        ps1a = st.enter_context(nc.psum_tensor("ps1a", [128, 1024], F32))
        ps1b = st.enter_context(nc.psum_tensor("ps1b", [128, 1024], F32))

        x16 = x_d[0:2048, :].rearrange("(c p) d -> p c d", p=128)
        xb16 = xb[:, 0:16 * D].rearrange("p (c d) -> p c d", c=16)
        p.dma("Y", lambda y: y.dma_start(out=xb16, in_=x16), "dx")
        p.dma("Y", lambda y: y.dma_start(
            out=xb[0:2, 16 * D:17 * D], in_=x_d[2048:2050, :]), "dx")
        cw0 = sb("cw0s", [128, 32]); cw1 = sb("cw1s", [128, 32])
        cw2 = sb("cw2s", [128, 32]); cb = sb("cbs", [128, 32])
        for buf, srcd in ((cw0, cw0_d), (cw1, cw1_d), (cw2, cw2_d), (cb, cb_d),
                          (bqkv, bqkv_d), (emf, emf_d), (eml, eml_d),
                          (identb, identb_d.ap()), (identf, identf_d.ap())):
            p.dma("Y", lambda y, b=buf, s=srcd: y.dma_start(out=b[:], in_=s[:, :]),
                  "dx")
        t_load = ("dx", p.cnt["dx"])

        p.step("V", lambda v: v.memset(prmt[:], 0.0))
        tm1 = p.step("V", lambda v: v.memset(sml[:, 0:36], 0.0))
        p.step("V", lambda v: v.memset(smlb[:], 0.0))
        p.step("V", lambda v: v.memset(ones5[:], 1.0))
        t_clr = p.step("V", lambda v: v.memset(srep[:], 0.0))
        t_c256 = p.step("V", lambda v: v.memset(c256[:], 256.0), waits=[tm1])
        p.barrier()
        t_bqdn = p.step("V", lambda v: v.tensor_scalar(
            bqkvDN[:], bqkv[:, 0:16], DN, None, ALU.mult), waits=[t_load])

        def bcast_scalar(dst, waits):
            tm = p.step("T", lambda t: t.matmul(psA[:, 0:1], allones[:], bcsrc[:],
                                                start=True, stop=True), waits=waits)
            return p.step("V", lambda v, d=dst: v.tensor_copy(d[:], psA[:, 0:1]),
                          waits=[tm])

        wtile = ybuf[:, 0:8192]
        wqfb = xqT[:, 0:4096]
        wstg = xqT[:, 4160:8256]
        wkstg = vtm[:, 0:3072]

        def mean_abs(w_src, ntiles, rowlen, dst, t_start=()):
            tws = []
            for t in range(ntiles):
                sl = (t % 2) * 4096
                tdw = p.dma("Y", lambda y, t=t, sl=sl, rowlen=rowlen: y.dma_start(
                    out=wtile[:, sl:sl + rowlen],
                    in_=w_src[t * 128:(t + 1) * 128, :]),
                    "da" if t % 2 == 0 else "db",
                    waits=[tws[t - 2]] if t >= 2 else list(t_start))
                tws.append(p.step("V", lambda v, t=t, sl=sl, rowlen=rowlen:
                    v.reduce_sum(wsum[:, t:t + 1], wtile[:, sl:sl + rowlen],
                                 AX.X, apply_absolute_value=True), waits=[tdw]))
            t1 = p.step("V", lambda v, n=ntiles: v.reduce_sum(
                srep[:, 1:2], wsum[:, 0:n], AX.X), waits=[tws[-1], t_clr])
            t2 = p.step("T", lambda t: t.matmul(
                psA[0:1, 0:1], allones[:, 0:1], srep[:, 1:2],
                start=True, stop=True), waits=[t1])
            t3 = p.step("V", lambda v: v.tensor_copy(bcsrc[0:1, 0:1],
                                                     psA[0:1, 0:1]), waits=[t2])
            t4 = p.step("V", lambda v, n=ntiles, rl=rowlen: v.tensor_scalar(
                bcsrc[0:1, 0:1], bcsrc[0:1, 0:1], 1.0 / (n * 128 * rl),
                None, ALU.mult), waits=[t3])
            tb = bcast_scalar(dst, [t4])
            return p.step("V", lambda v, d=dst: v.tensor_scalar(
                d[:], d[:], 1e-5, None, ALU.max), waits=[tb])

        t_m1 = mean_abs(w1_d, 32, D, mw1o127)
        t_m2 = mean_abs(w2_d, 8, HID, mw2o127h, t_start=[t_m1])
        t_s1 = p.step("V", lambda v: v.reciprocal(sw1[:], mw1o127[:]), waits=[t_m1])
        t_s2w = p.step("V", lambda v: v.reciprocal(sw2[:], mw2o127h[:]), waits=[t_m2])
        t_m1b = p.step("V", lambda v: v.tensor_scalar(
            mw1o127[:], mw1o127[:], 1.0 / 127.0, None, ALU.mult), waits=[t_s1])
        t_m2b = p.step("V", lambda v: v.tensor_scalar(
            mw2o127h[:], mw2o127h[:], 0.5 / 127.0, None, ALU.mult), waits=[t_s2w])

        def quant_w(w_src, ntiles, rowlen, s_w, dst_fn, t_start, psP0,
                    t_wstg_start=()):
            t_ev = None
            t_lastz = {}
            t_out = {}
            t_tts = {}
            tq0 = None
            for t in range(ntiles):
                sl = (t % 2) * 4096
                tdw = p.dma("Y", lambda y, t=t, sl=sl, rowlen=rowlen: y.dma_start(
                    out=wtile[:, sl:sl + rowlen],
                    in_=w_src[t * 128:(t + 1) * 128, :]),
                    "da" if t % 2 == 0 else "db",
                    waits=[t_lastz.get(t - 2)] if t >= 2 else t_start)
                for h in range(rowlen // 512):
                    tz = p.step("S", lambda s, sl=sl, h=h: s.activation(
                        zr[:, 0:512], wtile[:, sl + 512 * h:sl + 512 * h + 512],
                        AF.Identity, bias=c256[:, 0:1], scale=s_w[:, 0:1]),
                        waits=[tdw, tq0, t_c256]
                        + (list(t_start) if (t == 0 and h == 0) else []))
                    t_lastz[t] = tz
                    p.step("V", lambda v: v.tensor_copy(
                        i16r[:, 0:512], zr[:, 0:512]), waits=[tz])
                    p.step("V", lambda v: v.scalar_tensor_tensor(
                        arb[:, 0:512], zr[:, 0:512], -0.5, i16r[:, 0:512],
                        ALU.add, ALU.is_ge))
                    p.step("V", lambda v: v.scalar_tensor_tensor(
                        zr[:, 520:1032], i16r[:, 0:512], -256.0, arb[:, 0:512],
                        ALU.add, ALU.add))
                    tq0 = p.step("V", lambda v, h=h: v.tensor_scalar(
                        wqfb[:, 512 * h:512 * h + 512], zr[:, 520:1032],
                        1.0, -1.0, ALU.min, ALU.max),
                        waits=[t_tts.get(t - 1)] if h == 0 else [])
                for g in range(rowlen // 1024):
                    tts = None
                    for k in range(8):
                        tts = p.step("T", lambda t_, g=g, k=k: t_.transpose(
                            psP0[:, 128 * k:128 * k + 128],
                            wqfb[:, 1024 * g + 128 * k:1024 * g + 128 * k + 128],
                            identb[:]), waits=[tq0, t_ev] if k == 0 else [])
                    t_ev = p.step("V", lambda v, g=g: v.tensor_copy(
                        wstg[:, 1024 * g:1024 * g + 1024], psP0[:, 0:1024]),
                        waits=[tts, t_out.get(t - 1)]
                        + (list(t_wstg_start) if (t == 0 and g == 0) else []))
                t_tts[t] = tts
                t_out[t] = dst_fn(t, t_ev)
            return t_ev

        def w1_out(t, t_ev):
            return p.dma("Y", lambda y, t=t: y.dma_start(
                out=wq1s_d[t, :, :], in_=wstg[:, 0:1024]), "ds", waits=[t_ev])

        def w2_out(t, t_ev):
            src = wstg[:, 0:4096].rearrange("p (c j) -> p c j", c=32)
            dst = wq2s_d[:, :, 128 * t:128 * t + 128].rearrange("c p j -> p c j")
            return p.dma("Y", lambda y, s=src, d=dst: y.dma_start(out=d, in_=s),
                         "ds", waits=[t_ev])

        with nc.psum_tensor("psP0", [128, 1024], BF16) as psP0:
            tq1 = quant_w(w1_d, 32, D, sw1, w1_out, [t_s1, t_s2w], psP0)
            t_w1dma = ("ds", p.cnt["ds"])
            tq2 = quant_w(w2_d, 8, HID, sw2, w2_out, [tq1], psP0,
                          t_wstg_start=[t_w1dma, tq1])
        t_wq_dma = ("ds", p.cnt["ds"])

        t_wk = None
        for dch in range(8):
            tg = p.dma("G", lambda g, d=dch: g.dma_start(
                out=wkstg[:, :], in_=wqkv_d[128 * d:128 * d + 128, :]), "dg",
                waits=[t_wk] if t_wk else [])
            srcqk = wkstg[:, 0:2048].rearrange("p (c j) -> p c j", c=16)
            dstqk = wqks_d[:, :, 128 * dch:128 * dch + 128].rearrange("c p j -> p c j")
            p.dma("G", lambda g, s=srcqk, d2=dstqk: g.dma_start(out=d2, in_=s), "dg",
                  waits=[tg])
            srcv = wkstg[:, 2048:3072].rearrange("p (c j) -> p c j", c=8)
            dstv = wvs_d[:, :, 128 * dch:128 * dch + 128].rearrange("c p j -> p c j")
            t_wk = p.dma("G", lambda g, s=srcv, d2=dstv: g.dma_start(out=d2, in_=s),
                         "dg", waits=[tg])
        for b in range(8):
            tg = p.dma("G", lambda g, b=b: g.dma_start(
                out=wkstg[:, 0:1024], in_=wout_d[128 * b:128 * b + 128, :]), "dg",
                waits=[t_wk])
            t_wk = p.dma("G", lambda g, b=b: g.dma_start(
                out=wos_d[b, :, :], in_=wkstg[:, 0:1024]), "dg", waits=[tg])
        t_wk_done = ("dg", p.cnt["dg"])

        def emit_stats(ln_only, t_pre):
            tS = None
            tclr0 = p.step("V", lambda v: v.memset(prmt[:], 0.0),
                           waits=[t_pre])
            for i in range(NCH):
                ch = CH(i)
                c0 = D * i
                p.step("V", lambda v, c0=c0, ch=ch, i=i: v.reduce_sum(
                    prm["ssum"][0:ch, i:i + 1], xb[0:ch, c0:c0 + 1024], AX.X),
                    waits=[t_pre, tclr0] if i == 0 else [])
                tS = p.step("S", lambda s, c0=c0, ch=ch, i=i: s.activation(
                    sqv[0:ch, 0:1024], xb[0:ch, c0:c0 + 1024], AF.Square,
                    accum_out=prm["ssq"][0:ch, i:i + 1]),
                    waits=[t_pre, tclr0] if i == 0 else [])
                if not ln_only:
                    p.step("V", lambda v, c0=c0, ch=ch, i=i: v.reduce_max(
                        prm["smax"][0:ch, i:i + 1], xb[0:ch, c0:c0 + 1024], AX.X))
                    p.step("V", lambda v, c0=c0, ch=ch, i=i: v.tensor_reduce(
                        prm["smin"][0:ch, i:i + 1], xb[0:ch, c0:c0 + 1024],
                        AX.X, ALU.min))
            return tS

        def emit_params(quant, t_stats):
            P_ = prm
            p.step("V", lambda v: v.tensor_scalar(
                P_["pm"][:], P_["ssum"][:], 1.0 / D, None, ALU.mult),
                waits=[t_stats])
            p.step("V", lambda v: v.tensor_scalar(
                P_["pe2"][:], P_["ssq"][:], 1.0 / D, None, ALU.mult))
            p.step("V", lambda v: v.tensor_tensor(
                P_["pmm"][:], P_["pm"][:], P_["pm"][:], ALU.mult))
            p.step("V", lambda v: v.tensor_tensor(
                P_["ssq"][:], P_["pe2"][:], P_["pmm"][:], ALU.subtract))
            p.step("V", lambda v: v.tensor_scalar(
                P_["pe2"][:], P_["ssq"][:], 1e-5, None, ALU.add))
            tu = p.step("V", lambda v: v.reciprocal(P_["pmm"][:], P_["pe2"][:]))
            tr = p.step("S", lambda s: s.activation(
                P_["prstd"][:], P_["pmm"][:], AF.Sqrt), waits=[tu])
            if not quant:
                p.step("V", lambda v: v.tensor_tensor(
                    P_["pe2"][:], P_["pm"][:], P_["prstd"][:], ALU.mult),
                    waits=[tr])
                tb2 = p.step("V", lambda v: v.tensor_scalar(
                    P_["aB"][:], P_["pe2"][:], -1.0, None, ALU.mult))
                return tb2, tr
            p.step("V", lambda v: v.tensor_tensor(
                P_["ssum"][:], P_["smax"][:], P_["pm"][:], ALU.subtract))
            p.step("V", lambda v: v.tensor_tensor(
                P_["pe2"][:], P_["pm"][:], P_["smin"][:], ALU.subtract))
            p.step("V", lambda v: v.tensor_tensor(
                P_["pmabs"][:], P_["ssum"][:], P_["pe2"][:], ALU.max))
            p.step("V", lambda v: v.tensor_tensor(
                P_["ssq"][:], P_["prstd"][:], P_["pmabs"][:], ALU.mult), waits=[tr])
            p.step("V", lambda v: v.tensor_scalar(
                P_["pmyc"][:], P_["ssq"][:], 1e-5, None, ALU.max))
            p.step("V", lambda v: v.reciprocal(P_["ssum"][:], P_["pmyc"][:]))
            p.step("V", lambda v: v.tensor_scalar(
                P_["psa"][:], P_["ssum"][:], 127.0, None, ALU.mult))
            p.step("V", lambda v: v.tensor_tensor(
                P_["pA"][:], P_["prstd"][:], P_["psa"][:], ALU.mult))
            p.step("V", lambda v: v.tensor_tensor(
                P_["pe2"][:], P_["pm"][:], P_["pA"][:], ALU.mult))
            p.step("V", lambda v: v.tensor_scalar(
                P_["pB"][:], P_["pe2"][:], -1.0, 256.0, ALU.mult, ALU.add))
            tpr = p.step("V", lambda v: v.tensor_scalar(
                P_["pr1"][:], P_["pmyc"][:], mw1o127[:, 0:1], None, ALU.mult),
                waits=[t_m1b])
            return tpr, tr

        def emit_F(tag, t_pre):
            t_stats = emit_stats(False, t_pre)
            t_pp, t_rstd = emit_params(True, t_stats)

            def stage_A(ti_, T0, L, slot, t_free, t_psA_free):
                toks = []
                tpz = p.step("V", lambda v, slot=slot: v.memset(
                    xqT[:, 8 * 516 * slot:8 * 516 * (slot + 1)], 0.0),
                    waits=t_free + [t_pp])
                lo = max(0, T0 - 1)
                hi = min(NTOK, T0 + L + 1)
                ci0, ci1 = lo // 128, (hi - 1) // 128
                t_chain = None
                t_ev = {}
                for ci in range(ci0, ci1 + 1):
                    ch = CH(ci)
                    idx = ci - ci0
                    t_dg = p.step("V", lambda v, ci=ci, idx=idx: v.tensor_scalar(
                        diagr[:, (idx % 2) * 128:(idx % 2) * 128 + 128],
                        identb[:], prm["pr1"][:, ci:ci + 1], None, ALU.mult),
                        waits=[t_pp, t_ev.get(2 * idx - 4)])
                    for h in range(2):
                        hx = 2 * idx + h
                        zs = (hx % 2) * 512
                        tz = p.step("S", lambda s, ci=ci, h=h, zs=zs, ch=ch:
                            s.activation(
                                zrA[0:ch, zs:zs + 512],
                                xb[0:ch, D * ci + 512 * h:D * ci + 512 * h + 512],
                                AF.Identity, scale=prm["pA"][0:ch, ci:ci + 1],
                                bias=prm["pB"][0:ch, ci:ci + 1]),
                            waits=[t_pp, t_rstd, t_chain, tpz])
                        p.step("V", lambda v, zs=zs, ch=ch: v.tensor_copy(
                            i16A[0:ch, zs:zs + 512], zrA[0:ch, zs:zs + 512]),
                            waits=[tz])
                        p.step("V", lambda v, zs=zs, ch=ch:
                            v.scalar_tensor_tensor(
                                arbA[0:ch, zs:zs + 512], zrA[0:ch, zs:zs + 512],
                                -0.5, i16A[0:ch, zs:zs + 512],
                                ALU.add, ALU.is_ge))
                        t_chain = p.step("V", lambda v, zs=zs, ch=ch:
                            v.scalar_tensor_tensor(
                                xqrr[0:ch, zs:zs + 512],
                                i16A[0:ch, zs:zs + 512], -256.0,
                                arbA[0:ch, zs:zs + 512], ALU.add, ALU.add))
                        tts = None
                        for kk in range(4):
                            k = 4 * h + kk
                            tts = p.step("T", lambda t, idx=idx, k=k, kk=kk,
                                         ch=ch, zs=zs:
                                t.matmul(psA[:, 128 * k:128 * k + ch],
                                         xqrr[0:ch, zs + 128 * kk:zs + 128 * kk + 128],
                                         diagr[0:ch, (idx % 2) * 128:
                                               (idx % 2) * 128 + ch],
                                         start=True, stop=True),
                                waits=[t_chain, t_dg, t_ev.get(hx - 2)]
                                + t_psA_free if kk == 0 else [])
                        glo = max(lo, 128 * ci)
                        ghi = min(hi, 128 * ci + ch)
                        if ghi <= glo:
                            continue
                        rc0 = glo - (T0 - 1)
                        pc0 = glo - 128 * ci
                        n = ghi - glo
                        t_ev[hx] = p.step("V", lambda v, h=h, rc0=rc0, pc0=pc0,
                                          n=n, slot=slot:
                            v.tensor_copy(
                                xqT3[slot][:, 4 * h:4 * h + 4, rc0:rc0 + n],
                                psA[:, 0:1024]
                                .rearrange("p (k c) -> p k c", k=8)
                                [:, 4 * h:4 * h + 4, pc0:pc0 + n]),
                            waits=[tts, tpz])
                        toks.append(t_ev[hx])
                return toks

            t_tapC = {}
            t_mm1 = {}
            t_sA = {}
            t_sA[0] = stage_A(0, TILES[0][0], TILES[0][1], 0,
                              [t_wq_dma], [])
            t_hqm = None
            t_mm2_l = None
            t_ev2 = []
            for ti_, (T0, L) in enumerate(TILES):
                slot = ti_ % 2
                tcs = chunks_of(T0, L)
                if ti_ + 1 < len(TILES):
                    t_sA[ti_ + 1] = stage_A(
                        ti_ + 1, TILES[ti_ + 1][0], TILES[ti_ + 1][1],
                        1 - slot,
                        [t_wq_dma, t_mm1.get((ti_ - 1, 31))],
                        list(t_ev2))
                t_sil = {}
                t_mx = None
                t_z2 = {}
                t_hq = {}
                for c in range(32):
                    cslot = c % 2
                    ps1 = ps1a if cslot == 0 else ps1b
                    t_w1 = p.dma("Y", lambda y, c=c, cslot=cslot: y.dma_start(
                        out=w1r[:, 1024 * cslot:1024 * cslot + 1024],
                        in_=wq1s_d[c, :, :]),
                        "da" if cslot == 0 else "db",
                        waits=[t_wq_dma if (ti_ == 0 and c < 2 and tag == "f1")
                               else None,
                               t_mm1.get((ti_, c - 2)),
                               t_mm1.get((ti_ - 1, 30 + c)) if c < 2 else None])
                    tml = None
                    for k in range(8):
                        tml = p.step("T", lambda t, k=k, cslot=cslot, ps1=ps1,
                                     slot=slot:
                            t.matmul(ps1[:, 0:512],
                                     w1r[:, 1024 * cslot + 128 * k:
                                         1024 * cslot + 128 * k + 128],
                                     xqT3[slot][:, k, 0:512],
                                     start=(k == 0), stop=(k == 7)),
                            waits=[t_w1, t_tapC.get((ti_, c - 2)),
                                   (t_ev2[-1] if (c < 2 and t_ev2) else None)]
                            + (t_sA[ti_] if c == 0 else []) if k == 0 else [])
                    for k in range(8):
                        tml = p.step("T", lambda t, k=k, cslot=cslot, L=L,
                                     ps1=ps1, slot=slot:
                            t.matmul(ps1[:, 512:L + 2],
                                     w1r[:, 1024 * cslot + 128 * k:
                                         1024 * cslot + 128 * k + 128],
                                     xqT3[slot][:, k, 512:L + 2],
                                     start=(k == 0), stop=(k == 7)))
                    t_mm1[(ti_, c)] = tml
                    tA_ = p.step("S", lambda s, c=c, L=L, ps1=ps1: s.activation(
                        accr[:, 0:L], ps1[:, 0:L], AF.Identity,
                        scale=cw0[:, c:c + 1], bias=cb[:, c:c + 1]),
                        waits=[tml])
                    tB_ = p.step("V", lambda v, c=c, L=L, ps1=ps1:
                        v.scalar_tensor_tensor(
                            accr[:, 0:L], ps1[:, 1:L + 1], cw1[:, c:c + 1],
                            accr[:, 0:L], ALU.mult, ALU.add), waits=[tA_])
                    t_tapC[(ti_, c)] = p.step("V", lambda v, c=c, L=L, ps1=ps1:
                        v.scalar_tensor_tensor(
                            accr[:, 0:L], ps1[:, 2:L + 2], cw2[:, c:c + 1],
                            accr[:, 0:L], ALU.mult, ALU.add))
                    t_sil[c] = p.step("S", lambda s, c=c, L=L: s.activation(
                        ybuf[:, 514 * c:514 * c + L], accr[:, 0:L], AF.Silu),
                        waits=[t_tapC[(ti_, c)]])
                    if c == 0:
                        t_mx = p.step("V", lambda v, L=L: v.tensor_copy(
                            mxv[:, 0:L], ybuf[:, 0:L]), waits=[t_sil[0]])
                    else:
                        t_mx = p.step("V", lambda v, c=c, L=L: v.tensor_tensor(
                            mxv[:, 516 * (c % 2):516 * (c % 2) + L],
                            ybuf[:, 514 * c:514 * c + L],
                            mxv[:, 516 * ((c - 1) % 2):516 * ((c - 1) % 2) + L],
                            ALU.max), waits=[t_sil[c]])
                mxo = 516
                t_red = []
                for q, (ci, chq) in enumerate(tcs):
                    t_tr = p.step("T", lambda t, q=q, chq=chq: t.transpose(
                        psA[0:chq, 128 * q:128 * q + 128],
                        mxv[:, mxo + 128 * q:mxo + 128 * q + chq], identf[:]),
                        waits=([t_mx, (t_ev2[-1] if t_ev2 else None)]
                               + t_sA.get(ti_ + 1, []))
                        if q == 0 else [t_red[q - 1]])
                    t_red.append(p.step("V", lambda v, q=q, chq=chq: v.reduce_max(
                        s2m[0:chq, q:q + 1], psA[0:chq, 128 * q:128 * q + 128],
                        AX.X), waits=[t_tr]))
                tM = p.step("V", lambda v: v.tensor_scalar(
                    s2M[:], s2m[:], SILU_MIN, None, ALU.max), waits=[t_red[-1]])
                p.step("V", lambda v: v.reciprocal(s2vr[:], s2M[:]))
                tsv = p.step("V", lambda v: v.tensor_scalar(
                    s2v[:], s2vr[:], 127.0, None, ALU.mult))
                th2 = p.step("V", lambda v: v.tensor_scalar(
                    hr2[:], s2M[:], mw2o127h[:, 0:1], None, ALU.mult),
                    waits=[tM, t_m2b])
                t_s2b = None
                for q, (ci, chq) in enumerate(tcs):
                    t1 = p.step("V", lambda v, q=q: v.tensor_scalar(
                        srep[:], allones[:], s2v[:, q:q + 1], None, ALU.mult),
                        waits=[tsv, t_s2b])
                    t2 = p.step("T", lambda t, q=q: t.transpose(
                        psA[:, 128 * q:128 * q + 128], srep[:], identf[:]),
                        waits=[t1, t_red[q]])
                    t_s2b = p.step("V", lambda v, q=q, chq=chq: v.tensor_copy(
                        s2b[:, 128 * q:128 * q + chq],
                        psA[:, 128 * q:128 * q + chq]), waits=[t2])
                if ti_ == 0 and tag == "f1":
                    pass
                t_w2p = {}
                t_mm2 = {}
                for c in range(32):
                    zs = 516 * (c % 2)
                    hs = 514 * (c % 2)
                    tz2 = p.step("G", lambda g, c=c, L=L, zs=zs: g.tensor_tensor(
                        zr[:, zs:zs + L], ybuf[:, 514 * c:514 * c + L],
                        s2b[:, 0:L], ALU.mult),
                        waits=[t_s2b, t_sil[c], t_hq.get(c - 2)])
                    ti2 = p.step("V", lambda v, L=L, zs=zs: v.tensor_scalar(
                        i16r[:, zs:zs + L], zr[:, zs:zs + L], 256.0, None,
                        ALU.add), waits=[tz2, t_hq.get(c - 2)])
                    ta2 = p.step("V", lambda v, L=L, zs=zs:
                        v.scalar_tensor_tensor(
                            arb[:, zs:zs + L], zr[:, zs:zs + L], 255.5,
                            i16r[:, zs:zs + L], ALU.add, ALU.is_ge))
                    t_hq[c] = p.step("V", lambda v, c=c, L=L, zs=zs, hs=hs:
                        v.scalar_tensor_tensor(
                            hqr[:, hs:hs + L], i16r[:, zs:zs + L], -256.0,
                            arb[:, zs:zs + L], ALU.add, ALU.add),
                        waits=[ta2, t_mm2.get(c - 2)])
                    if L > 512:
                        t_hqm = p.step("V", lambda v, c=c, hs=hs: v.tensor_copy(
                            hqmini[:, 2 * c:2 * c + 2], hqr[:, hs + 512:hs + 514]),
                            waits=[t_hq[c]])
                    cslot = c % 2
                    t_w2p[c] = p.dma("Y", lambda y, c=c, cslot=cslot: y.dma_start(
                        out=w2r[:, 1024 * cslot:1024 * cslot + 1024],
                        in_=wq2s_d[c, :, :]),
                        "dc" if cslot == 0 else "dd",
                        waits=[t_mm2.get(c - 2),
                               t_wq_dma if (ti_ == 0 and c < 2 and tag == "f1")
                               else None,
                               (t_mm2_l if (c < 2 and ti_ > 0) else None)])
                    tml = None
                    for j in range(min(4, len(tcs))):
                        psv = [ps1a, ps1b, psA, ps2x][j]
                        for nh in range(2):
                            tml = p.step("T", lambda t, c=c, j=j, cslot=cslot,
                                         psv=psv, hs=hs, nh=nh:
                                t.matmul(psv[:, 512 * nh:512 * nh + 512],
                                         hqr[:, hs + 128 * j:hs + 128 * j + 128],
                                         w2r[:, 1024 * cslot + 512 * nh:
                                             1024 * cslot + 512 * nh + 512],
                                         start=(c == 0), stop=(c == 31)),
                                waits=([t_hq[c], t_w2p[c]]
                                       + ([t_tapC[(ti_, 31)], t_s2b]
                                          + t_ev2 if c == 0 else []))
                                if (j == 0 and nh == 0) else [])
                    t_mm2[c] = tml
                t_mm2_l = t_mm2[31]
                t_ev2 = []
                for j in range(min(4, len(tcs))):
                    psv = [ps1a, ps1b, psA, ps2x][j]
                    ci = T0 // 128 + j
                    t_ev2.append(p.step("V", lambda v, ci=ci, psv=psv, j=j:
                        v.scalar_tensor_tensor(
                            xb[:, D * ci:D * ci + 1024], psv[:, 0:1024],
                            hr2[:, j:j + 1], xb[:, D * ci:D * ci + 1024],
                            ALU.mult, ALU.add),
                        waits=[t_mm2[31], th2]))
            t_mm2m = {}
            t_w2m = {}
            for c in range(32):
                cslot = c % 2
                t_w2m[c] = p.dma("Y", lambda y, c=c, cslot=cslot: y.dma_start(
                    out=w2r[:, 1024 * cslot:1024 * cslot + 1024],
                    in_=wq2s_d[c, :, :]),
                    "dc" if cslot == 0 else "dd",
                    waits=[t_mm2m.get(c - 2), t_mm2_l if c < 2 else None])
                for nh in range(2):
                    t_mm2m[c] = p.step("T", lambda t, c=c, cslot=cslot, nh=nh:
                        t.matmul(
                            ps2x[0:2, 512 * nh:512 * nh + 512],
                            hqmini[:, 2 * c:2 * c + 2],
                            w2r[:, 1024 * cslot + 512 * nh:
                                1024 * cslot + 512 * nh + 512],
                            start=(c == 0), stop=(c == 31)),
                        waits=[t_w2m[c], t_hqm] + (t_ev2 if c == 0 else [])
                        if nh == 0 else [])
            t_out = p.step("V", lambda v: v.scalar_tensor_tensor(
                xb[0:2, 16 * D:17 * D], ps2x[0:2, 0:1024], hr2[0:2, 4:5],
                xb[0:2, 16 * D:17 * D], ALU.mult, ALU.add),
                waits=[t_mm2m[31], th2])
            return [t_out] + t_ev2

        with nc.psum_tensor("ps2x_f1", [128, 1024], F32) as ps2x:
            tF1 = emit_F("f1", t_load)
        p.barrier()
        if "x1" in dbg:
            for i in range(NCH):
                p.dma("Y", lambda y, i=i: y.dma_start(
                    out=dbg["x1"][128 * i:128 * i + CH(i), :],
                    in_=xb[0:CH(i), D * i:D * i + D]), "ds")
            p.barrier()

        with ExitStack() as ast:
            psB = ast.enter_context(nc.psum_tensor("psB", [128, 1024], BF16))
            psKV = ast.enter_context(nc.psum_tensor("psKV", [128, 512], F32))

            t_p = None
            for hpair in range(8):
                for h2 in range(2):
                    h = 2 * hpair + h2
                    tl1 = p.dma("Y", lambda y, h=h: y.dma_start(
                        out=srep[:, 0:64], in_=proj_d[h, 0:128, :]), "dc",
                        waits=[t_p])
                    tl2 = p.dma("Y", lambda y, h=h: y.dma_start(
                        out=srep[:, 64:128], in_=proj_d[h, 128:256, :]), "dc",
                        waits=[t_p])
                    tc1 = p.step("V", lambda v: v.tensor_copy(
                        frb[:, 0:128], srep[:, 0:128]), waits=[tl1, tl2, t_p])
                    tt1 = p.step("T", lambda t: t.transpose(
                        psB[0:64, 0:128], frb[:, 0:64], identb[:]), waits=[tc1])
                    tt2 = p.step("T", lambda t: t.transpose(
                        psB[0:64, 128:256], frb[:, 64:128], identb[:]))
                    t_p = p.step("V", lambda v, hpair=hpair, h2=h2: v.tensor_copy(
                        projTc[64 * h2:64 * h2 + 64,
                               256 * hpair:256 * hpair + 256],
                        psB[0:64, 0:256]), waits=[tt2, ("T", tt2[1])])

            t_bv = p.dma("Y", lambda y: y.dma_start(out=bvrow[:], in_=bv_d[:, :]),
                         "dk", waits=[t_p])
            t_st2 = emit_stats(True, None)
            t_pp2, t_rstd2 = emit_params(False, t_st2)

            tkvz = p.step("V", lambda v: v.memset(kvloc[:], 0.0))
            tvz = p.step("V", lambda v: v.memset(vtm[:], 0.0), waits=[t_wk_done])
            tvon = p.step("V", lambda v: v.memset(
                vtm[:, :].rearrange("p (tc hh c) -> p tc hh c", tc=5, c=65)
                [:, :, :, 64:65], 1.0), waits=[tvz])

            def y2T_build(si, T0, L, slot, t_free):
                tcs = chunks_of(T0, L)
                toks = []
                tprev = None
                for j, (ci, ch) in enumerate(tcs):
                    tz = p.step("S", lambda s, ci=ci, ch=ch: s.activation(
                        frb[0:ch, 0:1024], xb[0:ch, D * ci:D * ci + 1024],
                        AF.Identity, scale=prm["prstd"][0:ch, ci:ci + 1],
                        bias=prm["aB"][0:ch, ci:ci + 1]),
                        waits=[t_pp2, t_rstd2, tprev] + (t_free if j == 0 else []))
                    tts = None
                    for k in range(8):
                        tts = p.step("T", lambda t, k=k, ch=ch: t.transpose(
                            psB[:, 128 * k:128 * k + ch],
                            frb[0:ch, 128 * k:128 * k + 128],
                            identb[0:ch, 0:ch]),
                            waits=[tz, tprev] if k == 0 else [])
                    tprev = p.step("V", lambda v, j=j, ch=ch, slot=slot:
                        v.tensor_copy(
                            xqT3[slot][:, :, 128 * j:128 * j + ch],
                            psB[:, 0:1024].rearrange("p (k c) -> p k c", k=8)
                            [:, :, 0:ch]),
                        waits=[tts])
                    toks.append(tprev)
                return toks

            t_kv_ev = [tkvz, tkvz]
            t_qk = {}
            t_xk = {}
            t_y2 = {0: y2T_build(0, TILES[0][0], TILES[0][1], 0,
                                 [("T", p.cnt["T"])])}
            t_xq_dma = []
            for si, (T0, L) in enumerate(TILES):
                slot = si % 2
                tcs = chunks_of(T0, L)
                if si + 1 < len(TILES):
                    t_y2[si + 1] = y2T_build(si + 1, TILES[si + 1][0],
                                             TILES[si + 1][1], 1 - slot,
                                             [t_xk.get((si - 1, 15)),
                                              t_y2[si][-1]])
                t_vev = {}
                t_wv = {}
                t_ve_last = None
                for vc in range(8):
                    wslot = vc % 2
                    t_wv[vc] = p.dma("Y", lambda y, vc=vc, wslot=wslot:
                        y.dma_start(out=wvr[:, 1024 * wslot:1024 * wslot + 1024],
                                    in_=wvs_d[vc, :, :]),
                        "da" if wslot == 0 else "db",
                        waits=[t_wk_done if (si == 0 and vc < 2) else None,
                               t_vev.get(vc - 2),
                               (t_xk.get((si - 1, 14 + vc)) if vc < 2 else None)])
                    tml = None
                    for d in range(8):
                        tml = p.step("T", lambda t, d=d, wslot=wslot, slot=slot:
                            t.matmul(ps1b[:, 0:512],
                                     wvr[:, 1024 * wslot + 128 * d:
                                         1024 * wslot + 128 * d + 128],
                                     xqT3[slot][:, d, 0:512],
                                     start=(d == 0), stop=False),
                            waits=([t_wv[vc], t_vev.get(vc - 1)] + t_y2[si])
                            if d == 0 else [])
                        if L > 512:
                            tml = p.step("T", lambda t, d=d, wslot=wslot,
                                         slot=slot, L=L:
                                t.matmul(ps1b[:, 512:L],
                                         wvr[:, 1024 * wslot + 128 * d:
                                             1024 * wslot + 128 * d + 128],
                                         xqT3[slot][:, d, 512:L],
                                         start=(d == 0), stop=False))
                    b0 = 64 * (vc // 4)
                    bc = 128 * (vc % 4)
                    tml = p.step("T", lambda t, b0=b0, bc=bc: t.matmul(
                        ps1b[:, 0:512], bvrow[b0:b0 + 1, bc:bc + 128],
                        onesw[b0:b0 + 1, 0:512], start=False, stop=True),
                        waits=[t_bv])
                    if L > 512:
                        tml = p.step("T", lambda t, b0=b0, bc=bc, L=L: t.matmul(
                            ps1b[:, 512:L],
                            bvrow[b0:b0 + 1, bc:bc + 128],
                            onesw[b0:b0 + 1, 0:L - 512], start=False, stop=True))
                    tvs = p.step("V", lambda v, L=L: v.tensor_copy(
                        vstg[:, 0:L], ps1b[:, 0:L]), waits=[tml])
                    for j, (ci, ch) in enumerate(tcs):
                        ttv = p.step("T", lambda t, j=j, ch=ch: t.transpose(
                            psB[0:ch, 0:128], vstg[:, 128 * j:128 * j + ch],
                            identb[:]), waits=[tvs, t_ve_last])
                        t_ve_last = p.step("V", lambda v, j=j, ch=ch, vc=vc:
                            v.tensor_copy(
                                vtm[0:ch, 1040 * j + 130 * vc:
                                    1040 * j + 130 * vc + 130]
                                .rearrange("p (hh c) -> p hh c", c=65)
                                [:, :, 0:64],
                                psB[0:ch, 0:128]
                                .rearrange("p (hh c) -> p hh c", c=64)),
                            waits=[ttv, tvon])
                    t_vev[vc] = t_ve_last
                for qkc in range(16):
                    wslot = qkc % 2
                    t_qk[(si, qkc)] = p.dma("Y", lambda y, qkc=qkc, wslot=wslot:
                        y.dma_start(
                            out=wqkr[:, 1024 * wslot:1024 * wslot + 1024],
                            in_=wqks_d[qkc, :, :]),
                        "da" if wslot == 0 else "db",
                        waits=[(t_vev.get(6 + qkc) if qkc < 2 else None),
                               t_xk.get((si, qkc - 2))])
                    tml = None
                    for d in range(8):
                        tml = p.step("T", lambda t, d=d, wslot=wslot, slot=slot, L=L:
                            t.matmul(ps1a[:, 0:512],
                                     wqkr[:, 1024 * wslot + 128 * d:
                                          1024 * wslot + 128 * d + 128],
                                     xqT3[slot][:, d, 0:512],
                                     start=(d == 0), stop=(d == 7)),
                            waits=[t_qk[(si, qkc)], t_xk.get((si, qkc - 1))]
                            if d == 0 else [])
                        if L > 512:
                            tml = p.step("T", lambda t, d=d, wslot=wslot,
                                         slot=slot, L=L:
                                t.matmul(ps1a[:, 512:L],
                                         wqkr[:, 1024 * wslot + 128 * d:
                                              1024 * wslot + 128 * d + 128],
                                         xqT3[slot][:, d, 512:L],
                                         start=(d == 0), stop=(d == 7)))
                    if qkc < 8:
                        ss_ = (qkc % 2) * 516
                        txe = p.step("S", lambda s, qkc=qkc, L=L, ss_=ss_:
                            s.activation(
                                xstg[:, ss_:ss_ + L],
                                ps1a[:, 0:L], AF.Identity, scale=DN,
                                bias=bqkvDN[:, qkc:qkc + 1]),
                            waits=[tml, t_bqdn,
                                   (t_xq_dma[-2] if len(t_xq_dma) >= 2
                                    else None)])
                        t_xk[(si, qkc)] = txe
                        t_xq_dma.append(p.dma("Y", lambda y, qkc=qkc, T0=T0,
                                              L=L, ss_=ss_:
                            y.dma_start(out=xqs_d[qkc, :, T0:T0 + L],
                                        in_=xstg[:, ss_:ss_ + L]),
                            "dq" if qkc % 2 == 0 else "dp",
                            waits=[txe]))
                    else:
                        kc = qkc - 8
                        t_xkev = p.step("S", lambda s, L=L, qkc=qkc: s.activation(
                            xkr[:, 0:L], ps1a[:, 0:L], AF.Identity,
                            scale=DN, bias=bqkvDN[:, qkc:qkc + 1]),
                            waits=[tml, t_bqdn])
                        t_xk[(si, qkc)] = t_xkev
                        t_fmprev = None
                        t_kvprev = {}
                        for j, (ci, chq) in enumerate(tcs):
                            p.step("T", lambda t, kc=kc, j=j, chq=chq:
                                t.matmul(psA[0:chq, 0:256],
                                         xkr[0:64, 128 * j:128 * j + chq],
                                         projTc[0:64, 256 * kc:256 * kc + 256],
                                         start=True, stop=True),
                                waits=[t_xkev, t_fmprev])
                            tfm = p.step("T", lambda t, kc=kc, j=j, chq=chq:
                                t.matmul(psA[0:chq, 256:512],
                                         xkr[64:128, 128 * j:128 * j + chq],
                                         projTc[64:128, 256 * kc:256 * kc + 256],
                                         start=True, stop=True))
                            trd = p.step("V", lambda v, chq=chq: v.tensor_reduce(
                                negmx[0:chq, 0:2],
                                psA[0:chq, 0:512]
                                .rearrange("p (h m) -> p h m", h=2),
                                AX.X, ALU.max, negate=True), waits=[tfm])
                            tb2 = p.step("V", lambda v, chq=chq: v.tensor_scalar(
                                negmx2[0:chq, :], negmx[0:chq, :], -LN16, None,
                                ALU.add))
                            if ci == 0:
                                tb2 = p.step("V", lambda v, chq=chq:
                                    v.tensor_tensor(
                                        negmx2[0:chq, :], negmx2[0:chq, :],
                                        emf[0:chq, :], ALU.add))
                            if ci == 16:
                                tb2 = p.step("V", lambda v, chq=chq:
                                    v.tensor_tensor(
                                        negmx2[0:chq, :], negmx2[0:chq, :],
                                        eml[0:chq, :], ALU.add))
                            te1 = p.step("S", lambda s, chq=chq: s.activation(
                                kpr[0:chq, 0:256], psA[0:chq, 0:256],
                                AF.Exp, bias=negmx2[0:chq, 0:1]),
                                waits=[tb2] + list(t_kvprev.values()))
                            te2 = p.step("S", lambda s, chq=chq: s.activation(
                                kpr[0:chq, 256:512], psA[0:chq, 256:512],
                                AF.Exp, bias=negmx2[0:chq, 1:2]))
                            t_fmprev = te2
                            tep = p.step("V", lambda v, chq=chq: v.tensor_scalar(
                                kpr[0:chq, 0:512], kpr[0:chq, 0:512],
                                EPS_PHI, None, ALU.add), waits=[te1, te2])
                            t_kvprev = {}
                            for h2 in range(2):
                                h = 2 * kc + h2
                                psv = psKV if h2 == 0 else psA
                                off = 0 if h2 == 0 else 512
                                tkv = p.step("T", lambda t, j=j, h=h, h2=h2,
                                             chq=chq, psv=psv, off=off,
                                             last=(j == len(tcs) - 1):
                                    t.matmul(psv[0:65, off:off + 256],
                                             vtm[0:chq, 1040 * j + 65 * h:
                                                 1040 * j + 65 * h + 65],
                                             kpr[0:chq, 256 * h2:256 * h2 + 256],
                                             start=(j == 0),
                                             stop=last),
                                    waits=[tep, t_vev.get(7)]
                                    + ([t_kv_ev[-2], t_kv_ev[-1]]
                                       if j == 0 else []))
                                t_kvprev[h2] = tkv
                                if j == len(tcs) - 1:
                                    t_kv_ev.append(p.step("V",
                                        lambda v, h=h, psv=psv, off=off:
                                        v.scalar_tensor_tensor(
                                            kvloc[:, 256 * h:256 * h + 256],
                                            psv[0:65, off:off + 256], 1.0,
                                            kvloc[:, 256 * h:256 * h + 256],
                                            ALU.mult, ALU.add),
                                        waits=[tkv, tkvz]))

            t_ci = p.dma("Y", lambda y: y.dma_start(out=ccin_d[:, :], in_=kvloc[:]),
                         "dk", waits=[t_kv_ev[-1], t_kv_ev[-2]])
            t_cc = p.step("G", lambda g: g.collective_compute(
                "AllReduce", ALU.add,
                replica_groups=[[0, 1], [2, 3], [4, 5], [6, 7]],
                ins=[ccin_d[:, :]], outs=[ccout_d[:, :]]),
                waits=[t_ci, t_wk_done])
            t_co = p.dma("Y", lambda y: y.dma_start(out=kvloc[:], in_=ccout_d[:, :]),
                         "dk", waits=[t_cc])

            t_kvf = {}
            tprev = None
            for h in range(16):
                tc1 = p.step("V", lambda v, h=h: v.tensor_copy(
                    kvb[0:65, 0:256], kvloc[:, 256 * h:256 * h + 256]),
                    waits=[t_co, tprev])
                tt1 = p.step("T", lambda t: t.transpose(
                    psB[:, 0:65], kvb[0:65, 0:128], identb[0:65, 0:65]),
                    waits=[tc1, tprev])
                tt2 = p.step("T", lambda t: t.transpose(
                    psB[:, 128:193], kvb[0:65, 128:256], identb[0:65, 0:65]))
                p.step("V", lambda v, h=h: v.tensor_copy(
                    kvfm0[:, 65 * h:65 * h + 65], psB[:, 0:65]),
                    waits=[tt2, ("T", tt2[1])])
                tprev = p.step("V", lambda v, h=h: v.tensor_copy(
                    kvfm1[:, 65 * h:65 * h + 65], psB[:, 128:193]))
                t_kvf[h] = tprev
            t_kv_ready = tprev

            t_wo = None
            for b in range(8):
                t_wo = p.dma("Y", lambda y, b=b: y.dma_start(
                    out=woutp[b][:, 0:1024], in_=wos_d[b, :, :]), "dw",
                    waits=[t_kv_ev[-1], t_kv_ev[-2], t_wk_done, t_kv_ready])
            t_bo = p.dma("Y", lambda y: y.dma_start(out=bvrow[:], in_=bout_d[:, :]),
                         "dw", waits=[t_kv_ev[-1], t_kv_ev[-2], t_kv_ready])
            t_wo_done = ("dw", p.cnt["dw"])

            t_res = None
            t_at_prev = None
            t_xql = {}
            for i in range(NCH):
                ch = CH(i)
                lslot = i % 2
                t_xql[i] = p.dma("Y", lambda y, i=i, ch=ch, lslot=lslot:
                    y.dma_start(
                        out=xqld[:, 1024 * lslot:1024 * lslot + 8 * ch]
                        .rearrange("p (qc j) -> p qc j", qc=8),
                        in_=xqs_d[:, :, 128 * i:128 * i + ch]
                        .rearrange("qc p j -> p qc j")),
                    "dq", waits=[("dq", p.cnt["dq"]) if i == 0 else None,
                                 ("dp", p.cnt["dp"]) if i == 0 else None,
                                 t_at_prev])
                t_div = []
                t_numprev = []
                for qc in range(8):
                    tfm = p.step("T", lambda t, qc=qc, i=i, ch=ch, lslot=lslot:
                        t.matmul(
                            psA[0:ch, 0:256],
                            xqld[0:64, 1024 * lslot + 128 * qc:
                                 1024 * lslot + 128 * qc + ch],
                            projTc[0:64, 256 * qc:256 * qc + 256],
                            start=True, stop=True),
                        waits=[t_kv_ready, t_at_prev, t_xql[i],
                               t_numprev[-1] if t_numprev else None]
                        + ([("S", p.cnt["S"])] if qc == 0 and i == 0 else []))
                    tfm = p.step("T", lambda t, qc=qc, ch=ch, lslot=lslot:
                        t.matmul(
                            psA[0:ch, 256:512],
                            xqld[64:128, 1024 * lslot + 128 * qc:
                                 1024 * lslot + 128 * qc + ch],
                            projTc[64:128, 256 * qc:256 * qc + 256],
                            start=True, stop=True))
                    trd = p.step("V", lambda v, ch=ch: v.tensor_reduce(
                        negmx[0:ch, 0:2],
                        psA[0:ch, 0:512].rearrange("p (h m) -> p h m", h=2),
                        AX.X, ALU.max, negate=True), waits=[tfm])
                    tb2 = p.step("V", lambda v, ch=ch: v.tensor_scalar(
                        negmx2[0:ch, :], negmx[0:ch, :], -LN16, None, ALU.add))
                    te1 = p.step("S", lambda s, ch=ch: s.activation(
                        qpr[0:ch, 0:256], psA[0:ch, 0:256], AF.Exp,
                        bias=negmx2[0:ch, 0:1]), waits=[tb2])
                    te2 = p.step("S", lambda s, ch=ch: s.activation(
                        qpr[0:ch, 256:512], psA[0:ch, 256:512], AF.Exp,
                        bias=negmx2[0:ch, 1:2]))
                    tep = p.step("V", lambda v, ch=ch: v.tensor_scalar(
                        qpr[0:ch, 0:512], qpr[0:ch, 0:512], EPS_PHI, None,
                        ALU.add), waits=[te1, te2])
                    tts = None
                    for b in range(4):
                        tts = p.step("T", lambda t, b=b, ch=ch: t.transpose(
                            psB[:, 128 * b:128 * b + ch],
                            qpr[0:ch, 128 * b:128 * b + 128],
                            identb[0:ch, 0:ch]),
                            waits=[tep] if b == 0 else [])
                    tqf = p.step("V", lambda v, ch=ch: v.tensor_copy(
                        qfm[:, 0:512].rearrange("p (b c) -> p b c", b=4)
                        [:, :, 0:ch],
                        psB[:, 0:512].rearrange("p (b c) -> p b c", b=4)
                        [:, :, 0:ch]),
                        waits=[tts])
                    t_numprev = []
                    for h2 in range(2):
                        h = 2 * qc + h2
                        psv = psKV if h2 == 0 else psA
                        off = 0 if h2 == 0 else 512
                        tnm = None
                        for fc in range(2):
                            kvf = kvfm0 if fc == 0 else kvfm1
                            tnm = p.step("T", lambda t, h=h, h2=h2, fc=fc, ch=ch,
                                         psv=psv, off=off, kvf=kvf:
                                t.matmul(psv[0:ch, off:off + 65],
                                         qfm[:, 128 * (2 * h2 + fc):
                                             128 * (2 * h2 + fc) + ch],
                                         kvf[:, 65 * h:65 * h + 65],
                                         start=(fc == 0), stop=(fc == 1)),
                                waits=[tqf, t_kvf[h],
                                       t_div[-1] if t_div else None]
                                if fc == 0 else [])
                        t_numprev.append(tnm)
                        p.step("V", lambda v, ch=ch, psv=psv, off=off:
                            v.tensor_scalar(dent[0:ch, :],
                                            psv[0:ch, off + 64:off + 65],
                                            1e-6, None, ALU.add), waits=[tnm])
                        p.step("V", lambda v, ch=ch: v.reciprocal(
                            recr[0:ch, :], dent[0:ch, :]))
                        t_div.append(p.step("V", lambda v, h=h, ch=ch, psv=psv,
                                            off=off:
                            v.tensor_scalar(
                                atile[0:ch, 64 * h:64 * h + 64],
                                psv[0:ch, off:off + 64],
                                recr[0:ch, 0:1], None, ALU.mult)))
                tts = None
                for b in range(8):
                    tts = p.step("T", lambda t, b=b, ch=ch: t.transpose(
                        psB[:, 128 * b:128 * b + ch],
                        atile[0:ch, 128 * b:128 * b + 128], identb[0:ch, 0:ch]),
                        waits=([t_div[-1], t_div[-2]] if b == 0 else []))
                tat = p.step("V", lambda v, ch=ch: v.tensor_copy(
                    aTl[:, 0:1024].rearrange("p (b c) -> p b c", b=8)[:, :, 0:ch],
                    psB[:, 0:1024].rearrange("p (b c) -> p b c", b=8)[:, :, 0:ch]),
                    waits=[tts])
                t_at_prev = tat
                tml = None
                for b in range(8):
                    for nh in range(2):
                        tml = p.step("T", lambda t, b=b, ch=ch, nh=nh: t.matmul(
                            ps1b[0:ch, 512 * nh:512 * nh + 512],
                            aTl[:, 128 * b:128 * b + ch],
                            woutp[b][:, 512 * nh:512 * nh + 512],
                            start=(b == 0), stop=False),
                            waits=[tat, t_wo_done, t_res]
                            if (b == 0 and nh == 0) else [])
                for q4 in range(2):
                    tml = p.step("T", lambda t, ch=ch, q4=q4: t.matmul(
                        ps1b[0:ch, 512 * q4:512 * q4 + 512],
                        allones[64 * q4:64 * q4 + 1, 0:ch],
                        bvrow[64 * q4:64 * q4 + 1, 0:512],
                        start=False, stop=True),
                        waits=[t_bo] if q4 == 0 else [])
                t_res = p.step("V", lambda v, i=i, ch=ch: v.scalar_tensor_tensor(
                    xb[0:ch, D * i:D * i + 1024], ps1b[0:ch, 0:1024], 1.0,
                    xb[0:ch, D * i:D * i + 1024], ALU.mult, ALU.add),
                    waits=[tml])

        p.barrier()
        if "x2" in dbg:
            for i in range(NCH):
                p.dma("Y", lambda y, i=i: y.dma_start(
                    out=dbg["x2"][128 * i:128 * i + CH(i), :],
                    in_=xb[0:CH(i), D * i:D * i + D]), "ds")
            p.barrier()

        with nc.psum_tensor("ps2x_f2", [128, 1024], F32) as ps2x:
            emit_F("f2", None)
        p.barrier()
        for i in range(NCH):
            p.dma("Y", lambda y, i=i: y.dma_start(
                out=out_d[128 * i:128 * i + CH(i), :],
                in_=xb[0:CH(i), D * i:D * i + D]), "ds")
        p.add("Y", lambda y: y.nop(), waits=[("ds", p.cnt["ds"])])

        blk = st.enter_context(nc.Block())
        blk.tensor(lambda eng: [f(eng) for f in p.q["T"]])
        blk.vector(lambda eng: [f(eng) for f in p.q["V"]])
        blk.scalar(lambda eng: [f(eng) for f in p.q["S"]])
        blk.gpsimd(lambda eng: [f(eng) for f in p.q["G"]])
        blk.sync(lambda eng: [f(eng) for f in p.q["Y"]])

    return nc


def kernel(**inputs):
    x = np.ascontiguousarray(np.asarray(inputs["L0_attn_x"], np.float32))
    w_qkv = np.ascontiguousarray(np.asarray(inputs["L0_attn_w_qkv"], np.float32))
    b_qkv = np.asarray(inputs["L0_attn_b_qkv"], np.float32)
    w_out = np.ascontiguousarray(np.asarray(inputs["L0_attn_w_out"], np.float32))
    b_out = np.asarray(inputs["L0_attn_b_out"], np.float32)
    proj = np.ascontiguousarray(np.asarray(inputs["L0_attn_proj_matrix"], np.float32))
    w1 = np.ascontiguousarray(np.asarray(inputs["L0_ffn_w1"], np.float32))
    conv_w = np.asarray(inputs["L0_ffn_conv_w"], np.float32).reshape(HID, 3)
    conv_b = np.asarray(inputs["L0_ffn_conv_b"], np.float32)
    w2 = np.ascontiguousarray(np.asarray(inputs["L0_ffn_w2"], np.float32))

    cw = [np.ascontiguousarray(conv_w[:, j].reshape(32, 128).T) for j in range(3)]
    cbt = np.ascontiguousarray(conv_b.reshape(32, 128).T)
    bqkv_t = np.ascontiguousarray(b_qkv.reshape(24, 128).T)
    bv_row = np.zeros((128, 512), np.float32)
    bv_row[0, :] = b_qkv[2048:2560]
    bv_row[64, :] = b_qkv[2560:3072]
    bout_row = np.zeros((128, 512), np.float32)
    bout_row[0, :] = b_out[0:512]
    bout_row[64, :] = b_out[512:1024]

    in_maps = []
    for core in range(8):
        b = core // 2
        half = core % 2
        lo = half * 2046
        xs = np.ascontiguousarray(x[b, lo:lo + NTOK])
        emaskf = np.zeros((128, 2), np.float32)
        emaskl = np.zeros((128, 2), np.float32)
        if half == 1:
            emaskf[0:2, :] = -1e30
        else:
            emaskl[0:2, :] = -1e30
        in_maps.append({
            "x0": xs, "w1": w1, "w2": w2,
            "cw0": cw[0], "cw1": cw[1], "cw2": cw[2], "cb": cbt,
            "wqkv": w_qkv, "bqkv": bqkv_t, "bv": bv_row,
            "wout": w_out, "bout": bout_row, "proj": proj,
            "emaskf": emaskf, "emaskl": emaskl,
        })

    nc = build_program(tuple(DEBUG))
    res = run_bass_kernel_spmd(nc, in_maps, core_ids=list(range(8)))

    out = np.empty((4, 4096, D), np.float32)
    for core in range(8):
        b = core // 2
        half = core % 2
        full = np.asarray(res.results[core]["xout"], np.float32)
        own = full[0:2048] if half == 0 else full[2:2050]
        out[b, half * 2048:(half + 1) * 2048] = own
    if DEBUG:
        kernel._dbg = [{k: np.asarray(v) for k, v in r.items()}
                       for r in res.results]
    return out


# version 09
# speedup vs baseline: 1.0091x; 1.0091x over previous
import numpy as np
import ml_dtypes
from contextlib import ExitStack

import concourse.bass as bass
import concourse.mybir as mybir
from concourse.bass_utils import run_bass_kernel_spmd

F32 = mybir.dt.float32
BF16 = mybir.dt.bfloat16
I16 = mybir.dt.int16
AF = mybir.ActivationFunctionType
ALU = mybir.AluOpType
AX = mybir.AxisListType

NTOK = 2050
NCH = 17
D = 1024
HID = 4096
NHEAD, HD, NF = 16, 64, 256
TILES = [(0, 512), (512, 512), (1024, 512), (1536, 514)]
SILU_MIN = 0.27846455
LN16 = float(np.log(16.0))
EPS_PHI = float(1e-6 / 16.0)
DN = float(64.0 ** -0.25)

DEBUG = []


def CH(i):
    return 128 if i < 16 else 2


def chunks_of(t0, L):
    out = []
    j = 0
    while 128 * j < L:
        out.append((t0 // 128 + j, min(128, L - 128 * j)))
        j += 1
    return out


class Plan:
    def __init__(self, nc, st):
        self.nc = nc
        self.st = st
        self.q = {e: [] for e in ("T", "V", "S", "G", "Y")}
        self.sems = {}
        self.cnt = {}

    def sem(self, name):
        self.sems[name] = self.st.enter_context(self.nc.semaphore(name))
        self.cnt[name] = 0

    def add(self, e, fn, waits=(), inc=None):
        sems = self.sems
        waits = tuple(w for w in waits if w is not None)
        if e in self.cnt and self.cnt[e] > 0:
            waits = ((e, self.cnt[e]),) + waits
        if inc is not None:
            self.cnt[inc[0]] += inc[1]
        tok = None if inc is None else (inc[0], self.cnt[inc[0]])

        def run(eng, _w=waits, _fn=fn, _inc=inc):
            best = {}
            for s, v in _w:
                if v > best.get(s, -1):
                    best[s] = v
            for s, v in best.items():
                eng.wait_ge(sems[s], v)
            ins = _fn(eng)
            if _inc is not None:
                ins.then_inc(sems[_inc[0]], _inc[1])
        self.q[e].append(run)
        return tok

    def step(self, e, fn, waits=()):
        return self.add(e, fn, waits, inc=(e, 1))

    def dma(self, e, fn, sem, waits=()):
        return self.add(e, fn, waits, inc=(sem, 16))

    def barrier(self):
        toks = []
        for e in ("T", "V", "S", "G"):
            toks.append(self.step(e, lambda eng: eng.nop(),
                                  waits=[(e, self.cnt[e])]))
        dma_waits = [(s, self.cnt[s]) for s in self.sems
                     if s not in ("T", "V", "S", "G", "Y") and self.cnt[s] > 0]
        ytok = self.step("Y", lambda eng: eng.nop(), waits=toks + dma_waits)
        for e in ("T", "V", "S", "G"):
            self.add(e, lambda eng: eng.nop(), waits=[ytok])
        return ytok


def build_program(debug=()):
    nc = bass.Bass("TRN2", target_bir_lowering=False, debug=False)

    x_d = nc.dram_tensor("x0", [NTOK, D], F32, kind="ExternalInput").ap()
    w1_d = nc.dram_tensor("w1", [HID, D], F32, kind="ExternalInput").ap()
    w2_d = nc.dram_tensor("w2", [D, HID], F32, kind="ExternalInput").ap()
    cw0_d = nc.dram_tensor("cw0", [128, 32], F32, kind="ExternalInput").ap()
    cw1_d = nc.dram_tensor("cw1", [128, 32], F32, kind="ExternalInput").ap()
    cw2_d = nc.dram_tensor("cw2", [128, 32], F32, kind="ExternalInput").ap()
    cb_d = nc.dram_tensor("cb", [128, 32], F32, kind="ExternalInput").ap()
    wqkv_d = nc.dram_tensor("wqkv", [D, 3 * D], F32, kind="ExternalInput").ap()
    bqkv_d = nc.dram_tensor("bqkv", [128, 24], F32, kind="ExternalInput").ap()
    bv_d = nc.dram_tensor("bv", [128, 512], F32, kind="ExternalInput").ap()
    wout_d = nc.dram_tensor("wout", [D, D], F32, kind="ExternalInput").ap()
    bout_d = nc.dram_tensor("bout", [128, 512], F32, kind="ExternalInput").ap()
    proj_d = nc.dram_tensor("proj", [NHEAD, NF, HD], F32, kind="ExternalInput").ap()
    emf_d = nc.dram_tensor("emaskf", [128, 2], F32, kind="ExternalInput").ap()
    eml_d = nc.dram_tensor("emaskl", [128, 2], F32, kind="ExternalInput").ap()
    out_d = nc.dram_tensor("xout", [NTOK, D], F32, kind="ExternalOutput").ap()
    dbg = {}
    for name in debug:
        dbg[name] = nc.dram_tensor("dbg_" + name, [NTOK, D], F32,
                                   kind="ExternalOutput").ap()

    wq1s_d = nc.dram_tensor("wq1s", [32, 128, 1024], BF16, kind="Internal").ap()
    wq2s_d = nc.dram_tensor("wq2s", [32, 128, 1024], BF16, kind="Internal").ap()
    wqks_d = nc.dram_tensor("wqks", [16, 128, 1024], BF16, kind="Internal").ap()
    wvs_d = nc.dram_tensor("wvs", [8, 128, 1024], BF16, kind="Internal").ap()
    wos_d = nc.dram_tensor("wos", [8, 128, 1024], BF16, kind="Internal").ap()
    xqs_d = nc.dram_tensor("xqs", [8, 128, NTOK], BF16, kind="Internal").ap()
    ccin_d = nc.dram_tensor("ccin", [65, 16 * NF], F32, kind="Internal").ap()
    ccout_d = nc.dram_tensor("ccout", [65, 16 * NF], F32, kind="Internal").ap()

    identb_d = nc.inline_tensor(np.eye(128).astype(ml_dtypes.bfloat16), name="idb")
    identf_d = nc.inline_tensor(np.eye(128, dtype=np.float32), name="idf")

    st = ExitStack()
    with st:
        def sb(name, shape, dtype=F32):
            return st.enter_context(nc.sbuf_tensor(name, shape, dtype))

        p = Plan(nc, st)
        for s in ("T", "V", "S", "G", "Y",
                  "da", "db", "dc", "dd", "dx", "dg", "ds", "dk", "dq", "dp", "dw"):
            p.sem(s)

        xb = sb("xb", [128, NCH * D])
        ybuf = sb("ybuf", [128, 32 * 514])
        kvloc = ybuf[0:65, 0:16 * NF]
        zr = sb("zr", [128, 2 * 516])
        i16r = sb("i16r", [128, 2 * 516], I16)
        arb = sb("arb", [128, 2 * 516], BF16)
        zrA = sb("zrA", [128, 2 * 512])
        i16A = sb("i16A", [128, 2 * 512], I16)
        arbA = sb("arbA", [128, 2 * 512], BF16)
        sqv = zrA[:, 0:1024]
        accr = zr[:, 0:516]
        mxv = sb("mxv", [128, 2 * 516])
        s2b = sb("s2b", [128, 516])
        srep = sb("sreps", [128, 128])
        bcsrc = srep[:, 0:1]
        identf = sb("identfs", [128, 128], F32)
        ones5 = sb("ones5", [128, 512], F32)
        allones = ones5[:, 0:128]
        onesw = ones5
        bvrow = sb("bvrows", [128, 512])
        PRM_N = ("ssum ssq smax smin pm pe2 pmm prstd pmabs pmyc psa pA pB "
                 "pr1 aB").split()
        prmt = sb("prmt", [128, len(PRM_N) * NCH])
        prm = {n: prmt[:, j * NCH:(j + 1) * NCH] for j, n in enumerate(PRM_N)}
        sml = sb("sml", [128, 64])
        s2m = sml[:, 0:5]; s2M = sml[:, 5:10]; s2v = sml[:, 10:15]
        s2vr = sml[:, 15:20]; hr2 = sml[:, 20:25]
        mw1o127 = sml[:, 25:26]; mw2o127h = sml[:, 26:27]
        sw1 = sml[:, 27:28]; sw2 = sml[:, 28:29]; c256 = sml[:, 29:30]
        negmx = sml[:, 30:32]; negmx2 = sml[:, 32:34]
        dent = sml[:, 34:35]; recr = sml[:, 35:36]
        emf = sml[:, 36:38]; eml = sml[:, 38:40]
        bqkv = sml[:, 40:64]
        smlb = sb("smlb", [128, 32])
        wsum = smlb[:, 0:32]
        bqkvDN = sb("bqkvDN", [128, 16])

        xqT = sb("xqT", [128, 2 * 8 * 516], BF16)
        xqT3 = [xqT[:, 0:8 * 516].rearrange("p (k c) -> p k c", k=8),
                xqT[:, 8 * 516:16 * 516].rearrange("p (k c) -> p k c", k=8)]
        hqr = sb("hqr", [128, 2 * 514], BF16)
        projTc = sb("projTc", [128, 8 * 256], BF16)
        identb = sb("identb", [128, 128], BF16)
        w1r = sb("w1r", [128, 2 * 1024], BF16)
        w2r = sb("w2r", [128, 2 * 1024], BF16)
        diagr = sb("diagr", [128, 2 * 128], BF16)
        hqmini = diagr[:, 0:64]
        xqrr = sb("xqrr", [128, 2 * 512], BF16)
        vtm = sb("vtm", [128, 5 * 1040], BF16)
        frb = sb("frb", [128, 1024], BF16)
        wqkr = w1r
        wvr = w1r
        xstg = arb
        xqld = xqT[:, 4128:6176]
        xkr = w2r[:, 0:520]
        kpr = w2r[:, 1024:1536]
        kvfm0 = xqT[:, 6176:7216]
        kvfm1 = xqT[:, 7216:8256]
        kvb = hqr[:, 0:256]
        qpr = w2r[:, 0:512]
        qfm = w2r[:, 512:1024]
        atile = xqT[:, 0:1024]
        aTl = xqT[:, 1040:2064]
        woutp = [vtm[:, 0:1024], vtm[:, 1040:2064], vtm[:, 2080:3104],
                 vtm[:, 3120:4144], vtm[:, 4160:5184],
                 w2r[:, 1024:2048], hqr[:, 0:1024], xqT[:, 2080:3104]]
        vstg = hqr[:, 0:520]

        psA = st.enter_context(nc.psum_tensor("psA", [128, 1024], F32))
        ps1a = st.enter_context(nc.psum_tensor("ps1a", [128, 1024], F32))
        ps1b = st.enter_context(nc.psum_tensor("ps1b", [128, 1024], F32))

        x16 = x_d[0:2048, :].rearrange("(c p) d -> p c d", p=128)
        xb16 = xb[:, 0:16 * D].rearrange("p (c d) -> p c d", c=16)
        p.dma("Y", lambda y: y.dma_start(out=xb16, in_=x16), "dx")
        p.dma("Y", lambda y: y.dma_start(
            out=xb[0:2, 16 * D:17 * D], in_=x_d[2048:2050, :]), "dx")
        cw0 = sb("cw0s", [128, 32]); cw1 = sb("cw1s", [128, 32])
        cw2 = sb("cw2s", [128, 32]); cb = sb("cbs", [128, 32])
        for buf, srcd in ((cw0, cw0_d), (cw1, cw1_d), (cw2, cw2_d), (cb, cb_d),
                          (bqkv, bqkv_d), (emf, emf_d), (eml, eml_d),
                          (identb, identb_d.ap()), (identf, identf_d.ap())):
            p.dma("Y", lambda y, b=buf, s=srcd: y.dma_start(out=b[:], in_=s[:, :]),
                  "dx")
        t_load = ("dx", p.cnt["dx"])

        p.step("V", lambda v: v.memset(prmt[:], 0.0))
        tm1 = p.step("V", lambda v: v.memset(sml[:, 0:36], 0.0))
        p.step("V", lambda v: v.memset(smlb[:], 0.0))
        p.step("V", lambda v: v.memset(ones5[:], 1.0))
        t_clr = p.step("V", lambda v: v.memset(srep[:], 0.0))
        t_c256 = p.step("V", lambda v: v.memset(c256[:], 256.0), waits=[tm1])
        p.barrier()
        t_bqdn = p.step("V", lambda v: v.tensor_scalar(
            bqkvDN[:], bqkv[:, 0:16], DN, None, ALU.mult), waits=[t_load])

        def bcast_scalar(dst, waits):
            tm = p.step("T", lambda t: t.matmul(ps1a[:, 0:1], allones[:], bcsrc[:],
                                                start=True, stop=True), waits=waits)
            return p.step("V", lambda v, d=dst: v.tensor_copy(d[:], ps1a[:, 0:1]),
                          waits=[tm])

        wtile = ybuf[:, 0:8192]
        wqstg = vtm[:, 3072:4096]
        wstg = vtm[:, 4096:5120]
        wkstg = vtm[:, 0:3072]

        def mean_abs(w_src, ntiles, rowlen, dst, t_start=()):
            tws = []
            for t in range(ntiles):
                sl = (t % 2) * 4096
                tdw = p.dma("Y", lambda y, t=t, sl=sl, rowlen=rowlen: y.dma_start(
                    out=wtile[:, sl:sl + rowlen],
                    in_=w_src[t * 128:(t + 1) * 128, :]),
                    "da" if t % 2 == 0 else "db",
                    waits=[tws[t - 2]] if t >= 2 else list(t_start))
                tws.append(p.step("V", lambda v, t=t, sl=sl, rowlen=rowlen:
                    v.reduce_sum(wsum[:, t:t + 1], wtile[:, sl:sl + rowlen],
                                 AX.X, apply_absolute_value=True), waits=[tdw]))
            t1 = p.step("V", lambda v, n=ntiles: v.reduce_sum(
                srep[:, 1:2], wsum[:, 0:n], AX.X), waits=[tws[-1], t_clr])
            t2 = p.step("T", lambda t: t.matmul(
                ps1a[0:1, 0:1], allones[:, 0:1], srep[:, 1:2],
                start=True, stop=True), waits=[t1])
            t3 = p.step("V", lambda v: v.tensor_copy(bcsrc[0:1, 0:1],
                                                     ps1a[0:1, 0:1]), waits=[t2])
            t4 = p.step("V", lambda v, n=ntiles, rl=rowlen: v.tensor_scalar(
                bcsrc[0:1, 0:1], bcsrc[0:1, 0:1], 1.0 / (n * 128 * rl),
                None, ALU.mult), waits=[t3])
            tb = bcast_scalar(dst, [t4])
            return p.step("V", lambda v, d=dst: v.tensor_scalar(
                d[:], d[:], 1e-5, None, ALU.max), waits=[tb])

        t_m1 = mean_abs(w1_d, 32, D, mw1o127)
        t_m2 = mean_abs(w2_d, 8, HID, mw2o127h, t_start=[t_m1])
        t_s1 = p.step("V", lambda v: v.reciprocal(sw1[:], mw1o127[:]), waits=[t_m1])
        t_s2w = p.step("V", lambda v: v.reciprocal(sw2[:], mw2o127h[:]), waits=[t_m2])
        t_m1b = p.step("V", lambda v: v.tensor_scalar(
            mw1o127[:], mw1o127[:], 1.0 / 127.0, None, ALU.mult), waits=[t_s1])
        t_m2b = p.step("V", lambda v: v.tensor_scalar(
            mw2o127h[:], mw2o127h[:], 0.5 / 127.0, None, ALU.mult), waits=[t_s2w])

        def quant_w(w_src, ntiles, rowlen, s_w, dst_fn, t_start, psP0,
                    t_wstg_start=()):
            nh = rowlen // 512
            t_lastz = {}
            t_tts = {}
            t_ev = {}
            t_dma = {}
            tq0 = None
            g = -1
            for t in range(ntiles):
                sl = (t % 2) * 4096
                tdw = p.dma("Y", lambda y, t=t, sl=sl, rowlen=rowlen: y.dma_start(
                    out=wtile[:, sl:sl + rowlen],
                    in_=w_src[t * 128:(t + 1) * 128, :]),
                    "da" if t % 2 == 0 else "db",
                    waits=[t_lastz.get(t - 2)] if t >= 2 else t_start)
                for h in range(nh):
                    g += 1
                    hs = (g % 2) * 512
                    tz = p.step("S", lambda s, sl=sl, h=h: s.activation(
                        zr[:, 0:512], wtile[:, sl + 512 * h:sl + 512 * h + 512],
                        AF.Identity, bias=c256[:, 0:1], scale=s_w[:, 0:1]),
                        waits=[tdw, tq0, t_c256]
                        + (list(t_start) if (t == 0 and h == 0) else []))
                    t_lastz[t] = tz
                    p.step("V", lambda v: v.tensor_copy(
                        i16r[:, 0:512], zr[:, 0:512]), waits=[tz])
                    p.step("V", lambda v: v.scalar_tensor_tensor(
                        arb[:, 0:512], zr[:, 0:512], -0.5, i16r[:, 0:512],
                        ALU.add, ALU.is_ge))
                    p.step("V", lambda v: v.scalar_tensor_tensor(
                        zr[:, 520:1032], i16r[:, 0:512], -256.0, arb[:, 0:512],
                        ALU.add, ALU.add))
                    tq0 = p.step("V", lambda v, hs=hs: v.tensor_scalar(
                        wqstg[:, hs:hs + 512], zr[:, 520:1032],
                        1.0, -1.0, ALU.min, ALU.max),
                        waits=[t_tts.get(g - 2)]
                        + (list(t_start) if g == 0 else []))
                    tts = None
                    for k in range(4):
                        tts = p.step("T", lambda t_, hs=hs, k=k: t_.transpose(
                            psP0[:, hs + 128 * k:hs + 128 * k + 128],
                            wqstg[:, hs + 128 * k:hs + 128 * k + 128],
                            identb[:]),
                            waits=[tq0, t_ev.get(g - 2)] if k == 0 else [])
                    t_tts[g] = tts
                    t_ev[g] = p.step("V", lambda v, hs=hs: v.tensor_copy(
                        wstg[:, hs:hs + 512], psP0[:, hs:hs + 512]),
                        waits=[tts, t_dma.get(g - 1)]
                        + (list(t_wstg_start) if g == 0 else []))
                    t_dma[g] = dst_fn(t, h, hs, t_ev[g])
            return t_ev[g]

        def w1_out(t, h, hs, tok):
            return p.dma("Y", lambda y, t=t, h=h, hs=hs: y.dma_start(
                out=wq1s_d[t, :, 512 * h:512 * h + 512],
                in_=wstg[:, hs:hs + 512]), "ds", waits=[tok])

        def w2_out(t, h, hs, tok):
            src = wstg[:, hs:hs + 512].rearrange("p (c j) -> p c j", c=4)
            dst = (wq2s_d[4 * h:4 * h + 4, :, 128 * t:128 * t + 128]
                   .rearrange("c p j -> p c j"))
            return p.dma("Y", lambda y, s=src, d=dst: y.dma_start(out=d, in_=s),
                         "ds", waits=[tok])

        with nc.psum_tensor("psP0", [128, 1024], BF16) as psP0:
            tq1 = quant_w(w1_d, 32, D, sw1, w1_out, [t_s1, t_s2w], psP0)
            t_w1dma = ("ds", p.cnt["ds"])
            tq2 = quant_w(w2_d, 8, HID, sw2, w2_out, [tq1], psP0,
                          t_wstg_start=[t_w1dma, tq1])
        t_wq_dma = ("ds", p.cnt["ds"])

        t_wk = None
        for dch in range(8):
            tg = p.dma("G", lambda g, d=dch: g.dma_start(
                out=wkstg[:, :], in_=wqkv_d[128 * d:128 * d + 128, :]), "dg",
                waits=[t_wk] if t_wk else [])
            srcqk = wkstg[:, 0:2048].rearrange("p (c j) -> p c j", c=16)
            dstqk = wqks_d[:, :, 128 * dch:128 * dch + 128].rearrange("c p j -> p c j")
            p.dma("G", lambda g, s=srcqk, d2=dstqk: g.dma_start(out=d2, in_=s), "dg",
                  waits=[tg])
            srcv = wkstg[:, 2048:3072].rearrange("p (c j) -> p c j", c=8)
            dstv = wvs_d[:, :, 128 * dch:128 * dch + 128].rearrange("c p j -> p c j")
            t_wk = p.dma("G", lambda g, s=srcv, d2=dstv: g.dma_start(out=d2, in_=s),
                         "dg", waits=[tg])
        for b in range(8):
            tg = p.dma("G", lambda g, b=b: g.dma_start(
                out=wkstg[:, 0:1024], in_=wout_d[128 * b:128 * b + 128, :]), "dg",
                waits=[t_wk])
            t_wk = p.dma("G", lambda g, b=b: g.dma_start(
                out=wos_d[b, :, :], in_=wkstg[:, 0:1024]), "dg", waits=[tg])
        t_wk_done = ("dg", p.cnt["dg"])

        def emit_stats(ln_only, t_pre):
            tS = None
            tclr0 = p.step("V", lambda v: v.memset(prmt[:], 0.0),
                           waits=[t_pre])
            for i in range(NCH):
                ch = CH(i)
                c0 = D * i
                p.step("V", lambda v, c0=c0, ch=ch, i=i: v.reduce_sum(
                    prm["ssum"][0:ch, i:i + 1], xb[0:ch, c0:c0 + 1024], AX.X),
                    waits=[t_pre, tclr0] if i == 0 else [])
                tS = p.step("S", lambda s, c0=c0, ch=ch, i=i: s.activation(
                    sqv[0:ch, 0:1024], xb[0:ch, c0:c0 + 1024], AF.Square,
                    accum_out=prm["ssq"][0:ch, i:i + 1]),
                    waits=[t_pre, tclr0] if i == 0 else [])
                if not ln_only:
                    p.step("V", lambda v, c0=c0, ch=ch, i=i: v.reduce_max(
                        prm["smax"][0:ch, i:i + 1], xb[0:ch, c0:c0 + 1024], AX.X))
                    p.step("V", lambda v, c0=c0, ch=ch, i=i: v.tensor_reduce(
                        prm["smin"][0:ch, i:i + 1], xb[0:ch, c0:c0 + 1024],
                        AX.X, ALU.min))
            return tS

        def emit_params(quant, t_stats):
            P_ = prm
            p.step("V", lambda v: v.tensor_scalar(
                P_["pm"][:], P_["ssum"][:], 1.0 / D, None, ALU.mult),
                waits=[t_stats])
            p.step("V", lambda v: v.tensor_scalar(
                P_["pe2"][:], P_["ssq"][:], 1.0 / D, None, ALU.mult))
            p.step("V", lambda v: v.tensor_tensor(
                P_["pmm"][:], P_["pm"][:], P_["pm"][:], ALU.mult))
            p.step("V", lambda v: v.tensor_tensor(
                P_["ssq"][:], P_["pe2"][:], P_["pmm"][:], ALU.subtract))
            p.step("V", lambda v: v.tensor_scalar(
                P_["pe2"][:], P_["ssq"][:], 1e-5, None, ALU.add))
            tu = p.step("V", lambda v: v.reciprocal(P_["pmm"][:], P_["pe2"][:]))
            tr = p.step("S", lambda s: s.activation(
                P_["prstd"][:], P_["pmm"][:], AF.Sqrt), waits=[tu])
            if not quant:
                p.step("V", lambda v: v.tensor_tensor(
                    P_["pe2"][:], P_["pm"][:], P_["prstd"][:], ALU.mult),
                    waits=[tr])
                tb2 = p.step("V", lambda v: v.tensor_scalar(
                    P_["aB"][:], P_["pe2"][:], -1.0, None, ALU.mult))
                return tb2, tr
            p.step("V", lambda v: v.tensor_tensor(
                P_["ssum"][:], P_["smax"][:], P_["pm"][:], ALU.subtract))
            p.step("V", lambda v: v.tensor_tensor(
                P_["pe2"][:], P_["pm"][:], P_["smin"][:], ALU.subtract))
            p.step("V", lambda v: v.tensor_tensor(
                P_["pmabs"][:], P_["ssum"][:], P_["pe2"][:], ALU.max))
            p.step("V", lambda v: v.tensor_tensor(
                P_["ssq"][:], P_["prstd"][:], P_["pmabs"][:], ALU.mult), waits=[tr])
            p.step("V", lambda v: v.tensor_scalar(
                P_["pmyc"][:], P_["ssq"][:], 1e-5, None, ALU.max))
            p.step("V", lambda v: v.reciprocal(P_["ssum"][:], P_["pmyc"][:]))
            p.step("V", lambda v: v.tensor_scalar(
                P_["psa"][:], P_["ssum"][:], 127.0, None, ALU.mult))
            p.step("V", lambda v: v.tensor_tensor(
                P_["pA"][:], P_["prstd"][:], P_["psa"][:], ALU.mult))
            p.step("V", lambda v: v.tensor_tensor(
                P_["pe2"][:], P_["pm"][:], P_["pA"][:], ALU.mult))
            p.step("V", lambda v: v.tensor_scalar(
                P_["pB"][:], P_["pe2"][:], -1.0, 256.0, ALU.mult, ALU.add))
            tpr = p.step("V", lambda v: v.tensor_scalar(
                P_["pr1"][:], P_["pmyc"][:], mw1o127[:, 0:1], None, ALU.mult),
                waits=[t_m1b])
            return tpr, tr

        def emit_F(tag, t_pre):
            t_stats = emit_stats(False, t_pre)
            t_pp, t_rstd = emit_params(True, t_stats)

            def stage_A(ti_, T0, L, slot, t_free, t_psA_free):
                toks = []
                tpz = p.step("V", lambda v, slot=slot: v.memset(
                    xqT[:, 8 * 516 * slot:8 * 516 * (slot + 1)], 0.0),
                    waits=t_free + [t_pp])
                lo = max(0, T0 - 1)
                hi = min(NTOK, T0 + L + 1)
                ci0, ci1 = lo // 128, (hi - 1) // 128
                t_chain = None
                t_ev = {}
                for ci in range(ci0, ci1 + 1):
                    ch = CH(ci)
                    idx = ci - ci0
                    t_dg = p.step("V", lambda v, ci=ci, idx=idx: v.tensor_scalar(
                        diagr[:, (idx % 2) * 128:(idx % 2) * 128 + 128],
                        identb[:], prm["pr1"][:, ci:ci + 1], None, ALU.mult),
                        waits=[t_pp, t_ev.get(2 * idx - 4)])
                    for h in range(2):
                        hx = 2 * idx + h
                        zs = (hx % 2) * 512
                        tz = p.step("S", lambda s, ci=ci, h=h, zs=zs, ch=ch:
                            s.activation(
                                zrA[0:ch, zs:zs + 512],
                                xb[0:ch, D * ci + 512 * h:D * ci + 512 * h + 512],
                                AF.Identity, scale=prm["pA"][0:ch, ci:ci + 1],
                                bias=prm["pB"][0:ch, ci:ci + 1]),
                            waits=[t_pp, t_rstd, t_chain, tpz])
                        p.step("V", lambda v, zs=zs, ch=ch: v.tensor_copy(
                            i16A[0:ch, zs:zs + 512], zrA[0:ch, zs:zs + 512]),
                            waits=[tz])
                        p.step("V", lambda v, zs=zs, ch=ch:
                            v.scalar_tensor_tensor(
                                arbA[0:ch, zs:zs + 512], zrA[0:ch, zs:zs + 512],
                                -0.5, i16A[0:ch, zs:zs + 512],
                                ALU.add, ALU.is_ge))
                        t_chain = p.step("V", lambda v, zs=zs, ch=ch:
                            v.scalar_tensor_tensor(
                                xqrr[0:ch, zs:zs + 512],
                                i16A[0:ch, zs:zs + 512], -256.0,
                                arbA[0:ch, zs:zs + 512], ALU.add, ALU.add))
                        tts = None
                        for kk in range(4):
                            k = 4 * h + kk
                            tts = p.step("T", lambda t, idx=idx, k=k, kk=kk,
                                         ch=ch, zs=zs:
                                t.matmul(psA[:, 128 * k:128 * k + ch],
                                         xqrr[0:ch, zs + 128 * kk:zs + 128 * kk + 128],
                                         diagr[0:ch, (idx % 2) * 128:
                                               (idx % 2) * 128 + ch],
                                         start=True, stop=True),
                                waits=[t_chain, t_dg, t_ev.get(hx - 2)]
                                + t_psA_free if kk == 0 else [])
                        glo = max(lo, 128 * ci)
                        ghi = min(hi, 128 * ci + ch)
                        if ghi <= glo:
                            continue
                        rc0 = glo - (T0 - 1)
                        pc0 = glo - 128 * ci
                        n = ghi - glo
                        t_ev[hx] = p.step("V", lambda v, h=h, rc0=rc0, pc0=pc0,
                                          n=n, slot=slot:
                            v.tensor_copy(
                                xqT3[slot][:, 4 * h:4 * h + 4, rc0:rc0 + n],
                                psA[:, 0:1024]
                                .rearrange("p (k c) -> p k c", k=8)
                                [:, 4 * h:4 * h + 4, pc0:pc0 + n]),
                            waits=[tts, tpz])
                        toks.append(t_ev[hx])
                return toks

            t_tapC = {}
            t_mm1 = {}
            t_sA = {}
            t_sA[0] = stage_A(0, TILES[0][0], TILES[0][1], 0,
                              [t_load if tag == "f1" else t_wq_dma], [])
            t_hqm = None
            t_mm2_l = None
            t_ev2 = []
            for ti_, (T0, L) in enumerate(TILES):
                slot = ti_ % 2
                tcs = chunks_of(T0, L)
                if ti_ + 1 < len(TILES):
                    t_sA[ti_ + 1] = stage_A(
                        ti_ + 1, TILES[ti_ + 1][0], TILES[ti_ + 1][1],
                        1 - slot,
                        [t_wq_dma, t_mm1.get((ti_ - 1, 31))],
                        list(t_ev2))
                t_sil = {}
                t_mx = None
                t_z2 = {}
                t_hq = {}
                for c in range(32):
                    cslot = c % 2
                    ps1 = ps1a if cslot == 0 else ps1b
                    t_w1 = p.dma("Y", lambda y, c=c, cslot=cslot: y.dma_start(
                        out=w1r[:, 1024 * cslot:1024 * cslot + 1024],
                        in_=wq1s_d[c, :, :]),
                        "da" if cslot == 0 else "db",
                        waits=[t_wq_dma if (ti_ == 0 and c < 2 and tag == "f1")
                               else None,
                               t_mm1.get((ti_, c - 2)),
                               t_mm1.get((ti_ - 1, 30 + c)) if c < 2 else None])
                    tml = None
                    for k in range(8):
                        tml = p.step("T", lambda t, k=k, cslot=cslot, ps1=ps1,
                                     slot=slot:
                            t.matmul(ps1[:, 0:512],
                                     w1r[:, 1024 * cslot + 128 * k:
                                         1024 * cslot + 128 * k + 128],
                                     xqT3[slot][:, k, 0:512],
                                     start=(k == 0), stop=(k == 7)),
                            waits=[t_w1, t_tapC.get((ti_, c - 2)),
                                   (t_ev2[-1] if (c < 2 and t_ev2) else None)]
                            + (t_sA[ti_] if c == 0 else []) if k == 0 else [])
                    for k in range(8):
                        tml = p.step("T", lambda t, k=k, cslot=cslot, L=L,
                                     ps1=ps1, slot=slot:
                            t.matmul(ps1[:, 512:L + 2],
                                     w1r[:, 1024 * cslot + 128 * k:
                                         1024 * cslot + 128 * k + 128],
                                     xqT3[slot][:, k, 512:L + 2],
                                     start=(k == 0), stop=(k == 7)))
                    t_mm1[(ti_, c)] = tml
                    tA_ = p.step("S", lambda s, c=c, L=L, ps1=ps1: s.activation(
                        accr[:, 0:L], ps1[:, 0:L], AF.Identity,
                        scale=cw0[:, c:c + 1], bias=cb[:, c:c + 1]),
                        waits=[tml])
                    tB_ = p.step("V", lambda v, c=c, L=L, ps1=ps1:
                        v.scalar_tensor_tensor(
                            accr[:, 0:L], ps1[:, 1:L + 1], cw1[:, c:c + 1],
                            accr[:, 0:L], ALU.mult, ALU.add), waits=[tA_])
                    t_tapC[(ti_, c)] = p.step("V", lambda v, c=c, L=L, ps1=ps1:
                        v.scalar_tensor_tensor(
                            accr[:, 0:L], ps1[:, 2:L + 2], cw2[:, c:c + 1],
                            accr[:, 0:L], ALU.mult, ALU.add))
                    t_sil[c] = p.step("S", lambda s, c=c, L=L: s.activation(
                        ybuf[:, 514 * c:514 * c + L], accr[:, 0:L], AF.Silu),
                        waits=[t_tapC[(ti_, c)]])
                    if c == 0:
                        t_mx = p.step("V", lambda v, L=L: v.tensor_copy(
                            mxv[:, 0:L], ybuf[:, 0:L]), waits=[t_sil[0]])
                    else:
                        t_mx = p.step("V", lambda v, c=c, L=L: v.tensor_tensor(
                            mxv[:, 516 * (c % 2):516 * (c % 2) + L],
                            ybuf[:, 514 * c:514 * c + L],
                            mxv[:, 516 * ((c - 1) % 2):516 * ((c - 1) % 2) + L],
                            ALU.max), waits=[t_sil[c]])
                mxo = 516
                t_red = []
                for q, (ci, chq) in enumerate(tcs):
                    t_tr = p.step("T", lambda t, q=q, chq=chq: t.transpose(
                        psA[0:chq, 128 * q:128 * q + 128],
                        mxv[:, mxo + 128 * q:mxo + 128 * q + chq], identf[:]),
                        waits=([t_mx, (t_ev2[-1] if t_ev2 else None)]
                               + t_sA.get(ti_ + 1, []))
                        if q == 0 else [t_red[q - 1]])
                    t_red.append(p.step("V", lambda v, q=q, chq=chq: v.reduce_max(
                        s2m[0:chq, q:q + 1], psA[0:chq, 128 * q:128 * q + 128],
                        AX.X), waits=[t_tr]))
                tM = p.step("V", lambda v: v.tensor_scalar(
                    s2M[:], s2m[:], SILU_MIN, None, ALU.max), waits=[t_red[-1]])
                p.step("V", lambda v: v.reciprocal(s2vr[:], s2M[:]))
                tsv = p.step("V", lambda v: v.tensor_scalar(
                    s2v[:], s2vr[:], 127.0, None, ALU.mult))
                th2 = p.step("V", lambda v: v.tensor_scalar(
                    hr2[:], s2M[:], mw2o127h[:, 0:1], None, ALU.mult),
                    waits=[tM, t_m2b])
                t_s2b = None
                for q, (ci, chq) in enumerate(tcs):
                    t1 = p.step("V", lambda v, q=q: v.tensor_scalar(
                        srep[:], allones[:], s2v[:, q:q + 1], None, ALU.mult),
                        waits=[tsv, t_s2b])
                    t2 = p.step("T", lambda t, q=q: t.transpose(
                        psA[:, 128 * q:128 * q + 128], srep[:], identf[:]),
                        waits=[t1, t_red[q]])
                    t_s2b = p.step("V", lambda v, q=q, chq=chq: v.tensor_copy(
                        s2b[:, 128 * q:128 * q + chq],
                        psA[:, 128 * q:128 * q + chq]), waits=[t2])
                if ti_ == 0 and tag == "f1":
                    pass
                t_w2p = {}
                t_mm2 = {}
                for c in range(32):
                    zs = 516 * (c % 2)
                    hs = 514 * (c % 2)
                    tz2 = p.step("G", lambda g, c=c, L=L, zs=zs: g.tensor_tensor(
                        zr[:, zs:zs + L], ybuf[:, 514 * c:514 * c + L],
                        s2b[:, 0:L], ALU.mult),
                        waits=[t_s2b, t_sil[c], t_hq.get(c - 2)])
                    ti2 = p.step("V", lambda v, L=L, zs=zs: v.tensor_scalar(
                        i16r[:, zs:zs + L], zr[:, zs:zs + L], 256.0, None,
                        ALU.add), waits=[tz2, t_hq.get(c - 2)])
                    ta2 = p.step("V", lambda v, L=L, zs=zs:
                        v.scalar_tensor_tensor(
                            arb[:, zs:zs + L], zr[:, zs:zs + L], 255.5,
                            i16r[:, zs:zs + L], ALU.add, ALU.is_ge))
                    t_hq[c] = p.step("V", lambda v, c=c, L=L, zs=zs, hs=hs:
                        v.scalar_tensor_tensor(
                            hqr[:, hs:hs + L], i16r[:, zs:zs + L], -256.0,
                            arb[:, zs:zs + L], ALU.add, ALU.add),
                        waits=[ta2, t_mm2.get(c - 2)])
                    if L > 512:
                        t_hqm = p.step("V", lambda v, c=c, hs=hs: v.tensor_copy(
                            hqmini[:, 2 * c:2 * c + 2], hqr[:, hs + 512:hs + 514]),
                            waits=[t_hq[c]])
                    cslot = c % 2
                    t_w2p[c] = p.dma("Y", lambda y, c=c, cslot=cslot: y.dma_start(
                        out=w2r[:, 1024 * cslot:1024 * cslot + 1024],
                        in_=wq2s_d[c, :, :]),
                        "dc" if cslot == 0 else "dd",
                        waits=[t_mm2.get(c - 2),
                               t_wq_dma if (ti_ == 0 and c < 2 and tag == "f1")
                               else None,
                               (t_mm2_l if (c < 2 and ti_ > 0) else None)])
                    tml = None
                    for j in range(min(4, len(tcs))):
                        psv = [ps1a, ps1b, psA, ps2x][j]
                        for nh in range(2):
                            tml = p.step("T", lambda t, c=c, j=j, cslot=cslot,
                                         psv=psv, hs=hs, nh=nh:
                                t.matmul(psv[:, 512 * nh:512 * nh + 512],
                                         hqr[:, hs + 128 * j:hs + 128 * j + 128],
                                         w2r[:, 1024 * cslot + 512 * nh:
                                             1024 * cslot + 512 * nh + 512],
                                         start=(c == 0), stop=(c == 31)),
                                waits=([t_hq[c], t_w2p[c]]
                                       + ([t_tapC[(ti_, 31)], t_s2b]
                                          + t_ev2 if c == 0 else []))
                                if (j == 0 and nh == 0) else [])
                    t_mm2[c] = tml
                t_mm2_l = t_mm2[31]
                t_ev2 = []
                for j in range(min(4, len(tcs))):
                    psv = [ps1a, ps1b, psA, ps2x][j]
                    ci = T0 // 128 + j
                    t_ev2.append(p.step("V", lambda v, ci=ci, psv=psv, j=j:
                        v.scalar_tensor_tensor(
                            xb[:, D * ci:D * ci + 1024], psv[:, 0:1024],
                            hr2[:, j:j + 1], xb[:, D * ci:D * ci + 1024],
                            ALU.mult, ALU.add),
                        waits=[t_mm2[31], th2]))
            t_mm2m = {}
            t_w2m = {}
            for c in range(32):
                cslot = c % 2
                t_w2m[c] = p.dma("Y", lambda y, c=c, cslot=cslot: y.dma_start(
                    out=w2r[:, 1024 * cslot:1024 * cslot + 1024],
                    in_=wq2s_d[c, :, :]),
                    "dc" if cslot == 0 else "dd",
                    waits=[t_mm2m.get(c - 2), t_mm2_l if c < 2 else None])
                for nh in range(2):
                    t_mm2m[c] = p.step("T", lambda t, c=c, cslot=cslot, nh=nh:
                        t.matmul(
                            ps2x[0:2, 512 * nh:512 * nh + 512],
                            hqmini[:, 2 * c:2 * c + 2],
                            w2r[:, 1024 * cslot + 512 * nh:
                                1024 * cslot + 512 * nh + 512],
                            start=(c == 0), stop=(c == 31)),
                        waits=[t_w2m[c], t_hqm] + (t_ev2 if c == 0 else [])
                        if nh == 0 else [])
            t_out = p.step("V", lambda v: v.scalar_tensor_tensor(
                xb[0:2, 16 * D:17 * D], ps2x[0:2, 0:1024], hr2[0:2, 4:5],
                xb[0:2, 16 * D:17 * D], ALU.mult, ALU.add),
                waits=[t_mm2m[31], th2])
            return [t_out] + t_ev2

        with nc.psum_tensor("ps2x_f1", [128, 1024], F32) as ps2x:
            tF1 = emit_F("f1", t_load)
        p.barrier()
        if "x1" in dbg:
            for i in range(NCH):
                p.dma("Y", lambda y, i=i: y.dma_start(
                    out=dbg["x1"][128 * i:128 * i + CH(i), :],
                    in_=xb[0:CH(i), D * i:D * i + D]), "ds")
            p.barrier()

        with ExitStack() as ast:
            psB = ast.enter_context(nc.psum_tensor("psB", [128, 1024], BF16))
            psKV = ast.enter_context(nc.psum_tensor("psKV", [128, 512], F32))

            t_p = None
            for hpair in range(8):
                for h2 in range(2):
                    h = 2 * hpair + h2
                    tl1 = p.dma("Y", lambda y, h=h: y.dma_start(
                        out=srep[:, 0:64], in_=proj_d[h, 0:128, :]), "dc",
                        waits=[t_p])
                    tl2 = p.dma("Y", lambda y, h=h: y.dma_start(
                        out=srep[:, 64:128], in_=proj_d[h, 128:256, :]), "dc",
                        waits=[t_p])
                    tc1 = p.step("V", lambda v: v.tensor_copy(
                        frb[:, 0:128], srep[:, 0:128]), waits=[tl1, tl2, t_p])
                    tt1 = p.step("T", lambda t: t.transpose(
                        psB[0:64, 0:128], frb[:, 0:64], identb[:]), waits=[tc1])
                    tt2 = p.step("T", lambda t: t.transpose(
                        psB[0:64, 128:256], frb[:, 64:128], identb[:]))
                    t_p = p.step("V", lambda v, hpair=hpair, h2=h2: v.tensor_copy(
                        projTc[64 * h2:64 * h2 + 64,
                               256 * hpair:256 * hpair + 256],
                        psB[0:64, 0:256]), waits=[tt2, ("T", tt2[1])])

            t_bv = p.dma("Y", lambda y: y.dma_start(out=bvrow[:], in_=bv_d[:, :]),
                         "dk", waits=[t_p])
            t_st2 = emit_stats(True, None)
            t_pp2, t_rstd2 = emit_params(False, t_st2)

            tkvz = p.step("V", lambda v: v.memset(kvloc[:], 0.0))
            tvz = p.step("V", lambda v: v.memset(vtm[:], 0.0),
                         waits=[t_wk_done, t_wq_dma])
            tvon = p.step("V", lambda v: v.memset(
                vtm[:, :].rearrange("p (tc hh c) -> p tc hh c", tc=5, c=65)
                [:, :, :, 64:65], 1.0), waits=[tvz])

            def y2T_build(si, T0, L, slot, t_free):
                tcs = chunks_of(T0, L)
                toks = []
                tprev = None
                for j, (ci, ch) in enumerate(tcs):
                    tz = p.step("S", lambda s, ci=ci, ch=ch: s.activation(
                        frb[0:ch, 0:1024], xb[0:ch, D * ci:D * ci + 1024],
                        AF.Identity, scale=prm["prstd"][0:ch, ci:ci + 1],
                        bias=prm["aB"][0:ch, ci:ci + 1]),
                        waits=[t_pp2, t_rstd2, tprev] + (t_free if j == 0 else []))
                    tts = None
                    for k in range(8):
                        tts = p.step("T", lambda t, k=k, ch=ch: t.transpose(
                            psB[:, 128 * k:128 * k + ch],
                            frb[0:ch, 128 * k:128 * k + 128],
                            identb[0:ch, 0:ch]),
                            waits=[tz, tprev] if k == 0 else [])
                    tprev = p.step("V", lambda v, j=j, ch=ch, slot=slot:
                        v.tensor_copy(
                            xqT3[slot][:, :, 128 * j:128 * j + ch],
                            psB[:, 0:1024].rearrange("p (k c) -> p k c", k=8)
                            [:, :, 0:ch]),
                        waits=[tts])
                    toks.append(tprev)
                return toks

            t_kv_ev = [tkvz, tkvz]
            t_qk = {}
            t_xk = {}
            t_y2 = {0: y2T_build(0, TILES[0][0], TILES[0][1], 0,
                                 [("T", p.cnt["T"])])}
            t_xq_dma = []
            for si, (T0, L) in enumerate(TILES):
                slot = si % 2
                tcs = chunks_of(T0, L)
                if si + 1 < len(TILES):
                    t_y2[si + 1] = y2T_build(si + 1, TILES[si + 1][0],
                                             TILES[si + 1][1], 1 - slot,
                                             [t_xk.get((si - 1, 15)),
                                              t_y2[si][-1]])
                t_vev = {}
                t_wv = {}
                t_ve_last = None
                for vc in range(8):
                    wslot = vc % 2
                    t_wv[vc] = p.dma("Y", lambda y, vc=vc, wslot=wslot:
                        y.dma_start(out=wvr[:, 1024 * wslot:1024 * wslot + 1024],
                                    in_=wvs_d[vc, :, :]),
                        "da" if wslot == 0 else "db",
                        waits=[t_wk_done if (si == 0 and vc < 2) else None,
                               t_vev.get(vc - 2),
                               (t_xk.get((si - 1, 14 + vc)) if vc < 2 else None)])
                    tml = None
                    for d in range(8):
                        tml = p.step("T", lambda t, d=d, wslot=wslot, slot=slot:
                            t.matmul(ps1b[:, 0:512],
                                     wvr[:, 1024 * wslot + 128 * d:
                                         1024 * wslot + 128 * d + 128],
                                     xqT3[slot][:, d, 0:512],
                                     start=(d == 0), stop=False),
                            waits=([t_wv[vc], t_vev.get(vc - 1)] + t_y2[si])
                            if d == 0 else [])
                        if L > 512:
                            tml = p.step("T", lambda t, d=d, wslot=wslot,
                                         slot=slot, L=L:
                                t.matmul(ps1b[:, 512:L],
                                         wvr[:, 1024 * wslot + 128 * d:
                                             1024 * wslot + 128 * d + 128],
                                         xqT3[slot][:, d, 512:L],
                                         start=(d == 0), stop=False))
                    b0 = 64 * (vc // 4)
                    bc = 128 * (vc % 4)
                    tml = p.step("T", lambda t, b0=b0, bc=bc: t.matmul(
                        ps1b[:, 0:512], bvrow[b0:b0 + 1, bc:bc + 128],
                        onesw[b0:b0 + 1, 0:512], start=False, stop=True),
                        waits=[t_bv])
                    if L > 512:
                        tml = p.step("T", lambda t, b0=b0, bc=bc, L=L: t.matmul(
                            ps1b[:, 512:L],
                            bvrow[b0:b0 + 1, bc:bc + 128],
                            onesw[b0:b0 + 1, 0:L - 512], start=False, stop=True))
                    tvs = p.step("V", lambda v, L=L: v.tensor_copy(
                        vstg[:, 0:L], ps1b[:, 0:L]), waits=[tml])
                    for j, (ci, ch) in enumerate(tcs):
                        ttv = p.step("T", lambda t, j=j, ch=ch: t.transpose(
                            psB[0:ch, 0:128], vstg[:, 128 * j:128 * j + ch],
                            identb[:]), waits=[tvs, t_ve_last])
                        t_ve_last = p.step("V", lambda v, j=j, ch=ch, vc=vc:
                            v.tensor_copy(
                                vtm[0:ch, 1040 * j + 130 * vc:
                                    1040 * j + 130 * vc + 130]
                                .rearrange("p (hh c) -> p hh c", c=65)
                                [:, :, 0:64],
                                psB[0:ch, 0:128]
                                .rearrange("p (hh c) -> p hh c", c=64)),
                            waits=[ttv, tvon])
                    t_vev[vc] = t_ve_last
                for qkc in range(16):
                    wslot = qkc % 2
                    t_qk[(si, qkc)] = p.dma("Y", lambda y, qkc=qkc, wslot=wslot:
                        y.dma_start(
                            out=wqkr[:, 1024 * wslot:1024 * wslot + 1024],
                            in_=wqks_d[qkc, :, :]),
                        "da" if wslot == 0 else "db",
                        waits=[(t_vev.get(6 + qkc) if qkc < 2 else None),
                               t_xk.get((si, qkc - 2))])
                    tml = None
                    for d in range(8):
                        tml = p.step("T", lambda t, d=d, wslot=wslot, slot=slot, L=L:
                            t.matmul(ps1a[:, 0:512],
                                     wqkr[:, 1024 * wslot + 128 * d:
                                          1024 * wslot + 128 * d + 128],
                                     xqT3[slot][:, d, 0:512],
                                     start=(d == 0), stop=(d == 7)),
                            waits=[t_qk[(si, qkc)], t_xk.get((si, qkc - 1))]
                            if d == 0 else [])
                        if L > 512:
                            tml = p.step("T", lambda t, d=d, wslot=wslot,
                                         slot=slot, L=L:
                                t.matmul(ps1a[:, 512:L],
                                         wqkr[:, 1024 * wslot + 128 * d:
                                              1024 * wslot + 128 * d + 128],
                                         xqT3[slot][:, d, 512:L],
                                         start=(d == 0), stop=(d == 7)))
                    if qkc < 8:
                        ss_ = (qkc % 2) * 516
                        txe = p.step("S", lambda s, qkc=qkc, L=L, ss_=ss_:
                            s.activation(
                                xstg[:, ss_:ss_ + L],
                                ps1a[:, 0:L], AF.Identity, scale=DN,
                                bias=bqkvDN[:, qkc:qkc + 1]),
                            waits=[tml, t_bqdn,
                                   (t_xq_dma[-2] if len(t_xq_dma) >= 2
                                    else None)])
                        t_xk[(si, qkc)] = txe
                        t_xq_dma.append(p.dma("Y", lambda y, qkc=qkc, T0=T0,
                                              L=L, ss_=ss_:
                            y.dma_start(out=xqs_d[qkc, :, T0:T0 + L],
                                        in_=xstg[:, ss_:ss_ + L]),
                            "dq" if qkc % 2 == 0 else "dp",
                            waits=[txe]))
                    else:
                        kc = qkc - 8
                        t_xkev = p.step("S", lambda s, L=L, qkc=qkc: s.activation(
                            xkr[:, 0:L], ps1a[:, 0:L], AF.Identity,
                            scale=DN, bias=bqkvDN[:, qkc:qkc + 1]),
                            waits=[tml, t_bqdn])
                        t_xk[(si, qkc)] = t_xkev
                        t_fmprev = None
                        t_kvprev = {}
                        for j, (ci, chq) in enumerate(tcs):
                            p.step("T", lambda t, kc=kc, j=j, chq=chq:
                                t.matmul(psA[0:chq, 0:256],
                                         xkr[0:64, 128 * j:128 * j + chq],
                                         projTc[0:64, 256 * kc:256 * kc + 256],
                                         start=True, stop=True),
                                waits=[t_xkev, t_fmprev])
                            tfm = p.step("T", lambda t, kc=kc, j=j, chq=chq:
                                t.matmul(psA[0:chq, 256:512],
                                         xkr[64:128, 128 * j:128 * j + chq],
                                         projTc[64:128, 256 * kc:256 * kc + 256],
                                         start=True, stop=True))
                            trd = p.step("V", lambda v, chq=chq: v.tensor_reduce(
                                negmx[0:chq, 0:2],
                                psA[0:chq, 0:512]
                                .rearrange("p (h m) -> p h m", h=2),
                                AX.X, ALU.max, negate=True), waits=[tfm])
                            tb2 = p.step("V", lambda v, chq=chq: v.tensor_scalar(
                                negmx2[0:chq, :], negmx[0:chq, :], -LN16, None,
                                ALU.add))
                            if ci == 0:
                                tb2 = p.step("V", lambda v, chq=chq:
                                    v.tensor_tensor(
                                        negmx2[0:chq, :], negmx2[0:chq, :],
                                        emf[0:chq, :], ALU.add))
                            if ci == 16:
                                tb2 = p.step("V", lambda v, chq=chq:
                                    v.tensor_tensor(
                                        negmx2[0:chq, :], negmx2[0:chq, :],
                                        eml[0:chq, :], ALU.add))
                            te1 = p.step("S", lambda s, chq=chq: s.activation(
                                kpr[0:chq, 0:256], psA[0:chq, 0:256],
                                AF.Exp, bias=negmx2[0:chq, 0:1]),
                                waits=[tb2] + list(t_kvprev.values()))
                            te2 = p.step("S", lambda s, chq=chq: s.activation(
                                kpr[0:chq, 256:512], psA[0:chq, 256:512],
                                AF.Exp, bias=negmx2[0:chq, 1:2]))
                            t_fmprev = te2
                            tep = p.step("V", lambda v, chq=chq: v.tensor_scalar(
                                kpr[0:chq, 0:512], kpr[0:chq, 0:512],
                                EPS_PHI, None, ALU.add), waits=[te1, te2])
                            t_kvprev = {}
                            for h2 in range(2):
                                h = 2 * kc + h2
                                psv = psKV if h2 == 0 else psA
                                off = 0 if h2 == 0 else 512
                                tkv = p.step("T", lambda t, j=j, h=h, h2=h2,
                                             chq=chq, psv=psv, off=off,
                                             last=(j == len(tcs) - 1):
                                    t.matmul(psv[0:65, off:off + 256],
                                             vtm[0:chq, 1040 * j + 65 * h:
                                                 1040 * j + 65 * h + 65],
                                             kpr[0:chq, 256 * h2:256 * h2 + 256],
                                             start=(j == 0),
                                             stop=last),
                                    waits=[tep, t_vev.get(7)]
                                    + ([t_kv_ev[-2], t_kv_ev[-1]]
                                       if j == 0 else []))
                                t_kvprev[h2] = tkv
                                if j == len(tcs) - 1:
                                    t_kv_ev.append(p.step("V",
                                        lambda v, h=h, psv=psv, off=off:
                                        v.scalar_tensor_tensor(
                                            kvloc[:, 256 * h:256 * h + 256],
                                            psv[0:65, off:off + 256], 1.0,
                                            kvloc[:, 256 * h:256 * h + 256],
                                            ALU.mult, ALU.add),
                                        waits=[tkv, tkvz]))

            t_ci = p.dma("Y", lambda y: y.dma_start(out=ccin_d[:, :], in_=kvloc[:]),
                         "dk", waits=[t_kv_ev[-1], t_kv_ev[-2]])
            t_cc = p.step("G", lambda g: g.collective_compute(
                "AllReduce", ALU.add,
                replica_groups=[[0, 1], [2, 3], [4, 5], [6, 7]],
                ins=[ccin_d[:, :]], outs=[ccout_d[:, :]]),
                waits=[t_ci, t_wk_done])
            t_co = p.dma("Y", lambda y: y.dma_start(out=kvloc[:], in_=ccout_d[:, :]),
                         "dk", waits=[t_cc])

            t_kvf = {}
            tprev = None
            for h in range(16):
                tc1 = p.step("V", lambda v, h=h: v.tensor_copy(
                    kvb[0:65, 0:256], kvloc[:, 256 * h:256 * h + 256]),
                    waits=[t_co, tprev])
                tt1 = p.step("T", lambda t: t.transpose(
                    psB[:, 0:65], kvb[0:65, 0:128], identb[0:65, 0:65]),
                    waits=[tc1, tprev])
                tt2 = p.step("T", lambda t: t.transpose(
                    psB[:, 128:193], kvb[0:65, 128:256], identb[0:65, 0:65]))
                p.step("V", lambda v, h=h: v.tensor_copy(
                    kvfm0[:, 65 * h:65 * h + 65], psB[:, 0:65]),
                    waits=[tt2, ("T", tt2[1])])
                tprev = p.step("V", lambda v, h=h: v.tensor_copy(
                    kvfm1[:, 65 * h:65 * h + 65], psB[:, 128:193]))
                t_kvf[h] = tprev
            t_kv_ready = tprev

            t_wo = None
            for b in range(8):
                t_wo = p.dma("Y", lambda y, b=b: y.dma_start(
                    out=woutp[b][:, 0:1024], in_=wos_d[b, :, :]), "dw",
                    waits=[t_kv_ev[-1], t_kv_ev[-2], t_wk_done, t_kv_ready])
            t_bo = p.dma("Y", lambda y: y.dma_start(out=bvrow[:], in_=bout_d[:, :]),
                         "dw", waits=[t_kv_ev[-1], t_kv_ev[-2], t_kv_ready])
            t_wo_done = ("dw", p.cnt["dw"])

            t_res = None
            t_at_prev = None
            t_xql = {}
            for i in range(NCH):
                ch = CH(i)
                lslot = i % 2
                t_xql[i] = p.dma("Y", lambda y, i=i, ch=ch, lslot=lslot:
                    y.dma_start(
                        out=xqld[:, 1024 * lslot:1024 * lslot + 8 * ch]
                        .rearrange("p (qc j) -> p qc j", qc=8),
                        in_=xqs_d[:, :, 128 * i:128 * i + ch]
                        .rearrange("qc p j -> p qc j")),
                    "dq", waits=[("dq", p.cnt["dq"]) if i == 0 else None,
                                 ("dp", p.cnt["dp"]) if i == 0 else None,
                                 t_at_prev])
                t_div = []
                t_numprev = []
                for qc in range(8):
                    tfm = p.step("T", lambda t, qc=qc, i=i, ch=ch, lslot=lslot:
                        t.matmul(
                            psA[0:ch, 0:256],
                            xqld[0:64, 1024 * lslot + 128 * qc:
                                 1024 * lslot + 128 * qc + ch],
                            projTc[0:64, 256 * qc:256 * qc + 256],
                            start=True, stop=True),
                        waits=[t_kv_ready, t_at_prev, t_xql[i],
                               t_numprev[-1] if t_numprev else None]
                        + ([("S", p.cnt["S"])] if qc == 0 and i == 0 else []))
                    tfm = p.step("T", lambda t, qc=qc, ch=ch, lslot=lslot:
                        t.matmul(
                            psA[0:ch, 256:512],
                            xqld[64:128, 1024 * lslot + 128 * qc:
                                 1024 * lslot + 128 * qc + ch],
                            projTc[64:128, 256 * qc:256 * qc + 256],
                            start=True, stop=True))
                    trd = p.step("V", lambda v, ch=ch: v.tensor_reduce(
                        negmx[0:ch, 0:2],
                        psA[0:ch, 0:512].rearrange("p (h m) -> p h m", h=2),
                        AX.X, ALU.max, negate=True), waits=[tfm])
                    tb2 = p.step("V", lambda v, ch=ch: v.tensor_scalar(
                        negmx2[0:ch, :], negmx[0:ch, :], -LN16, None, ALU.add))
                    te1 = p.step("S", lambda s, ch=ch: s.activation(
                        qpr[0:ch, 0:256], psA[0:ch, 0:256], AF.Exp,
                        bias=negmx2[0:ch, 0:1]), waits=[tb2])
                    te2 = p.step("S", lambda s, ch=ch: s.activation(
                        qpr[0:ch, 256:512], psA[0:ch, 256:512], AF.Exp,
                        bias=negmx2[0:ch, 1:2]))
                    tep = p.step("V", lambda v, ch=ch: v.tensor_scalar(
                        qpr[0:ch, 0:512], qpr[0:ch, 0:512], EPS_PHI, None,
                        ALU.add), waits=[te1, te2])
                    tts = None
                    for b in range(4):
                        tts = p.step("T", lambda t, b=b, ch=ch: t.transpose(
                            psB[:, 128 * b:128 * b + ch],
                            qpr[0:ch, 128 * b:128 * b + 128],
                            identb[0:ch, 0:ch]),
                            waits=[tep] if b == 0 else [])
                    tqf = p.step("V", lambda v, ch=ch: v.tensor_copy(
                        qfm[:, 0:512].rearrange("p (b c) -> p b c", b=4)
                        [:, :, 0:ch],
                        psB[:, 0:512].rearrange("p (b c) -> p b c", b=4)
                        [:, :, 0:ch]),
                        waits=[tts])
                    t_numprev = []
                    for h2 in range(2):
                        h = 2 * qc + h2
                        psv = psKV if h2 == 0 else psA
                        off = 0 if h2 == 0 else 512
                        tnm = None
                        for fc in range(2):
                            kvf = kvfm0 if fc == 0 else kvfm1
                            tnm = p.step("T", lambda t, h=h, h2=h2, fc=fc, ch=ch,
                                         psv=psv, off=off, kvf=kvf:
                                t.matmul(psv[0:ch, off:off + 65],
                                         qfm[:, 128 * (2 * h2 + fc):
                                             128 * (2 * h2 + fc) + ch],
                                         kvf[:, 65 * h:65 * h + 65],
                                         start=(fc == 0), stop=(fc == 1)),
                                waits=[tqf, t_kvf[h],
                                       t_div[-1] if t_div else None]
                                if fc == 0 else [])
                        t_numprev.append(tnm)
                        p.step("V", lambda v, ch=ch, psv=psv, off=off:
                            v.tensor_scalar(dent[0:ch, :],
                                            psv[0:ch, off + 64:off + 65],
                                            1e-6, None, ALU.add), waits=[tnm])
                        p.step("V", lambda v, ch=ch: v.reciprocal(
                            recr[0:ch, :], dent[0:ch, :]))
                        t_div.append(p.step("V", lambda v, h=h, ch=ch, psv=psv,
                                            off=off:
                            v.tensor_scalar(
                                atile[0:ch, 64 * h:64 * h + 64],
                                psv[0:ch, off:off + 64],
                                recr[0:ch, 0:1], None, ALU.mult)))
                tts = None
                for b in range(8):
                    tts = p.step("T", lambda t, b=b, ch=ch: t.transpose(
                        psB[:, 128 * b:128 * b + ch],
                        atile[0:ch, 128 * b:128 * b + 128], identb[0:ch, 0:ch]),
                        waits=([t_div[-1], t_div[-2]] if b == 0 else []))
                tat = p.step("V", lambda v, ch=ch: v.tensor_copy(
                    aTl[:, 0:1024].rearrange("p (b c) -> p b c", b=8)[:, :, 0:ch],
                    psB[:, 0:1024].rearrange("p (b c) -> p b c", b=8)[:, :, 0:ch]),
                    waits=[tts])
                t_at_prev = tat
                tml = None
                for b in range(8):
                    for nh in range(2):
                        tml = p.step("T", lambda t, b=b, ch=ch, nh=nh: t.matmul(
                            ps1b[0:ch, 512 * nh:512 * nh + 512],
                            aTl[:, 128 * b:128 * b + ch],
                            woutp[b][:, 512 * nh:512 * nh + 512],
                            start=(b == 0), stop=False),
                            waits=[tat, t_wo_done, t_res]
                            if (b == 0 and nh == 0) else [])
                for q4 in range(2):
                    tml = p.step("T", lambda t, ch=ch, q4=q4: t.matmul(
                        ps1b[0:ch, 512 * q4:512 * q4 + 512],
                        allones[64 * q4:64 * q4 + 1, 0:ch],
                        bvrow[64 * q4:64 * q4 + 1, 0:512],
                        start=False, stop=True),
                        waits=[t_bo] if q4 == 0 else [])
                t_res = p.step("V", lambda v, i=i, ch=ch: v.scalar_tensor_tensor(
                    xb[0:ch, D * i:D * i + 1024], ps1b[0:ch, 0:1024], 1.0,
                    xb[0:ch, D * i:D * i + 1024], ALU.mult, ALU.add),
                    waits=[tml])

        p.barrier()
        if "x2" in dbg:
            for i in range(NCH):
                p.dma("Y", lambda y, i=i: y.dma_start(
                    out=dbg["x2"][128 * i:128 * i + CH(i), :],
                    in_=xb[0:CH(i), D * i:D * i + D]), "ds")
            p.barrier()

        with nc.psum_tensor("ps2x_f2", [128, 1024], F32) as ps2x:
            emit_F("f2", None)
        p.barrier()
        for i in range(NCH):
            p.dma("Y", lambda y, i=i: y.dma_start(
                out=out_d[128 * i:128 * i + CH(i), :],
                in_=xb[0:CH(i), D * i:D * i + D]), "ds")
        p.add("Y", lambda y: y.nop(), waits=[("ds", p.cnt["ds"])])

        blk = st.enter_context(nc.Block())
        blk.tensor(lambda eng: [f(eng) for f in p.q["T"]])
        blk.vector(lambda eng: [f(eng) for f in p.q["V"]])
        blk.scalar(lambda eng: [f(eng) for f in p.q["S"]])
        blk.gpsimd(lambda eng: [f(eng) for f in p.q["G"]])
        blk.sync(lambda eng: [f(eng) for f in p.q["Y"]])

    return nc


def kernel(**inputs):
    x = np.ascontiguousarray(np.asarray(inputs["L0_attn_x"], np.float32))
    w_qkv = np.ascontiguousarray(np.asarray(inputs["L0_attn_w_qkv"], np.float32))
    b_qkv = np.asarray(inputs["L0_attn_b_qkv"], np.float32)
    w_out = np.ascontiguousarray(np.asarray(inputs["L0_attn_w_out"], np.float32))
    b_out = np.asarray(inputs["L0_attn_b_out"], np.float32)
    proj = np.ascontiguousarray(np.asarray(inputs["L0_attn_proj_matrix"], np.float32))
    w1 = np.ascontiguousarray(np.asarray(inputs["L0_ffn_w1"], np.float32))
    conv_w = np.asarray(inputs["L0_ffn_conv_w"], np.float32).reshape(HID, 3)
    conv_b = np.asarray(inputs["L0_ffn_conv_b"], np.float32)
    w2 = np.ascontiguousarray(np.asarray(inputs["L0_ffn_w2"], np.float32))

    cw = [np.ascontiguousarray(conv_w[:, j].reshape(32, 128).T) for j in range(3)]
    cbt = np.ascontiguousarray(conv_b.reshape(32, 128).T)
    bqkv_t = np.ascontiguousarray(b_qkv.reshape(24, 128).T)
    bv_row = np.zeros((128, 512), np.float32)
    bv_row[0, :] = b_qkv[2048:2560]
    bv_row[64, :] = b_qkv[2560:3072]
    bout_row = np.zeros((128, 512), np.float32)
    bout_row[0, :] = b_out[0:512]
    bout_row[64, :] = b_out[512:1024]

    in_maps = []
    for core in range(8):
        b = core // 2
        half = core % 2
        lo = half * 2046
        xs = np.ascontiguousarray(x[b, lo:lo + NTOK])
        emaskf = np.zeros((128, 2), np.float32)
        emaskl = np.zeros((128, 2), np.float32)
        if half == 1:
            emaskf[0:2, :] = -1e30
        else:
            emaskl[0:2, :] = -1e30
        in_maps.append({
            "x0": xs, "w1": w1, "w2": w2,
            "cw0": cw[0], "cw1": cw[1], "cw2": cw[2], "cb": cbt,
            "wqkv": w_qkv, "bqkv": bqkv_t, "bv": bv_row,
            "wout": w_out, "bout": bout_row, "proj": proj,
            "emaskf": emaskf, "emaskl": emaskl,
        })

    nc = build_program(tuple(DEBUG))
    res = run_bass_kernel_spmd(nc, in_maps, core_ids=list(range(8)))

    out = np.empty((4, 4096, D), np.float32)
    for core in range(8):
        b = core // 2
        half = core % 2
        full = np.asarray(res.results[core]["xout"], np.float32)
        own = full[0:2048] if half == 0 else full[2:2050]
        out[b, half * 2048:(half + 1) * 2048] = own
    if DEBUG:
        kernel._dbg = [{k: np.asarray(v) for k, v in r.items()}
                       for r in res.results]
    return out


# version 11
# speedup vs baseline: 1.0333x; 1.0240x over previous
import numpy as np
import ml_dtypes
from contextlib import ExitStack

import concourse.bass as bass
import concourse.mybir as mybir
from concourse.bass_utils import run_bass_kernel_spmd

F32 = mybir.dt.float32
BF16 = mybir.dt.bfloat16
I16 = mybir.dt.int16
AF = mybir.ActivationFunctionType
ALU = mybir.AluOpType
AX = mybir.AxisListType

NTOK = 2050
NCH = 17
D = 1024
HID = 4096
NHEAD, HD, NF = 16, 64, 256
TILES = [(0, 512), (512, 512), (1024, 512), (1536, 514)]
SILU_MIN = 0.27846455
LN16 = float(np.log(16.0))
EPS_PHI = float(1e-6 / 16.0)
DN = float(64.0 ** -0.25)

DEBUG = []


def CH(i):
    return 128 if i < 16 else 2


def chunks_of(t0, L):
    out = []
    j = 0
    while 128 * j < L:
        out.append((t0 // 128 + j, min(128, L - 128 * j)))
        j += 1
    return out


class Plan:
    def __init__(self, nc, st):
        self.nc = nc
        self.st = st
        self.q = {e: [] for e in ("T", "V", "S", "G", "Y")}
        self.sems = {}
        self.cnt = {}

    def sem(self, name):
        self.sems[name] = self.st.enter_context(self.nc.semaphore(name))
        self.cnt[name] = 0

    def add(self, e, fn, waits=(), inc=None):
        sems = self.sems
        waits = tuple(w for w in waits if w is not None)
        if e in self.cnt and self.cnt[e] > 0:
            waits = ((e, self.cnt[e]),) + waits
        if inc is not None:
            self.cnt[inc[0]] += inc[1]
        tok = None if inc is None else (inc[0], self.cnt[inc[0]])

        def run(eng, _w=waits, _fn=fn, _inc=inc):
            best = {}
            for s, v in _w:
                if v > best.get(s, -1):
                    best[s] = v
            for s, v in best.items():
                eng.wait_ge(sems[s], v)
            ins = _fn(eng)
            if _inc is not None:
                ins.then_inc(sems[_inc[0]], _inc[1])
        self.q[e].append(run)
        return tok

    def step(self, e, fn, waits=()):
        return self.add(e, fn, waits, inc=(e, 1))

    def dma(self, e, fn, sem, waits=()):
        return self.add(e, fn, waits, inc=(sem, 16))

    def barrier(self):
        toks = []
        for e in ("T", "V", "S", "G"):
            toks.append(self.step(e, lambda eng: eng.nop(),
                                  waits=[(e, self.cnt[e])]))
        dma_waits = [(s, self.cnt[s]) for s in self.sems
                     if s not in ("T", "V", "S", "G", "Y") and self.cnt[s] > 0]
        ytok = self.step("Y", lambda eng: eng.nop(), waits=toks + dma_waits)
        for e in ("T", "V", "S", "G"):
            self.add(e, lambda eng: eng.nop(), waits=[ytok])
        return ytok


def build_program(debug=()):
    nc = bass.Bass("TRN2", target_bir_lowering=False, debug=False)

    x_d = nc.dram_tensor("x0", [NTOK, D], F32, kind="ExternalInput").ap()
    w1_d = nc.dram_tensor("w1", [HID, D], F32, kind="ExternalInput").ap()
    w2_d = nc.dram_tensor("w2", [D, HID], F32, kind="ExternalInput").ap()
    cw0_d = nc.dram_tensor("cw0", [128, 32], F32, kind="ExternalInput").ap()
    cw1_d = nc.dram_tensor("cw1", [128, 32], F32, kind="ExternalInput").ap()
    cw2_d = nc.dram_tensor("cw2", [128, 32], F32, kind="ExternalInput").ap()
    cb_d = nc.dram_tensor("cb", [128, 32], F32, kind="ExternalInput").ap()
    wqkv_d = nc.dram_tensor("wqkv", [D, 3 * D], F32, kind="ExternalInput").ap()
    bqkv_d = nc.dram_tensor("bqkv", [128, 24], F32, kind="ExternalInput").ap()
    bv_d = nc.dram_tensor("bv", [128, 512], F32, kind="ExternalInput").ap()
    wout_d = nc.dram_tensor("wout", [D, D], F32, kind="ExternalInput").ap()
    bout_d = nc.dram_tensor("bout", [128, 512], F32, kind="ExternalInput").ap()
    proj_d = nc.dram_tensor("proj", [NHEAD, NF, HD], F32, kind="ExternalInput").ap()
    emf_d = nc.dram_tensor("emaskf", [128, 2], F32, kind="ExternalInput").ap()
    eml_d = nc.dram_tensor("emaskl", [128, 2], F32, kind="ExternalInput").ap()
    out_d = nc.dram_tensor("xout", [NTOK, D], F32, kind="ExternalOutput").ap()
    dbg = {}
    for name in debug:
        dbg[name] = nc.dram_tensor("dbg_" + name, [NTOK, D], F32,
                                   kind="ExternalOutput").ap()

    wq1s_d = nc.dram_tensor("wq1s", [32, 128, 1024], BF16, kind="Internal").ap()
    wq2s_d = nc.dram_tensor("wq2s", [32, 128, 1024], BF16, kind="Internal").ap()
    wqks_d = nc.dram_tensor("wqks", [16, 128, 1024], BF16, kind="Internal").ap()
    wvs_d = nc.dram_tensor("wvs", [8, 128, 1024], BF16, kind="Internal").ap()
    wos_d = nc.dram_tensor("wos", [8, 128, 1024], BF16, kind="Internal").ap()
    xqs_d = nc.dram_tensor("xqs", [8, 128, NTOK], BF16, kind="Internal").ap()
    ccin_d = nc.dram_tensor("ccin", [65, 16 * NF], F32, kind="Internal").ap()
    ccout_d = nc.dram_tensor("ccout", [65, 16 * NF], F32, kind="Internal").ap()

    identb_d = nc.inline_tensor(np.eye(128).astype(ml_dtypes.bfloat16), name="idb")
    identf_d = nc.inline_tensor(np.eye(128, dtype=np.float32), name="idf")

    st = ExitStack()
    with st:
        def sb(name, shape, dtype=F32):
            return st.enter_context(nc.sbuf_tensor(name, shape, dtype))

        p = Plan(nc, st)
        for s in ("T", "V", "S", "G", "Y",
                  "da", "db", "dc", "dd", "dx", "dg", "ds", "dk", "dq", "dp", "dw"):
            p.sem(s)

        xb = sb("xb", [128, NCH * D])
        ybuf = sb("ybuf", [128, 32 * 514])
        kvloc = ybuf[0:65, 0:16 * NF]
        zr = sb("zr", [128, 2 * 516])
        i16r = sb("i16r", [128, 2 * 516], I16)
        arb = sb("arb", [128, 2 * 516], BF16)
        zrA = sb("zrA", [128, 2 * 512])
        i16A = sb("i16A", [128, 2 * 512], I16)
        arbA = sb("arbA", [128, 2 * 512], BF16)
        sqv = zrA[:, 0:1024]
        accr = zr[:, 0:516]
        mxv = sb("mxv", [128, 2 * 516])
        s2b = sb("s2b", [128, 516])
        srep = sb("sreps", [128, 128])
        bcsrc = srep[:, 0:1]
        identf = sb("identfs", [128, 128], F32)
        ones5 = sb("ones5", [128, 512], F32)
        allones = ones5[:, 0:128]
        onesw = ones5
        bvrow = sb("bvrows", [128, 512])
        PRM_N = ("ssum ssq smax smin pm pe2 pmm prstd pmabs pmyc psa pA pB "
                 "pr1 aB").split()
        prmt = sb("prmt", [128, len(PRM_N) * NCH])
        prm = {n: prmt[:, j * NCH:(j + 1) * NCH] for j, n in enumerate(PRM_N)}
        sml = sb("sml", [128, 64])
        s2m = sml[:, 0:5]; s2M = sml[:, 5:10]; s2v = sml[:, 10:15]
        s2vr = sml[:, 15:20]; hr2 = sml[:, 20:25]
        mw1o127 = sml[:, 25:26]; mw2o127h = sml[:, 26:27]
        sw1 = sml[:, 27:28]; sw2 = sml[:, 28:29]; c256 = sml[:, 29:30]
        negmx = sml[:, 30:32]; negmx2 = sml[:, 32:34]
        dent = sml[:, 34:35]; recr = sml[:, 35:36]
        emf = sml[:, 36:38]; eml = sml[:, 38:40]
        bqkv = sml[:, 40:64]
        smlb = sb("smlb", [128, 32])
        wsum = smlb[:, 0:32]
        bqkvDN = sb("bqkvDN", [128, 16])

        xqT = sb("xqT", [128, 2 * 8 * 516], BF16)
        xqT3 = [xqT[:, 0:8 * 516].rearrange("p (k c) -> p k c", k=8),
                xqT[:, 8 * 516:16 * 516].rearrange("p (k c) -> p k c", k=8)]
        hqr = sb("hqr", [128, 2 * 514], BF16)
        projTc = sb("projTc", [128, 8 * 256], BF16)
        identb = sb("identb", [128, 128], BF16)
        w1r = sb("w1r", [128, 2 * 1024], BF16)
        w2r = sb("w2r", [128, 2 * 1024], BF16)
        diagr = sb("diagr", [128, 2 * 128], BF16)
        hqmini = diagr[:, 0:64]
        xqrr = sb("xqrr", [128, 2 * 512], BF16)
        vtm = sb("vtm", [128, 5 * 1040], BF16)
        frb = sb("frb", [128, 1024], BF16)
        wqkr = w1r
        wvr = w1r
        xstg = arb
        xqld = xqT[:, 4128:6176]
        xkr = w2r[:, 0:520]
        kpr = w2r[:, 1024:1536]
        kvfm0 = xqT[:, 6176:7216]
        kvfm1 = xqT[:, 7216:8256]
        kvb = hqr[:, 0:256]
        qpr = w2r[:, 0:512]
        qfm = w2r[:, 512:1024]
        atile = xqT[:, 0:1024]
        aTl = xqT[:, 1040:2064]
        woutp = [vtm[:, 0:1024], vtm[:, 1040:2064], vtm[:, 2080:3104],
                 vtm[:, 3120:4144], vtm[:, 4160:5184],
                 w2r[:, 1024:2048], hqr[:, 0:1024], xqT[:, 2080:3104]]
        vstg = hqr[:, 0:520]

        psA = st.enter_context(nc.psum_tensor("psA", [128, 1024], F32))
        ps1a = st.enter_context(nc.psum_tensor("ps1a", [128, 1024], F32))
        ps1b = st.enter_context(nc.psum_tensor("ps1b", [128, 1024], F32))

        x16 = x_d[0:2048, :].rearrange("(c p) d -> p c d", p=128)
        xb16 = xb[:, 0:16 * D].rearrange("p (c d) -> p c d", c=16)
        p.dma("Y", lambda y: y.dma_start(out=xb16, in_=x16), "dx")
        p.dma("Y", lambda y: y.dma_start(
            out=xb[0:2, 16 * D:17 * D], in_=x_d[2048:2050, :]), "dx")
        cw0 = sb("cw0s", [128, 32]); cw1 = sb("cw1s", [128, 32])
        cw2 = sb("cw2s", [128, 32]); cb = sb("cbs", [128, 32])
        for buf, srcd in ((cw0, cw0_d), (cw1, cw1_d), (cw2, cw2_d), (cb, cb_d),
                          (bqkv, bqkv_d), (emf, emf_d), (eml, eml_d),
                          (identb, identb_d.ap()), (identf, identf_d.ap())):
            p.dma("Y", lambda y, b=buf, s=srcd: y.dma_start(out=b[:], in_=s[:, :]),
                  "dx")
        t_load = ("dx", p.cnt["dx"])

        p.step("V", lambda v: v.memset(prmt[:], 0.0))
        tm1 = p.step("V", lambda v: v.memset(sml[:, 0:36], 0.0))
        p.step("V", lambda v: v.memset(smlb[:], 0.0))
        p.step("V", lambda v: v.memset(ones5[:], 1.0))
        t_clr = p.step("V", lambda v: v.memset(srep[:], 0.0))
        t_c256 = p.step("V", lambda v: v.memset(c256[:], 256.0), waits=[tm1])
        p.barrier()
        t_bqdn = p.step("V", lambda v: v.tensor_scalar(
            bqkvDN[:], bqkv[:, 0:16], DN, None, ALU.mult), waits=[t_load])

        def bcast_scalar(dst, waits):
            tm = p.step("T", lambda t: t.matmul(ps1a[:, 0:1], allones[:], bcsrc[:],
                                                start=True, stop=True), waits=waits)
            return p.step("V", lambda v, d=dst: v.tensor_copy(d[:], ps1a[:, 0:1]),
                          waits=[tm])

        wtile = ybuf[:, 0:8192]
        wqstg = vtm[:, 3072:4096]
        wstg = vtm[:, 4096:5120]
        wkstg = vtm[:, 0:3072]

        def mean_abs(w_src, ntiles, rowlen, dst, t_start=()):
            tws = []
            for t in range(ntiles):
                sl = (t % 2) * 4096
                tdw = p.dma("Y", lambda y, t=t, sl=sl, rowlen=rowlen: y.dma_start(
                    out=wtile[:, sl:sl + rowlen],
                    in_=w_src[t * 128:(t + 1) * 128, :]),
                    "da" if t % 2 == 0 else "db",
                    waits=[tws[t - 2]] if t >= 2 else list(t_start))
                tws.append(p.step("V", lambda v, t=t, sl=sl, rowlen=rowlen:
                    v.reduce_sum(wsum[:, t:t + 1], wtile[:, sl:sl + rowlen],
                                 AX.X, apply_absolute_value=True), waits=[tdw]))
            t1 = p.step("V", lambda v, n=ntiles: v.reduce_sum(
                srep[:, 1:2], wsum[:, 0:n], AX.X), waits=[tws[-1], t_clr])
            t2 = p.step("T", lambda t: t.matmul(
                ps1a[0:1, 0:1], allones[:, 0:1], srep[:, 1:2],
                start=True, stop=True), waits=[t1])
            t3 = p.step("V", lambda v: v.tensor_copy(bcsrc[0:1, 0:1],
                                                     ps1a[0:1, 0:1]), waits=[t2])
            t4 = p.step("V", lambda v, n=ntiles, rl=rowlen: v.tensor_scalar(
                bcsrc[0:1, 0:1], bcsrc[0:1, 0:1], 1.0 / (n * 128 * rl),
                None, ALU.mult), waits=[t3])
            tb = bcast_scalar(dst, [t4])
            return p.step("V", lambda v, d=dst: v.tensor_scalar(
                d[:], d[:], 1e-5, None, ALU.max), waits=[tb])

        t_m1 = mean_abs(w1_d, 32, D, mw1o127)
        t_m2 = mean_abs(w2_d, 8, HID, mw2o127h, t_start=[t_m1])
        t_s1 = p.step("V", lambda v: v.reciprocal(sw1[:], mw1o127[:]), waits=[t_m1])
        t_s2w = p.step("V", lambda v: v.reciprocal(sw2[:], mw2o127h[:]), waits=[t_m2])
        t_m1b = p.step("V", lambda v: v.tensor_scalar(
            mw1o127[:], mw1o127[:], 1.0 / 127.0, None, ALU.mult), waits=[t_s1])
        t_m2b = p.step("V", lambda v: v.tensor_scalar(
            mw2o127h[:], mw2o127h[:], 0.5 / 127.0, None, ALU.mult), waits=[t_s2w])

        def quant_w(w_src, ntiles, rowlen, s_w, dst_fn, t_start, psP0,
                    t_wstg_start=()):
            nh = rowlen // 512
            t_lastz = {}
            t_tts = {}
            t_ev = {}
            t_dma = {}
            tq0 = None
            g = -1
            for t in range(ntiles):
                sl = (t % 2) * 4096
                tdw = p.dma("Y", lambda y, t=t, sl=sl, rowlen=rowlen: y.dma_start(
                    out=wtile[:, sl:sl + rowlen],
                    in_=w_src[t * 128:(t + 1) * 128, :]),
                    "da" if t % 2 == 0 else "db",
                    waits=[t_lastz.get(t - 2)] if t >= 2 else t_start)
                for h in range(nh):
                    g += 1
                    hs = (g % 2) * 512
                    tz = p.step("S", lambda s, sl=sl, h=h: s.activation(
                        zr[:, 0:512], wtile[:, sl + 512 * h:sl + 512 * h + 512],
                        AF.Identity, bias=c256[:, 0:1], scale=s_w[:, 0:1]),
                        waits=[tdw, tq0, t_c256]
                        + (list(t_start) if (t == 0 and h == 0) else []))
                    t_lastz[t] = tz
                    p.step("V", lambda v: v.tensor_copy(
                        i16r[:, 0:512], zr[:, 0:512]), waits=[tz])
                    p.step("V", lambda v: v.scalar_tensor_tensor(
                        arb[:, 0:512], zr[:, 0:512], -0.5, i16r[:, 0:512],
                        ALU.add, ALU.is_ge))
                    p.step("V", lambda v: v.scalar_tensor_tensor(
                        zr[:, 520:1032], i16r[:, 0:512], -256.0, arb[:, 0:512],
                        ALU.add, ALU.add))
                    tq0 = p.step("V", lambda v, hs=hs: v.tensor_scalar(
                        wqstg[:, hs:hs + 512], zr[:, 520:1032],
                        1.0, -1.0, ALU.min, ALU.max),
                        waits=[t_tts.get(g - 2)]
                        + (list(t_start) if g == 0 else []))
                    tts = None
                    for k in range(4):
                        tts = p.step("T", lambda t_, hs=hs, k=k: t_.transpose(
                            psP0[:, hs + 128 * k:hs + 128 * k + 128],
                            wqstg[:, hs + 128 * k:hs + 128 * k + 128],
                            identb[:]),
                            waits=[tq0, t_ev.get(g - 2)] if k == 0 else [])
                    t_tts[g] = tts
                    t_ev[g] = p.step("V", lambda v, hs=hs: v.tensor_copy(
                        wstg[:, hs:hs + 512], psP0[:, hs:hs + 512]),
                        waits=[tts, t_dma.get(g - 1)]
                        + (list(t_wstg_start) if g == 0 else []))
                    t_dma[g] = dst_fn(t, h, hs, t_ev[g])
            return t_ev[g]

        def w1_out(t, h, hs, tok):
            return p.dma("Y", lambda y, t=t, h=h, hs=hs: y.dma_start(
                out=wq1s_d[t, :, 512 * h:512 * h + 512],
                in_=wstg[:, hs:hs + 512]), "ds", waits=[tok])

        def w2_out(t, h, hs, tok):
            src = wstg[:, hs:hs + 512].rearrange("p (c j) -> p c j", c=4)
            dst = (wq2s_d[4 * h:4 * h + 4, :, 128 * t:128 * t + 128]
                   .rearrange("c p j -> p c j"))
            return p.dma("Y", lambda y, s=src, d=dst: y.dma_start(out=d, in_=s),
                         "ds", waits=[tok])

        with nc.psum_tensor("psP0", [128, 1024], BF16) as psP0:
            tq1 = quant_w(w1_d, 32, D, sw1, w1_out, [t_s1, t_s2w], psP0)
            t_w1dma = ("ds", p.cnt["ds"])
            tq2 = quant_w(w2_d, 8, HID, sw2, w2_out, [tq1], psP0,
                          t_wstg_start=[t_w1dma, tq1])
        t_wq_dma = ("ds", p.cnt["ds"])

        t_wk = None
        for dch in range(8):
            tg = p.dma("G", lambda g, d=dch: g.dma_start(
                out=wkstg[:, :], in_=wqkv_d[128 * d:128 * d + 128, :]), "dg",
                waits=[t_wk] if t_wk else [])
            srcqk = wkstg[:, 0:2048].rearrange("p (c j) -> p c j", c=16)
            dstqk = wqks_d[:, :, 128 * dch:128 * dch + 128].rearrange("c p j -> p c j")
            p.dma("G", lambda g, s=srcqk, d2=dstqk: g.dma_start(out=d2, in_=s), "dg",
                  waits=[tg])
            srcv = wkstg[:, 2048:3072].rearrange("p (c j) -> p c j", c=8)
            dstv = wvs_d[:, :, 128 * dch:128 * dch + 128].rearrange("c p j -> p c j")
            t_wk = p.dma("G", lambda g, s=srcv, d2=dstv: g.dma_start(out=d2, in_=s),
                         "dg", waits=[tg])
        for b in range(8):
            tg = p.dma("G", lambda g, b=b: g.dma_start(
                out=wkstg[:, 0:1024], in_=wout_d[128 * b:128 * b + 128, :]), "dg",
                waits=[t_wk])
            t_wk = p.dma("G", lambda g, b=b: g.dma_start(
                out=wos_d[b, :, :], in_=wkstg[:, 0:1024]), "dg", waits=[tg])
        t_wk_done = ("dg", p.cnt["dg"])

        def emit_stats(ln_only, t_pre):
            tS = None
            tclr0 = p.step("V", lambda v: v.memset(prmt[:], 0.0),
                           waits=[t_pre])
            for i in range(NCH):
                ch = CH(i)
                c0 = D * i
                p.step("V", lambda v, c0=c0, ch=ch, i=i: v.reduce_sum(
                    prm["ssum"][0:ch, i:i + 1], xb[0:ch, c0:c0 + 1024], AX.X),
                    waits=[t_pre, tclr0] if i == 0 else [])
                tS = p.step("S", lambda s, c0=c0, ch=ch, i=i: s.activation(
                    sqv[0:ch, 0:1024], xb[0:ch, c0:c0 + 1024], AF.Square,
                    accum_out=prm["ssq"][0:ch, i:i + 1]),
                    waits=[t_pre, tclr0] if i == 0 else [])
                if not ln_only:
                    p.step("V", lambda v, c0=c0, ch=ch, i=i: v.reduce_max(
                        prm["smax"][0:ch, i:i + 1], xb[0:ch, c0:c0 + 1024], AX.X))
                    p.step("V", lambda v, c0=c0, ch=ch, i=i: v.tensor_reduce(
                        prm["smin"][0:ch, i:i + 1], xb[0:ch, c0:c0 + 1024],
                        AX.X, ALU.min))
            return tS

        def emit_params(quant, t_stats):
            P_ = prm
            p.step("V", lambda v: v.tensor_scalar(
                P_["pm"][:], P_["ssum"][:], 1.0 / D, None, ALU.mult),
                waits=[t_stats])
            p.step("V", lambda v: v.tensor_scalar(
                P_["pe2"][:], P_["ssq"][:], 1.0 / D, None, ALU.mult))
            p.step("V", lambda v: v.tensor_tensor(
                P_["pmm"][:], P_["pm"][:], P_["pm"][:], ALU.mult))
            p.step("V", lambda v: v.tensor_tensor(
                P_["ssq"][:], P_["pe2"][:], P_["pmm"][:], ALU.subtract))
            p.step("V", lambda v: v.tensor_scalar(
                P_["pe2"][:], P_["ssq"][:], 1e-5, None, ALU.add))
            tu = p.step("V", lambda v: v.reciprocal(P_["pmm"][:], P_["pe2"][:]))
            tr = p.step("S", lambda s: s.activation(
                P_["prstd"][:], P_["pmm"][:], AF.Sqrt), waits=[tu])
            if not quant:
                p.step("V", lambda v: v.tensor_tensor(
                    P_["pe2"][:], P_["pm"][:], P_["prstd"][:], ALU.mult),
                    waits=[tr])
                tb2 = p.step("V", lambda v: v.tensor_scalar(
                    P_["aB"][:], P_["pe2"][:], -1.0, None, ALU.mult))
                return tb2, tr
            p.step("V", lambda v: v.tensor_tensor(
                P_["ssum"][:], P_["smax"][:], P_["pm"][:], ALU.subtract))
            p.step("V", lambda v: v.tensor_tensor(
                P_["pe2"][:], P_["pm"][:], P_["smin"][:], ALU.subtract))
            p.step("V", lambda v: v.tensor_tensor(
                P_["pmabs"][:], P_["ssum"][:], P_["pe2"][:], ALU.max))
            p.step("V", lambda v: v.tensor_tensor(
                P_["ssq"][:], P_["prstd"][:], P_["pmabs"][:], ALU.mult), waits=[tr])
            p.step("V", lambda v: v.tensor_scalar(
                P_["pmyc"][:], P_["ssq"][:], 1e-5, None, ALU.max))
            p.step("V", lambda v: v.reciprocal(P_["ssum"][:], P_["pmyc"][:]))
            p.step("V", lambda v: v.tensor_scalar(
                P_["psa"][:], P_["ssum"][:], 127.0, None, ALU.mult))
            p.step("V", lambda v: v.tensor_tensor(
                P_["pA"][:], P_["prstd"][:], P_["psa"][:], ALU.mult))
            p.step("V", lambda v: v.tensor_tensor(
                P_["pe2"][:], P_["pm"][:], P_["pA"][:], ALU.mult))
            p.step("V", lambda v: v.tensor_scalar(
                P_["pB"][:], P_["pe2"][:], -1.0, 256.0, ALU.mult, ALU.add))
            tpr = p.step("V", lambda v: v.tensor_scalar(
                P_["pr1"][:], P_["pmyc"][:], mw1o127[:, 0:1], None, ALU.mult),
                waits=[t_m1b])
            return tpr, tr

        def emit_F(tag, t_pre):
            t_stats = emit_stats(False, t_pre)
            t_pp, t_rstd = emit_params(True, t_stats)

            def stage_A(ti_, T0, L, slot, t_free, t_psA_free):
                toks = []
                tpz = p.step("V", lambda v, slot=slot: v.memset(
                    xqT[:, 8 * 516 * slot:8 * 516 * (slot + 1)], 0.0),
                    waits=t_free + [t_pp])
                lo = max(0, T0 - 1)
                hi = min(NTOK, T0 + L + 1)
                ci0, ci1 = lo // 128, (hi - 1) // 128
                t_chain = None
                t_ev = {}
                for ci in range(ci0, ci1 + 1):
                    ch = CH(ci)
                    idx = ci - ci0
                    t_dg = p.step("V", lambda v, ci=ci, idx=idx: v.tensor_scalar(
                        diagr[:, (idx % 2) * 128:(idx % 2) * 128 + 128],
                        identb[:], prm["pr1"][:, ci:ci + 1], None, ALU.mult),
                        waits=[t_pp, t_ev.get(2 * idx - 4)])
                    for h in range(2):
                        hx = 2 * idx + h
                        zs = (hx % 2) * 512
                        tz = p.step("S", lambda s, ci=ci, h=h, zs=zs, ch=ch:
                            s.activation(
                                zrA[0:ch, zs:zs + 512],
                                xb[0:ch, D * ci + 512 * h:D * ci + 512 * h + 512],
                                AF.Identity, scale=prm["pA"][0:ch, ci:ci + 1],
                                bias=prm["pB"][0:ch, ci:ci + 1]),
                            waits=[t_pp, t_rstd, t_chain, tpz])
                        p.step("V", lambda v, zs=zs, ch=ch: v.tensor_copy(
                            i16A[0:ch, zs:zs + 512], zrA[0:ch, zs:zs + 512]),
                            waits=[tz])
                        p.step("V", lambda v, zs=zs, ch=ch:
                            v.scalar_tensor_tensor(
                                arbA[0:ch, zs:zs + 512], zrA[0:ch, zs:zs + 512],
                                -0.5, i16A[0:ch, zs:zs + 512],
                                ALU.add, ALU.is_ge))
                        t_chain = p.step("V", lambda v, zs=zs, ch=ch:
                            v.scalar_tensor_tensor(
                                xqrr[0:ch, zs:zs + 512],
                                i16A[0:ch, zs:zs + 512], -256.0,
                                arbA[0:ch, zs:zs + 512], ALU.add, ALU.add))
                        tts = None
                        for kk in range(4):
                            k = 4 * h + kk
                            tts = p.step("T", lambda t, idx=idx, k=k, kk=kk,
                                         ch=ch, zs=zs:
                                t.matmul(psA[:, 128 * k:128 * k + ch],
                                         xqrr[0:ch, zs + 128 * kk:zs + 128 * kk + 128],
                                         diagr[0:ch, (idx % 2) * 128:
                                               (idx % 2) * 128 + ch],
                                         start=True, stop=True),
                                waits=[t_chain, t_dg, t_ev.get(hx - 2)]
                                + t_psA_free if kk == 0 else [])
                        glo = max(lo, 128 * ci)
                        ghi = min(hi, 128 * ci + ch)
                        if ghi <= glo:
                            continue
                        rc0 = glo - (T0 - 1)
                        pc0 = glo - 128 * ci
                        n = ghi - glo
                        t_ev[hx] = p.step("V", lambda v, h=h, rc0=rc0, pc0=pc0,
                                          n=n, slot=slot:
                            v.tensor_copy(
                                xqT3[slot][:, 4 * h:4 * h + 4, rc0:rc0 + n],
                                psA[:, 0:1024]
                                .rearrange("p (k c) -> p k c", k=8)
                                [:, 4 * h:4 * h + 4, pc0:pc0 + n]),
                            waits=[tts, tpz])
                        toks.append(t_ev[hx])
                return toks

            t_tapC = {}
            t_mm1 = {}
            t_sA = {}
            t_sA[0] = stage_A(0, TILES[0][0], TILES[0][1], 0,
                              [t_load if tag == "f1" else t_wq_dma], [])
            t_hqm = None
            t_mm2_l = None
            t_ev2 = []
            for ti_, (T0, L) in enumerate(TILES):
                slot = ti_ % 2
                tcs = chunks_of(T0, L)
                if ti_ + 1 < len(TILES):
                    t_sA[ti_ + 1] = stage_A(
                        ti_ + 1, TILES[ti_ + 1][0], TILES[ti_ + 1][1],
                        1 - slot,
                        [t_wq_dma, t_mm1.get((ti_ - 1, 31))],
                        list(t_ev2))
                t_sil = {}
                t_mx = None
                t_z2 = {}
                t_hq = {}
                for c in range(32):
                    cslot = c % 2
                    ps1 = ps1a if cslot == 0 else ps1b
                    t_w1 = p.dma("Y", lambda y, c=c, cslot=cslot: y.dma_start(
                        out=w1r[:, 1024 * cslot:1024 * cslot + 1024],
                        in_=wq1s_d[c, :, :]),
                        "da" if cslot == 0 else "db",
                        waits=[t_wq_dma if (ti_ == 0 and c < 2 and tag == "f1")
                               else None,
                               t_mm1.get((ti_, c - 2)),
                               t_mm1.get((ti_ - 1, 30 + c)) if c < 2 else None])
                    tml = None
                    for k in range(8):
                        tml = p.step("T", lambda t, k=k, cslot=cslot, ps1=ps1,
                                     slot=slot:
                            t.matmul(ps1[:, 0:512],
                                     w1r[:, 1024 * cslot + 128 * k:
                                         1024 * cslot + 128 * k + 128],
                                     xqT3[slot][:, k, 0:512],
                                     start=(k == 0), stop=(k == 7)),
                            waits=[t_w1, t_tapC.get((ti_, c - 2)),
                                   (t_ev2[-1] if (c < 2 and t_ev2) else None)]
                            + (t_sA[ti_] if c == 0 else []) if k == 0 else [])
                    for k in range(8):
                        tml = p.step("T", lambda t, k=k, cslot=cslot, L=L,
                                     ps1=ps1, slot=slot:
                            t.matmul(ps1[:, 512:L + 2],
                                     w1r[:, 1024 * cslot + 128 * k:
                                         1024 * cslot + 128 * k + 128],
                                     xqT3[slot][:, k, 512:L + 2],
                                     start=(k == 0), stop=(k == 7)))
                    t_mm1[(ti_, c)] = tml
                    tA_ = p.step("S", lambda s, c=c, L=L, ps1=ps1: s.activation(
                        accr[:, 0:L], ps1[:, 0:L], AF.Identity,
                        scale=cw0[:, c:c + 1], bias=cb[:, c:c + 1]),
                        waits=[tml])
                    tB_ = p.step("V", lambda v, c=c, L=L, ps1=ps1:
                        v.scalar_tensor_tensor(
                            accr[:, 0:L], ps1[:, 1:L + 1], cw1[:, c:c + 1],
                            accr[:, 0:L], ALU.mult, ALU.add), waits=[tA_])
                    t_tapC[(ti_, c)] = p.step("V", lambda v, c=c, L=L, ps1=ps1:
                        v.scalar_tensor_tensor(
                            accr[:, 0:L], ps1[:, 2:L + 2], cw2[:, c:c + 1],
                            accr[:, 0:L], ALU.mult, ALU.add))
                    t_sil[c] = p.step("S", lambda s, c=c, L=L: s.activation(
                        ybuf[:, 514 * c:514 * c + L], accr[:, 0:L], AF.Silu),
                        waits=[t_tapC[(ti_, c)]])
                    if c == 0:
                        t_mx = p.step("V", lambda v, L=L: v.tensor_copy(
                            mxv[:, 0:L], ybuf[:, 0:L]), waits=[t_sil[0]])
                    else:
                        t_mx = p.step("V", lambda v, c=c, L=L: v.tensor_tensor(
                            mxv[:, 516 * (c % 2):516 * (c % 2) + L],
                            ybuf[:, 514 * c:514 * c + L],
                            mxv[:, 516 * ((c - 1) % 2):516 * ((c - 1) % 2) + L],
                            ALU.max), waits=[t_sil[c]])
                mxo = 516
                t_red = []
                for q, (ci, chq) in enumerate(tcs):
                    t_tr = p.step("T", lambda t, q=q, chq=chq: t.transpose(
                        psA[0:chq, 128 * q:128 * q + 128],
                        mxv[:, mxo + 128 * q:mxo + 128 * q + chq], identf[:]),
                        waits=([t_mx, (t_ev2[-1] if t_ev2 else None)]
                               + t_sA.get(ti_ + 1, []))
                        if q == 0 else [t_red[q - 1]])
                    t_red.append(p.step("V", lambda v, q=q, chq=chq: v.reduce_max(
                        s2m[0:chq, q:q + 1], psA[0:chq, 128 * q:128 * q + 128],
                        AX.X), waits=[t_tr]))
                tM = p.step("V", lambda v: v.tensor_scalar(
                    s2M[:], s2m[:], SILU_MIN, None, ALU.max), waits=[t_red[-1]])
                p.step("V", lambda v: v.reciprocal(s2vr[:], s2M[:]))
                tsv = p.step("V", lambda v: v.tensor_scalar(
                    s2v[:], s2vr[:], 127.0, None, ALU.mult))
                th2 = p.step("V", lambda v: v.tensor_scalar(
                    hr2[:], s2M[:], mw2o127h[:, 0:1], None, ALU.mult),
                    waits=[tM, t_m2b])
                t_s2b = None
                for q, (ci, chq) in enumerate(tcs):
                    t1 = p.step("V", lambda v, q=q: v.tensor_scalar(
                        srep[:], allones[:], s2v[:, q:q + 1], None, ALU.mult),
                        waits=[tsv, t_s2b])
                    t2 = p.step("T", lambda t, q=q: t.transpose(
                        psA[:, 128 * q:128 * q + 128], srep[:], identf[:]),
                        waits=[t1, t_red[q]])
                    t_s2b = p.step("V", lambda v, q=q, chq=chq: v.tensor_copy(
                        s2b[:, 128 * q:128 * q + chq],
                        psA[:, 128 * q:128 * q + chq]), waits=[t2])
                if ti_ == 0 and tag == "f1":
                    pass
                t_w2p = {}
                t_mm2 = {}
                for c in range(32):
                    zs = 516 * (c % 2)
                    hs = 514 * (c % 2)
                    tz2 = p.step("G", lambda g, c=c, L=L, zs=zs: g.tensor_tensor(
                        zr[:, zs:zs + L], ybuf[:, 514 * c:514 * c + L],
                        s2b[:, 0:L], ALU.mult),
                        waits=[t_s2b, t_sil[c], t_hq.get(c - 2)])
                    ti2 = p.step("V", lambda v, L=L, zs=zs: v.tensor_scalar(
                        i16r[:, zs:zs + L], zr[:, zs:zs + L], 256.0, None,
                        ALU.add), waits=[tz2, t_hq.get(c - 2)])
                    ta2 = p.step("V", lambda v, L=L, zs=zs:
                        v.scalar_tensor_tensor(
                            arb[:, zs:zs + L], zr[:, zs:zs + L], 255.5,
                            i16r[:, zs:zs + L], ALU.add, ALU.is_ge))
                    t_hq[c] = p.step("V", lambda v, c=c, L=L, zs=zs, hs=hs:
                        v.scalar_tensor_tensor(
                            hqr[:, hs:hs + L], i16r[:, zs:zs + L], -256.0,
                            arb[:, zs:zs + L], ALU.add, ALU.add),
                        waits=[ta2, t_mm2.get(c - 2)])
                    if L > 512:
                        t_hqm = p.step("V", lambda v, c=c, hs=hs: v.tensor_copy(
                            hqmini[:, 2 * c:2 * c + 2], hqr[:, hs + 512:hs + 514]),
                            waits=[t_hq[c]])
                    cslot = c % 2
                    t_w2p[c] = p.dma("Y", lambda y, c=c, cslot=cslot: y.dma_start(
                        out=w2r[:, 1024 * cslot:1024 * cslot + 1024],
                        in_=wq2s_d[c, :, :]),
                        "dc" if cslot == 0 else "dd",
                        waits=[t_mm2.get(c - 2),
                               t_wq_dma if (ti_ == 0 and c < 2 and tag == "f1")
                               else None,
                               (t_mm2_l if (c < 2 and ti_ > 0) else None)])
                    tml = None
                    for j in range(min(4, len(tcs))):
                        psv = [ps1a, ps1b, psA, ps2x][j]
                        for nh in range(2):
                            tml = p.step("T", lambda t, c=c, j=j, cslot=cslot,
                                         psv=psv, hs=hs, nh=nh:
                                t.matmul(psv[:, 512 * nh:512 * nh + 512],
                                         hqr[:, hs + 128 * j:hs + 128 * j + 128],
                                         w2r[:, 1024 * cslot + 512 * nh:
                                             1024 * cslot + 512 * nh + 512],
                                         start=(c == 0), stop=(c == 31)),
                                waits=([t_hq[c], t_w2p[c]]
                                       + ([t_tapC[(ti_, 31)], t_s2b]
                                          + t_ev2 if c == 0 else []))
                                if (j == 0 and nh == 0) else [])
                    t_mm2[c] = tml
                t_mm2_l = t_mm2[31]
                t_ev2 = []
                for j in range(min(4, len(tcs))):
                    psv = [ps1a, ps1b, psA, ps2x][j]
                    ci = T0 // 128 + j
                    t_ev2.append(p.step("V", lambda v, ci=ci, psv=psv, j=j:
                        v.scalar_tensor_tensor(
                            xb[:, D * ci:D * ci + 1024], psv[:, 0:1024],
                            hr2[:, j:j + 1], xb[:, D * ci:D * ci + 1024],
                            ALU.mult, ALU.add),
                        waits=[t_mm2[31], th2]))
            t_mm2m = {}
            t_w2m = {}
            for c in range(32):
                cslot = c % 2
                t_w2m[c] = p.dma("Y", lambda y, c=c, cslot=cslot: y.dma_start(
                    out=w2r[:, 1024 * cslot:1024 * cslot + 1024],
                    in_=wq2s_d[c, :, :]),
                    "dc" if cslot == 0 else "dd",
                    waits=[t_mm2m.get(c - 2), t_mm2_l if c < 2 else None])
                for nh in range(2):
                    t_mm2m[c] = p.step("T", lambda t, c=c, cslot=cslot, nh=nh:
                        t.matmul(
                            ps2x[0:2, 512 * nh:512 * nh + 512],
                            hqmini[:, 2 * c:2 * c + 2],
                            w2r[:, 1024 * cslot + 512 * nh:
                                1024 * cslot + 512 * nh + 512],
                            start=(c == 0), stop=(c == 31)),
                        waits=[t_w2m[c], t_hqm] + (t_ev2 if c == 0 else [])
                        if nh == 0 else [])
            t_out = p.step("V", lambda v: v.scalar_tensor_tensor(
                xb[0:2, 16 * D:17 * D], ps2x[0:2, 0:1024], hr2[0:2, 4:5],
                xb[0:2, 16 * D:17 * D], ALU.mult, ALU.add),
                waits=[t_mm2m[31], th2])
            return [t_out] + t_ev2

        with nc.psum_tensor("ps2x_f1", [128, 1024], F32) as ps2x:
            tF1 = emit_F("f1", t_load)
        p.barrier()
        if "x1" in dbg:
            for i in range(NCH):
                p.dma("Y", lambda y, i=i: y.dma_start(
                    out=dbg["x1"][128 * i:128 * i + CH(i), :],
                    in_=xb[0:CH(i), D * i:D * i + D]), "ds")
            p.barrier()

        with ExitStack() as ast:
            psB = ast.enter_context(nc.psum_tensor("psB", [128, 1024], BF16))
            psKV = ast.enter_context(nc.psum_tensor("psKV", [128, 512], F32))

            t_p = None
            for hpair in range(8):
                for h2 in range(2):
                    h = 2 * hpair + h2
                    tl1 = p.dma("Y", lambda y, h=h: y.dma_start(
                        out=srep[:, 0:64], in_=proj_d[h, 0:128, :]), "dc",
                        waits=[t_p])
                    tl2 = p.dma("Y", lambda y, h=h: y.dma_start(
                        out=srep[:, 64:128], in_=proj_d[h, 128:256, :]), "dc",
                        waits=[t_p])
                    tc1 = p.step("V", lambda v: v.tensor_copy(
                        frb[:, 0:128], srep[:, 0:128]), waits=[tl1, tl2, t_p])
                    tt1 = p.step("T", lambda t: t.transpose(
                        psB[0:64, 0:128], frb[:, 0:64], identb[:]), waits=[tc1])
                    tt2 = p.step("T", lambda t: t.transpose(
                        psB[0:64, 128:256], frb[:, 64:128], identb[:]))
                    t_p = p.step("V", lambda v, hpair=hpair, h2=h2: v.tensor_copy(
                        projTc[64 * h2:64 * h2 + 64,
                               256 * hpair:256 * hpair + 256],
                        psB[0:64, 0:256]), waits=[tt2, ("T", tt2[1])])

            t_bv = p.dma("Y", lambda y: y.dma_start(out=bvrow[:], in_=bv_d[:, :]),
                         "dk", waits=[t_p])
            t_st2 = emit_stats(True, None)
            t_pp2, t_rstd2 = emit_params(False, t_st2)

            tkvz = p.step("V", lambda v: v.memset(kvloc[:], 0.0))
            tvz = p.step("V", lambda v: v.memset(vtm[:], 0.0),
                         waits=[t_wk_done, t_wq_dma])
            tvon = p.step("V", lambda v: v.memset(
                vtm[:, :].rearrange("p (tc hh c) -> p tc hh c", tc=5, c=65)
                [:, :, :, 64:65], 1.0), waits=[tvz])

            def y2T_build(si, T0, L, slot, t_free):
                tcs = chunks_of(T0, L)
                toks = []
                tprev = None
                for j, (ci, ch) in enumerate(tcs):
                    tz = p.step("S", lambda s, ci=ci, ch=ch: s.activation(
                        frb[0:ch, 0:1024], xb[0:ch, D * ci:D * ci + 1024],
                        AF.Identity, scale=prm["prstd"][0:ch, ci:ci + 1],
                        bias=prm["aB"][0:ch, ci:ci + 1]),
                        waits=[t_pp2, t_rstd2, tprev] + (t_free if j == 0 else []))
                    tts = None
                    for k in range(8):
                        tts = p.step("T", lambda t, k=k, ch=ch: t.transpose(
                            psB[:, 128 * k:128 * k + ch],
                            frb[0:ch, 128 * k:128 * k + 128],
                            identb[0:ch, 0:ch]),
                            waits=[tz, tprev] if k == 0 else [])
                    tprev = p.step("V", lambda v, j=j, ch=ch, slot=slot:
                        v.tensor_copy(
                            xqT3[slot][:, :, 128 * j:128 * j + ch],
                            psB[:, 0:1024].rearrange("p (k c) -> p k c", k=8)
                            [:, :, 0:ch]),
                        waits=[tts])
                    toks.append(tprev)
                return toks

            t_kv_ev = [tkvz, tkvz]
            t_qk = {}
            t_xk = {}
            t_y2 = {0: y2T_build(0, TILES[0][0], TILES[0][1], 0,
                                 [("T", p.cnt["T"])])}
            t_xq_dma = []
            for si, (T0, L) in enumerate(TILES):
                slot = si % 2
                tcs = chunks_of(T0, L)
                if si + 1 < len(TILES):
                    t_y2[si + 1] = y2T_build(si + 1, TILES[si + 1][0],
                                             TILES[si + 1][1], 1 - slot,
                                             [t_xk.get((si - 1, 15)),
                                              t_y2[si][-1]])
                t_vev = {}
                t_wv = {}
                t_ve_last = None
                for vc in range(8):
                    wslot = vc % 2
                    t_wv[vc] = p.dma("Y", lambda y, vc=vc, wslot=wslot:
                        y.dma_start(out=wvr[:, 1024 * wslot:1024 * wslot + 1024],
                                    in_=wvs_d[vc, :, :]),
                        "da" if wslot == 0 else "db",
                        waits=[t_wk_done if (si == 0 and vc < 2) else None,
                               t_vev.get(vc - 2),
                               (t_xk.get((si - 1, 14 + vc)) if vc < 2 else None)])
                    tml = None
                    for d in range(8):
                        tml = p.step("T", lambda t, d=d, wslot=wslot, slot=slot:
                            t.matmul(ps1b[:, 0:512],
                                     wvr[:, 1024 * wslot + 128 * d:
                                         1024 * wslot + 128 * d + 128],
                                     xqT3[slot][:, d, 0:512],
                                     start=(d == 0), stop=False),
                            waits=([t_wv[vc], t_vev.get(vc - 1)] + t_y2[si])
                            if d == 0 else [])
                        if L > 512:
                            tml = p.step("T", lambda t, d=d, wslot=wslot,
                                         slot=slot, L=L:
                                t.matmul(ps1b[:, 512:L],
                                         wvr[:, 1024 * wslot + 128 * d:
                                             1024 * wslot + 128 * d + 128],
                                         xqT3[slot][:, d, 512:L],
                                         start=(d == 0), stop=False))
                    b0 = 64 * (vc // 4)
                    bc = 128 * (vc % 4)
                    tml = p.step("T", lambda t, b0=b0, bc=bc: t.matmul(
                        ps1b[:, 0:512], bvrow[b0:b0 + 1, bc:bc + 128],
                        onesw[b0:b0 + 1, 0:512], start=False, stop=True),
                        waits=[t_bv])
                    if L > 512:
                        tml = p.step("T", lambda t, b0=b0, bc=bc, L=L: t.matmul(
                            ps1b[:, 512:L],
                            bvrow[b0:b0 + 1, bc:bc + 128],
                            onesw[b0:b0 + 1, 0:L - 512], start=False, stop=True))
                    tvs = p.step("V", lambda v, L=L: v.tensor_copy(
                        vstg[:, 0:L], ps1b[:, 0:L]), waits=[tml])
                    for j, (ci, ch) in enumerate(tcs):
                        ttv = p.step("T", lambda t, j=j, ch=ch: t.transpose(
                            psB[0:ch, 0:128], vstg[:, 128 * j:128 * j + ch],
                            identb[:]), waits=[tvs, t_ve_last])
                        t_ve_last = p.step("V", lambda v, j=j, ch=ch, vc=vc:
                            v.tensor_copy(
                                vtm[0:ch, 1040 * j + 130 * vc:
                                    1040 * j + 130 * vc + 130]
                                .rearrange("p (hh c) -> p hh c", c=65)
                                [:, :, 0:64],
                                psB[0:ch, 0:128]
                                .rearrange("p (hh c) -> p hh c", c=64)),
                            waits=[ttv, tvon])
                    t_vev[vc] = t_ve_last
                for qkc in range(16):
                    wslot = qkc % 2
                    t_qk[(si, qkc)] = p.dma("Y", lambda y, qkc=qkc, wslot=wslot:
                        y.dma_start(
                            out=wqkr[:, 1024 * wslot:1024 * wslot + 1024],
                            in_=wqks_d[qkc, :, :]),
                        "da" if wslot == 0 else "db",
                        waits=[(t_vev.get(6 + qkc) if qkc < 2 else None),
                               t_xk.get((si, qkc - 2))])
                    tml = None
                    for d in range(8):
                        tml = p.step("T", lambda t, d=d, wslot=wslot, slot=slot, L=L:
                            t.matmul(ps1a[:, 0:512],
                                     wqkr[:, 1024 * wslot + 128 * d:
                                          1024 * wslot + 128 * d + 128],
                                     xqT3[slot][:, d, 0:512],
                                     start=(d == 0), stop=(d == 7)),
                            waits=[t_qk[(si, qkc)], t_xk.get((si, qkc - 1))]
                            if d == 0 else [])
                        if L > 512:
                            tml = p.step("T", lambda t, d=d, wslot=wslot,
                                         slot=slot, L=L:
                                t.matmul(ps1a[:, 512:L],
                                         wqkr[:, 1024 * wslot + 128 * d:
                                              1024 * wslot + 128 * d + 128],
                                         xqT3[slot][:, d, 512:L],
                                         start=(d == 0), stop=(d == 7)))
                    if qkc < 8:
                        ss_ = (qkc % 2) * 516
                        txe = p.step("S", lambda s, qkc=qkc, L=L, ss_=ss_:
                            s.activation(
                                xstg[:, ss_:ss_ + L],
                                ps1a[:, 0:L], AF.Identity, scale=DN,
                                bias=bqkvDN[:, qkc:qkc + 1]),
                            waits=[tml, t_bqdn,
                                   (t_xq_dma[-2] if len(t_xq_dma) >= 2
                                    else None)])
                        t_xk[(si, qkc)] = txe
                        t_xq_dma.append(p.dma("Y", lambda y, qkc=qkc, T0=T0,
                                              L=L, ss_=ss_:
                            y.dma_start(out=xqs_d[qkc, :, T0:T0 + L],
                                        in_=xstg[:, ss_:ss_ + L]),
                            "dq" if qkc % 2 == 0 else "dp",
                            waits=[txe]))
                    else:
                        kc = qkc - 8
                        t_xkev = p.step("S", lambda s, L=L, qkc=qkc: s.activation(
                            xkr[:, 0:L], ps1a[:, 0:L], AF.Identity,
                            scale=DN, bias=bqkvDN[:, qkc:qkc + 1]),
                            waits=[tml, t_bqdn])
                        t_xk[(si, qkc)] = t_xkev
                        kvp_by_par = {0: {}, 1: {}}
                        fmtk = {}

                        def emit_fmk(j, kc=kc, tcs=tcs, t_xkev=t_xkev,
                                     t_v7=t_vev.get(7)):
                            chq = tcs[j][1]
                            psf = psA if j % 2 == 0 else ps1b
                            p.step("T", lambda t, kc=kc, j=j, chq=chq, psf=psf:
                                t.matmul(psf[0:chq, 0:256],
                                         xkr[0:64, 128 * j:128 * j + chq],
                                         projTc[0:64, 256 * kc:256 * kc + 256],
                                         start=True, stop=True),
                                waits=[t_xkev, t_v7])
                            fmtk[j] = p.step("T", lambda t, kc=kc, j=j, chq=chq,
                                             psf=psf:
                                t.matmul(psf[0:chq, 256:512],
                                         xkr[64:128, 128 * j:128 * j + chq],
                                         projTc[64:128, 256 * kc:256 * kc + 256],
                                         start=True, stop=True))
                        emit_fmk(0)
                        if len(tcs) > 1:
                            emit_fmk(1)
                        for j, (ci, chq) in enumerate(tcs):
                            psf = psA if j % 2 == 0 else ps1b
                            kprv = kpr if j % 2 == 0 else w2r[:, 1536:2048]
                            trd = p.step("V", lambda v, chq=chq, psf=psf:
                                v.tensor_reduce(
                                    negmx[0:chq, 0:2],
                                    psf[0:chq, 0:512]
                                    .rearrange("p (h m) -> p h m", h=2),
                                    AX.X, ALU.max, negate=True),
                                waits=[fmtk[j]])
                            tb2 = p.step("V", lambda v, chq=chq: v.tensor_scalar(
                                negmx2[0:chq, :], negmx[0:chq, :], -LN16, None,
                                ALU.add))
                            if ci == 0:
                                tb2 = p.step("V", lambda v, chq=chq:
                                    v.tensor_tensor(
                                        negmx2[0:chq, :], negmx2[0:chq, :],
                                        emf[0:chq, :], ALU.add))
                            if ci == 16:
                                tb2 = p.step("V", lambda v, chq=chq:
                                    v.tensor_tensor(
                                        negmx2[0:chq, :], negmx2[0:chq, :],
                                        eml[0:chq, :], ALU.add))
                            te1 = p.step("S", lambda s, chq=chq, psf=psf,
                                         kprv=kprv: s.activation(
                                kprv[0:chq, 0:256], psf[0:chq, 0:256],
                                AF.Exp, bias=negmx2[0:chq, 0:1]),
                                waits=[tb2] + list(kvp_by_par[j % 2].values()))
                            te2 = p.step("S", lambda s, chq=chq, psf=psf,
                                         kprv=kprv: s.activation(
                                kprv[0:chq, 256:512], psf[0:chq, 256:512],
                                AF.Exp, bias=negmx2[0:chq, 1:2]))
                            tep = p.step("V", lambda v, chq=chq, kprv=kprv:
                                v.tensor_scalar(
                                    kprv[0:chq, 0:512], kprv[0:chq, 0:512],
                                    EPS_PHI, None, ALU.add), waits=[te1, te2])
                            kvp_by_par[j % 2] = {}
                            for h2 in range(2):
                                h = 2 * kc + h2
                                psv = psKV if h2 == 0 else psA
                                off = 0 if h2 == 0 else 512
                                tkv = p.step("T", lambda t, j=j, h=h, h2=h2,
                                             chq=chq, psv=psv, off=off,
                                             kprv=kprv,
                                             last=(j == len(tcs) - 1):
                                    t.matmul(psv[0:65, off:off + 256],
                                             vtm[0:chq, 1040 * j + 65 * h:
                                                 1040 * j + 65 * h + 65],
                                             kprv[0:chq, 256 * h2:
                                                  256 * h2 + 256],
                                             start=(j == 0),
                                             stop=last),
                                    waits=[tep, t_vev.get(7)]
                                    + ([t_kv_ev[-2], t_kv_ev[-1]]
                                       if j == 0 else []))
                                kvp_by_par[j % 2][h2] = tkv
                                if j == len(tcs) - 1:
                                    t_kv_ev.append(p.step("V",
                                        lambda v, h=h, psv=psv, off=off:
                                        v.scalar_tensor_tensor(
                                            kvloc[:, 256 * h:256 * h + 256],
                                            psv[0:65, off:off + 256], 1.0,
                                            kvloc[:, 256 * h:256 * h + 256],
                                            ALU.mult, ALU.add),
                                        waits=[tkv, tkvz]))
                            if j + 2 < len(tcs):
                                emit_fmk(j + 2)

            t_ci = p.dma("Y", lambda y: y.dma_start(out=ccin_d[:, :], in_=kvloc[:]),
                         "dk", waits=[t_kv_ev[-1], t_kv_ev[-2]])
            t_cc = p.step("G", lambda g: g.collective_compute(
                "AllReduce", ALU.add,
                replica_groups=[[0, 1], [2, 3], [4, 5], [6, 7]],
                ins=[ccin_d[:, :]], outs=[ccout_d[:, :]]),
                waits=[t_ci, t_wk_done])
            t_co = p.dma("Y", lambda y: y.dma_start(out=kvloc[:], in_=ccout_d[:, :]),
                         "dk", waits=[t_cc])

            t_kvf = {}
            tprev = None
            for h in range(16):
                tc1 = p.step("V", lambda v, h=h: v.tensor_copy(
                    kvb[0:65, 0:256], kvloc[:, 256 * h:256 * h + 256]),
                    waits=[t_co, tprev])
                tt1 = p.step("T", lambda t: t.transpose(
                    psB[:, 0:65], kvb[0:65, 0:128], identb[0:65, 0:65]),
                    waits=[tc1, tprev])
                tt2 = p.step("T", lambda t: t.transpose(
                    psB[:, 128:193], kvb[0:65, 128:256], identb[0:65, 0:65]))
                p.step("V", lambda v, h=h: v.tensor_copy(
                    kvfm0[:, 65 * h:65 * h + 65], psB[:, 0:65]),
                    waits=[tt2, ("T", tt2[1])])
                tprev = p.step("V", lambda v, h=h: v.tensor_copy(
                    kvfm1[:, 65 * h:65 * h + 65], psB[:, 128:193]))
                t_kvf[h] = tprev
            t_kv_ready = tprev

            t_wo = None
            for b in range(8):
                t_wo = p.dma("Y", lambda y, b=b: y.dma_start(
                    out=woutp[b][:, 0:1024], in_=wos_d[b, :, :]), "dw",
                    waits=[t_kv_ev[-1], t_kv_ev[-2], t_wk_done, t_kv_ready])
            t_bo = p.dma("Y", lambda y: y.dma_start(out=bvrow[:], in_=bout_d[:, :]),
                         "dw", waits=[t_kv_ev[-1], t_kv_ev[-2], t_kv_ready])
            t_wo_done = ("dw", p.cnt["dw"])

            t_res = None
            t_at_prev = None
            t_xql = {}
            for i in range(NCH):
                ch = CH(i)
                lslot = i % 2
                t_xql[i] = p.dma("Y", lambda y, i=i, ch=ch, lslot=lslot:
                    y.dma_start(
                        out=xqld[:, 1024 * lslot:1024 * lslot + 8 * ch]
                        .rearrange("p (qc j) -> p qc j", qc=8),
                        in_=xqs_d[:, :, 128 * i:128 * i + ch]
                        .rearrange("qc p j -> p qc j")),
                    "dq", waits=[("dq", p.cnt["dq"]) if i == 0 else None,
                                 ("dp", p.cnt["dp"]) if i == 0 else None,
                                 t_at_prev])
                t_div = []
                fmtok = {}

                def emit_fm(qc, i=i, ch=ch, lslot=lslot):
                    psfm = psA if qc % 2 == 0 else ps1a
                    p.step("T", lambda t, qc=qc, i=i, ch=ch, lslot=lslot,
                           psfm=psfm:
                        t.matmul(
                            psfm[0:ch, 0:256],
                            xqld[0:64, 1024 * lslot + 128 * qc:
                                 1024 * lslot + 128 * qc + ch],
                            projTc[0:64, 256 * qc:256 * qc + 256],
                            start=True, stop=True),
                        waits=[t_kv_ready, t_at_prev, t_xql[i]]
                        + ([("S", p.cnt["S"])] if qc < 2 and i == 0 else []))
                    fmtok[qc] = p.step("T", lambda t, qc=qc, ch=ch, lslot=lslot,
                                       psfm=psfm:
                        t.matmul(
                            psfm[0:ch, 256:512],
                            xqld[64:128, 1024 * lslot + 128 * qc:
                                 1024 * lslot + 128 * qc + ch],
                            projTc[64:128, 256 * qc:256 * qc + 256],
                            start=True, stop=True))
                emit_fm(0)
                emit_fm(1)
                for qc in range(8):
                    psfm = psA if qc % 2 == 0 else ps1a
                    qprv = qpr if qc % 2 == 0 else xqT[:, 3104:3616]
                    qfmv = qfm if qc % 2 == 0 else xqT[:, 3616:4128]
                    trd = p.step("V", lambda v, ch=ch, psfm=psfm: v.tensor_reduce(
                        negmx[0:ch, 0:2],
                        psfm[0:ch, 0:512].rearrange("p (h m) -> p h m", h=2),
                        AX.X, ALU.max, negate=True), waits=[fmtok[qc]])
                    tb2 = p.step("V", lambda v, ch=ch: v.tensor_scalar(
                        negmx2[0:ch, :], negmx[0:ch, :], -LN16, None, ALU.add))
                    te1 = p.step("S", lambda s, ch=ch, psfm=psfm, qprv=qprv:
                        s.activation(
                            qprv[0:ch, 0:256], psfm[0:ch, 0:256], AF.Exp,
                            bias=negmx2[0:ch, 0:1]), waits=[tb2])
                    te2 = p.step("S", lambda s, ch=ch, psfm=psfm, qprv=qprv:
                        s.activation(
                            qprv[0:ch, 256:512], psfm[0:ch, 256:512], AF.Exp,
                            bias=negmx2[0:ch, 1:2]))
                    tep = p.step("V", lambda v, ch=ch, qprv=qprv: v.tensor_scalar(
                        qprv[0:ch, 0:512], qprv[0:ch, 0:512], EPS_PHI, None,
                        ALU.add), waits=[te1, te2])
                    tts = None
                    for b in range(4):
                        tts = p.step("T", lambda t, b=b, ch=ch, qprv=qprv:
                            t.transpose(
                                psB[:, 128 * b:128 * b + ch],
                                qprv[0:ch, 128 * b:128 * b + 128],
                                identb[0:ch, 0:ch]),
                            waits=[tep] if b == 0 else [])
                    tqf = p.step("V", lambda v, ch=ch, qfmv=qfmv: v.tensor_copy(
                        qfmv[:, 0:512].rearrange("p (b c) -> p b c", b=4)
                        [:, :, 0:ch],
                        psB[:, 0:512].rearrange("p (b c) -> p b c", b=4)
                        [:, :, 0:ch]),
                        waits=[tts])
                    for h2 in range(2):
                        h = 2 * qc + h2
                        psv = psKV if h2 == 0 else psA
                        off = 0 if h2 == 0 else 512
                        tnm = None
                        for fc in range(2):
                            kvf = kvfm0 if fc == 0 else kvfm1
                            tnm = p.step("T", lambda t, h=h, h2=h2, fc=fc, ch=ch,
                                         psv=psv, off=off, kvf=kvf, qfmv=qfmv:
                                t.matmul(psv[0:ch, off:off + 65],
                                         qfmv[:, 128 * (2 * h2 + fc):
                                              128 * (2 * h2 + fc) + ch],
                                         kvf[:, 65 * h:65 * h + 65],
                                         start=(fc == 0), stop=(fc == 1)),
                                waits=[tqf, t_kvf[h],
                                       t_div[-1] if t_div else None]
                                if fc == 0 else [])
                        p.step("V", lambda v, ch=ch, psv=psv, off=off:
                            v.tensor_scalar(dent[0:ch, :],
                                            psv[0:ch, off + 64:off + 65],
                                            1e-6, None, ALU.add), waits=[tnm])
                        p.step("V", lambda v, ch=ch: v.reciprocal(
                            recr[0:ch, :], dent[0:ch, :]))
                        t_div.append(p.step("V", lambda v, h=h, ch=ch, psv=psv,
                                            off=off:
                            v.tensor_scalar(
                                atile[0:ch, 64 * h:64 * h + 64],
                                psv[0:ch, off:off + 64],
                                recr[0:ch, 0:1], None, ALU.mult)))
                    if qc + 2 < 8:
                        emit_fm(qc + 2)
                tts = None
                for b in range(8):
                    tts = p.step("T", lambda t, b=b, ch=ch: t.transpose(
                        psB[:, 128 * b:128 * b + ch],
                        atile[0:ch, 128 * b:128 * b + 128], identb[0:ch, 0:ch]),
                        waits=([t_div[-1], t_div[-2]] if b == 0 else []))
                tat = p.step("V", lambda v, ch=ch: v.tensor_copy(
                    aTl[:, 0:1024].rearrange("p (b c) -> p b c", b=8)[:, :, 0:ch],
                    psB[:, 0:1024].rearrange("p (b c) -> p b c", b=8)[:, :, 0:ch]),
                    waits=[tts])
                t_at_prev = tat
                tml = None
                for b in range(8):
                    for nh in range(2):
                        tml = p.step("T", lambda t, b=b, ch=ch, nh=nh: t.matmul(
                            ps1b[0:ch, 512 * nh:512 * nh + 512],
                            aTl[:, 128 * b:128 * b + ch],
                            woutp[b][:, 512 * nh:512 * nh + 512],
                            start=(b == 0), stop=False),
                            waits=[tat, t_wo_done, t_res]
                            if (b == 0 and nh == 0) else [])
                for q4 in range(2):
                    tml = p.step("T", lambda t, ch=ch, q4=q4: t.matmul(
                        ps1b[0:ch, 512 * q4:512 * q4 + 512],
                        allones[64 * q4:64 * q4 + 1, 0:ch],
                        bvrow[64 * q4:64 * q4 + 1, 0:512],
                        start=False, stop=True),
                        waits=[t_bo] if q4 == 0 else [])
                t_res = p.step("V", lambda v, i=i, ch=ch: v.scalar_tensor_tensor(
                    xb[0:ch, D * i:D * i + 1024], ps1b[0:ch, 0:1024], 1.0,
                    xb[0:ch, D * i:D * i + 1024], ALU.mult, ALU.add),
                    waits=[tml])

        p.barrier()
        if "x2" in dbg:
            for i in range(NCH):
                p.dma("Y", lambda y, i=i: y.dma_start(
                    out=dbg["x2"][128 * i:128 * i + CH(i), :],
                    in_=xb[0:CH(i), D * i:D * i + D]), "ds")
            p.barrier()

        with nc.psum_tensor("ps2x_f2", [128, 1024], F32) as ps2x:
            emit_F("f2", None)
        p.barrier()
        for i in range(NCH):
            p.dma("Y", lambda y, i=i: y.dma_start(
                out=out_d[128 * i:128 * i + CH(i), :],
                in_=xb[0:CH(i), D * i:D * i + D]), "ds")
        p.add("Y", lambda y: y.nop(), waits=[("ds", p.cnt["ds"])])

        blk = st.enter_context(nc.Block())
        blk.tensor(lambda eng: [f(eng) for f in p.q["T"]])
        blk.vector(lambda eng: [f(eng) for f in p.q["V"]])
        blk.scalar(lambda eng: [f(eng) for f in p.q["S"]])
        blk.gpsimd(lambda eng: [f(eng) for f in p.q["G"]])
        blk.sync(lambda eng: [f(eng) for f in p.q["Y"]])

    return nc


def kernel(**inputs):
    x = np.ascontiguousarray(np.asarray(inputs["L0_attn_x"], np.float32))
    w_qkv = np.ascontiguousarray(np.asarray(inputs["L0_attn_w_qkv"], np.float32))
    b_qkv = np.asarray(inputs["L0_attn_b_qkv"], np.float32)
    w_out = np.ascontiguousarray(np.asarray(inputs["L0_attn_w_out"], np.float32))
    b_out = np.asarray(inputs["L0_attn_b_out"], np.float32)
    proj = np.ascontiguousarray(np.asarray(inputs["L0_attn_proj_matrix"], np.float32))
    w1 = np.ascontiguousarray(np.asarray(inputs["L0_ffn_w1"], np.float32))
    conv_w = np.asarray(inputs["L0_ffn_conv_w"], np.float32).reshape(HID, 3)
    conv_b = np.asarray(inputs["L0_ffn_conv_b"], np.float32)
    w2 = np.ascontiguousarray(np.asarray(inputs["L0_ffn_w2"], np.float32))

    cw = [np.ascontiguousarray(conv_w[:, j].reshape(32, 128).T) for j in range(3)]
    cbt = np.ascontiguousarray(conv_b.reshape(32, 128).T)
    bqkv_t = np.ascontiguousarray(b_qkv.reshape(24, 128).T)
    bv_row = np.zeros((128, 512), np.float32)
    bv_row[0, :] = b_qkv[2048:2560]
    bv_row[64, :] = b_qkv[2560:3072]
    bout_row = np.zeros((128, 512), np.float32)
    bout_row[0, :] = b_out[0:512]
    bout_row[64, :] = b_out[512:1024]

    in_maps = []
    for core in range(8):
        b = core // 2
        half = core % 2
        lo = half * 2046
        xs = np.ascontiguousarray(x[b, lo:lo + NTOK])
        emaskf = np.zeros((128, 2), np.float32)
        emaskl = np.zeros((128, 2), np.float32)
        if half == 1:
            emaskf[0:2, :] = -1e30
        else:
            emaskl[0:2, :] = -1e30
        in_maps.append({
            "x0": xs, "w1": w1, "w2": w2,
            "cw0": cw[0], "cw1": cw[1], "cw2": cw[2], "cb": cbt,
            "wqkv": w_qkv, "bqkv": bqkv_t, "bv": bv_row,
            "wout": w_out, "bout": bout_row, "proj": proj,
            "emaskf": emaskf, "emaskl": emaskl,
        })

    nc = build_program(tuple(DEBUG))
    res = run_bass_kernel_spmd(nc, in_maps, core_ids=list(range(8)))

    out = np.empty((4, 4096, D), np.float32)
    for core in range(8):
        b = core // 2
        half = core % 2
        full = np.asarray(res.results[core]["xout"], np.float32)
        own = full[0:2048] if half == 0 else full[2:2050]
        out[b, half * 2048:(half + 1) * 2048] = own
    if DEBUG:
        kernel._dbg = [{k: np.asarray(v) for k, v in r.items()}
                       for r in res.results]
    return out


# version 13 (final)
# speedup vs baseline: 1.0347x; 1.0013x over previous
import numpy as np
import ml_dtypes
from contextlib import ExitStack

import concourse.bass as bass
import concourse.mybir as mybir
from concourse.bass_utils import run_bass_kernel_spmd

F32 = mybir.dt.float32
BF16 = mybir.dt.bfloat16
I16 = mybir.dt.int16
AF = mybir.ActivationFunctionType
ALU = mybir.AluOpType
AX = mybir.AxisListType

NTOK = 2050
NCH = 17
D = 1024
HID = 4096
NHEAD, HD, NF = 16, 64, 256
TILES = [(0, 512), (512, 512), (1024, 512), (1536, 514)]
SILU_MIN = 0.27846455
LN16 = float(np.log(16.0))
EPS_PHI = float(1e-6 / 16.0)
DN = float(64.0 ** -0.25)

DEBUG = []


def CH(i):
    return 128 if i < 16 else 2


def chunks_of(t0, L):
    out = []
    j = 0
    while 128 * j < L:
        out.append((t0 // 128 + j, min(128, L - 128 * j)))
        j += 1
    return out


class Plan:
    def __init__(self, nc, st):
        self.nc = nc
        self.st = st
        self.q = {e: [] for e in ("T", "V", "S", "G", "Y")}
        self.sems = {}
        self.cnt = {}

    def sem(self, name):
        self.sems[name] = self.st.enter_context(self.nc.semaphore(name))
        self.cnt[name] = 0

    def add(self, e, fn, waits=(), inc=None):
        sems = self.sems
        waits = tuple(w for w in waits if w is not None)
        if e in self.cnt and self.cnt[e] > 0:
            waits = ((e, self.cnt[e]),) + waits
        if inc is not None:
            self.cnt[inc[0]] += inc[1]
        tok = None if inc is None else (inc[0], self.cnt[inc[0]])

        def run(eng, _w=waits, _fn=fn, _inc=inc):
            best = {}
            for s, v in _w:
                if v > best.get(s, -1):
                    best[s] = v
            for s, v in best.items():
                eng.wait_ge(sems[s], v)
            ins = _fn(eng)
            if _inc is not None:
                ins.then_inc(sems[_inc[0]], _inc[1])
        self.q[e].append(run)
        return tok

    def step(self, e, fn, waits=()):
        return self.add(e, fn, waits, inc=(e, 1))

    def dma(self, e, fn, sem, waits=()):
        return self.add(e, fn, waits, inc=(sem, 16))

    def barrier(self):
        toks = []
        for e in ("T", "V", "S", "G"):
            toks.append(self.step(e, lambda eng: eng.nop(),
                                  waits=[(e, self.cnt[e])]))
        dma_waits = [(s, self.cnt[s]) for s in self.sems
                     if s not in ("T", "V", "S", "G", "Y") and self.cnt[s] > 0]
        ytok = self.step("Y", lambda eng: eng.nop(), waits=toks + dma_waits)
        for e in ("T", "V", "S", "G"):
            self.add(e, lambda eng: eng.nop(), waits=[ytok])
        return ytok


def build_program(debug=()):
    nc = bass.Bass("TRN2", target_bir_lowering=False, debug=False)

    x_d = nc.dram_tensor("x0", [NTOK, D], F32, kind="ExternalInput").ap()
    w1_d = nc.dram_tensor("w1", [HID, D], F32, kind="ExternalInput").ap()
    w2_d = nc.dram_tensor("w2", [D, HID], F32, kind="ExternalInput").ap()
    cw0_d = nc.dram_tensor("cw0", [128, 32], F32, kind="ExternalInput").ap()
    cw1_d = nc.dram_tensor("cw1", [128, 32], F32, kind="ExternalInput").ap()
    cw2_d = nc.dram_tensor("cw2", [128, 32], F32, kind="ExternalInput").ap()
    cb_d = nc.dram_tensor("cb", [128, 32], F32, kind="ExternalInput").ap()
    wqkv_d = nc.dram_tensor("wqkv", [D, 3 * D], F32, kind="ExternalInput").ap()
    bqkv_d = nc.dram_tensor("bqkv", [128, 24], F32, kind="ExternalInput").ap()
    bv_d = nc.dram_tensor("bv", [128, 512], F32, kind="ExternalInput").ap()
    wout_d = nc.dram_tensor("wout", [D, D], F32, kind="ExternalInput").ap()
    bout_d = nc.dram_tensor("bout", [128, 512], F32, kind="ExternalInput").ap()
    proj_d = nc.dram_tensor("proj", [NHEAD, NF, HD], F32, kind="ExternalInput").ap()
    emf_d = nc.dram_tensor("emaskf", [128, 2], F32, kind="ExternalInput").ap()
    eml_d = nc.dram_tensor("emaskl", [128, 2], F32, kind="ExternalInput").ap()
    out_d = nc.dram_tensor("xout", [NTOK, D], F32, kind="ExternalOutput").ap()
    dbg = {}
    for name in debug:
        dbg[name] = nc.dram_tensor("dbg_" + name, [NTOK, D], F32,
                                   kind="ExternalOutput").ap()

    wq1s_d = nc.dram_tensor("wq1s", [32, 128, 1024], BF16, kind="Internal").ap()
    wq2s_d = nc.dram_tensor("wq2s", [32, 128, 1024], BF16, kind="Internal").ap()
    wqks_d = nc.dram_tensor("wqks", [16, 128, 1024], BF16, kind="Internal").ap()
    wvs_d = nc.dram_tensor("wvs", [8, 128, 1024], BF16, kind="Internal").ap()
    wos_d = nc.dram_tensor("wos", [8, 128, 1024], BF16, kind="Internal").ap()
    xqs_d = nc.dram_tensor("xqs", [8, 128, NTOK], BF16, kind="Internal").ap()
    ccin_d = nc.dram_tensor("ccin", [65, 16 * NF], F32, kind="Internal").ap()
    ccout_d = nc.dram_tensor("ccout", [65, 16 * NF], F32, kind="Internal").ap()

    identb_d = nc.inline_tensor(np.eye(128).astype(ml_dtypes.bfloat16), name="idb")
    identf_d = nc.inline_tensor(np.eye(128, dtype=np.float32), name="idf")

    st = ExitStack()
    with st:
        def sb(name, shape, dtype=F32):
            return st.enter_context(nc.sbuf_tensor(name, shape, dtype))

        p = Plan(nc, st)
        for s in ("T", "V", "S", "G", "Y",
                  "da", "db", "dc", "dd", "dx", "dg", "ds", "dk", "dq", "dp", "dw"):
            p.sem(s)

        xb = sb("xb", [128, NCH * D])
        ybuf = sb("ybuf", [128, 32 * 514])
        kvloc = ybuf[0:65, 0:16 * NF]
        zr = sb("zr", [128, 2 * 516])
        i16r = sb("i16r", [128, 2 * 516], I16)
        arb = sb("arb", [128, 2 * 516], BF16)
        zrA = sb("zrA", [128, 2 * 512])
        i16A = sb("i16A", [128, 2 * 512], I16)
        arbA = sb("arbA", [128, 2 * 512], BF16)
        sqv = zrA[:, 0:1024]
        accr = zr[:, 0:516]
        mxv = sb("mxv", [128, 2 * 516])
        s2b = sb("s2b", [128, 516])
        srep = sb("sreps", [128, 128])
        bcsrc = srep[:, 0:1]
        identf = sb("identfs", [128, 128], F32)
        ones5 = sb("ones5", [128, 512], F32)
        allones = ones5[:, 0:128]
        onesw = ones5
        bvrow = sb("bvrows", [128, 512])
        PRM_N = ("ssum ssq smax smin pm pe2 pmm prstd pmabs pmyc psa pA pB "
                 "pr1 aB").split()
        prmt = sb("prmt", [128, len(PRM_N) * NCH])
        prm = {n: prmt[:, j * NCH:(j + 1) * NCH] for j, n in enumerate(PRM_N)}
        sml = sb("sml", [128, 64])
        s2m = sml[:, 0:5]; s2M = sml[:, 5:10]; s2v = sml[:, 10:15]
        s2vr = sml[:, 15:20]; hr2 = sml[:, 20:25]
        mw1o127 = sml[:, 25:26]; mw2o127h = sml[:, 26:27]
        sw1 = sml[:, 27:28]; sw2 = sml[:, 28:29]; c256 = sml[:, 29:30]
        negmx = sml[:, 30:32]; negmx2 = sml[:, 32:34]
        dent = sml[:, 34:35]; recr = sml[:, 35:36]
        emf = sml[:, 36:38]; eml = sml[:, 38:40]
        bqkv = sml[:, 40:64]
        smlb = sb("smlb", [128, 32])
        wsum = smlb[:, 0:32]
        bqkvDN = sb("bqkvDN", [128, 16])

        xqT = sb("xqT", [128, 2 * 8 * 516], BF16)
        xqT3 = [xqT[:, 0:8 * 516].rearrange("p (k c) -> p k c", k=8),
                xqT[:, 8 * 516:16 * 516].rearrange("p (k c) -> p k c", k=8)]
        hqr = sb("hqr", [128, 2 * 514], BF16)
        projTc = sb("projTc", [128, 8 * 256], BF16)
        identb = sb("identb", [128, 128], BF16)
        w1r = sb("w1r", [128, 2 * 1024], BF16)
        w2r = sb("w2r", [128, 2 * 1024], BF16)
        diagr = sb("diagr", [128, 2 * 128], BF16)
        hqmini = diagr[:, 0:64]
        xqrr = sb("xqrr", [128, 2 * 512], BF16)
        vtm = sb("vtm", [128, 5 * 1040], BF16)
        frb = sb("frb", [128, 1024], BF16)
        wqkr = w1r
        wvr = w1r
        xstg = arb
        xqld = xqT[:, 4128:6176]
        xkr = w2r[:, 0:520]
        kpr = w2r[:, 1024:1536]
        kvfm0 = xqT[:, 6176:7216]
        kvfm1 = xqT[:, 7216:8256]
        kvb = hqr[:, 0:256]
        qpr = w2r[:, 0:512]
        qfm = w2r[:, 512:1024]
        atile = xqT[:, 0:1024]
        aTl = xqT[:, 1040:2064]
        woutp = [vtm[:, 0:1024], vtm[:, 1040:2064], vtm[:, 2080:3104],
                 vtm[:, 3120:4144], vtm[:, 4160:5184],
                 w2r[:, 1024:2048], hqr[:, 0:1024], xqT[:, 2080:3104]]
        vstg = hqr[:, 0:520]

        psA = st.enter_context(nc.psum_tensor("psA", [128, 1024], F32))
        ps1a = st.enter_context(nc.psum_tensor("ps1a", [128, 1024], F32))
        ps1b = st.enter_context(nc.psum_tensor("ps1b", [128, 1024], F32))

        x16 = x_d[0:2048, :].rearrange("(c p) d -> p c d", p=128)
        xb16 = xb[:, 0:16 * D].rearrange("p (c d) -> p c d", c=16)
        p.dma("Y", lambda y: y.dma_start(out=xb16, in_=x16), "dx")
        p.dma("Y", lambda y: y.dma_start(
            out=xb[0:2, 16 * D:17 * D], in_=x_d[2048:2050, :]), "dx")
        cw0 = sb("cw0s", [128, 32]); cw1 = sb("cw1s", [128, 32])
        cw2 = sb("cw2s", [128, 32]); cb = sb("cbs", [128, 32])
        for buf, srcd in ((cw0, cw0_d), (cw1, cw1_d), (cw2, cw2_d), (cb, cb_d),
                          (bqkv, bqkv_d), (emf, emf_d), (eml, eml_d),
                          (identb, identb_d.ap()), (identf, identf_d.ap())):
            p.dma("Y", lambda y, b=buf, s=srcd: y.dma_start(out=b[:], in_=s[:, :]),
                  "dx")
        t_load = ("dx", p.cnt["dx"])

        p.step("V", lambda v: v.memset(prmt[:], 0.0))
        tm1 = p.step("V", lambda v: v.memset(sml[:, 0:36], 0.0))
        p.step("V", lambda v: v.memset(smlb[:], 0.0))
        p.step("V", lambda v: v.memset(ones5[:], 1.0))
        t_clr = p.step("V", lambda v: v.memset(srep[:], 0.0))
        t_c256 = p.step("V", lambda v: v.memset(c256[:], 256.0), waits=[tm1])
        p.barrier()
        t_bqdn = p.step("V", lambda v: v.tensor_scalar(
            bqkvDN[:], bqkv[:, 0:16], DN, None, ALU.mult), waits=[t_load])

        def bcast_scalar(dst, waits):
            tm = p.step("T", lambda t: t.matmul(ps1a[:, 0:1], allones[:], bcsrc[:],
                                                start=True, stop=True), waits=waits)
            return p.step("V", lambda v, d=dst: v.tensor_copy(d[:], ps1a[:, 0:1]),
                          waits=[tm])

        wtile = ybuf[:, 0:8192]
        wqstg = vtm[:, 3072:4096]
        wstg = vtm[:, 4096:5120]
        wkstg = vtm[:, 0:3072]

        def mean_abs(w_src, ntiles, rowlen, dst, t_start=()):
            tws = []
            for t in range(ntiles):
                sl = (t % 2) * 4096
                tdw = p.dma("Y", lambda y, t=t, sl=sl, rowlen=rowlen: y.dma_start(
                    out=wtile[:, sl:sl + rowlen],
                    in_=w_src[t * 128:(t + 1) * 128, :]),
                    "da" if t % 2 == 0 else "db",
                    waits=[tws[t - 2]] if t >= 2 else list(t_start))
                tws.append(p.step("V", lambda v, t=t, sl=sl, rowlen=rowlen:
                    v.reduce_sum(wsum[:, t:t + 1], wtile[:, sl:sl + rowlen],
                                 AX.X, apply_absolute_value=True), waits=[tdw]))
            t1 = p.step("V", lambda v, n=ntiles: v.reduce_sum(
                srep[:, 1:2], wsum[:, 0:n], AX.X), waits=[tws[-1], t_clr])
            t2 = p.step("T", lambda t: t.matmul(
                ps1a[0:1, 0:1], allones[:, 0:1], srep[:, 1:2],
                start=True, stop=True), waits=[t1])
            t3 = p.step("V", lambda v: v.tensor_copy(bcsrc[0:1, 0:1],
                                                     ps1a[0:1, 0:1]), waits=[t2])
            t4 = p.step("V", lambda v, n=ntiles, rl=rowlen: v.tensor_scalar(
                bcsrc[0:1, 0:1], bcsrc[0:1, 0:1], 1.0 / (n * 128 * rl),
                None, ALU.mult), waits=[t3])
            tb = bcast_scalar(dst, [t4])
            return p.step("V", lambda v, d=dst: v.tensor_scalar(
                d[:], d[:], 1e-5, None, ALU.max), waits=[tb])

        t_m1 = mean_abs(w1_d, 32, D, mw1o127)
        t_m2 = mean_abs(w2_d, 8, HID, mw2o127h, t_start=[t_m1])
        t_s1 = p.step("V", lambda v: v.reciprocal(sw1[:], mw1o127[:]), waits=[t_m1])
        t_s2w = p.step("V", lambda v: v.reciprocal(sw2[:], mw2o127h[:]), waits=[t_m2])
        t_m1b = p.step("V", lambda v: v.tensor_scalar(
            mw1o127[:], mw1o127[:], 1.0 / 127.0, None, ALU.mult), waits=[t_s1])
        t_m2b = p.step("V", lambda v: v.tensor_scalar(
            mw2o127h[:], mw2o127h[:], 0.5 / 127.0, None, ALU.mult), waits=[t_s2w])

        def quant_w(w_src, ntiles, rowlen, s_w, dst_fn, t_start, psP0,
                    t_wstg_start=()):
            nh = rowlen // 512
            t_lastz = {}
            t_tts = {}
            t_ev = {}
            t_dma = {}
            tq0 = None
            g = -1
            for t in range(ntiles):
                sl = (t % 2) * 4096
                tdw = p.dma("Y", lambda y, t=t, sl=sl, rowlen=rowlen: y.dma_start(
                    out=wtile[:, sl:sl + rowlen],
                    in_=w_src[t * 128:(t + 1) * 128, :]),
                    "da" if t % 2 == 0 else "db",
                    waits=[t_lastz.get(t - 2)] if t >= 2 else t_start)
                for h in range(nh):
                    g += 1
                    hs = (g % 2) * 512
                    tz = p.step("S", lambda s, sl=sl, h=h: s.activation(
                        zr[:, 0:512], wtile[:, sl + 512 * h:sl + 512 * h + 512],
                        AF.Identity, bias=c256[:, 0:1], scale=s_w[:, 0:1]),
                        waits=[tdw, tq0, t_c256]
                        + (list(t_start) if (t == 0 and h == 0) else []))
                    t_lastz[t] = tz
                    p.step("V", lambda v: v.tensor_copy(
                        i16r[:, 0:512], zr[:, 0:512]), waits=[tz])
                    p.step("V", lambda v: v.scalar_tensor_tensor(
                        arb[:, 0:512], zr[:, 0:512], -0.5, i16r[:, 0:512],
                        ALU.add, ALU.is_ge))
                    p.step("V", lambda v: v.scalar_tensor_tensor(
                        zr[:, 520:1032], i16r[:, 0:512], -256.0, arb[:, 0:512],
                        ALU.add, ALU.add))
                    tq0 = p.step("V", lambda v, hs=hs: v.tensor_scalar(
                        wqstg[:, hs:hs + 512], zr[:, 520:1032],
                        1.0, -1.0, ALU.min, ALU.max),
                        waits=[t_tts.get(g - 2)]
                        + (list(t_start) if g == 0 else []))
                    tts = None
                    for k in range(4):
                        tts = p.step("T", lambda t_, hs=hs, k=k: t_.transpose(
                            psP0[:, hs + 128 * k:hs + 128 * k + 128],
                            wqstg[:, hs + 128 * k:hs + 128 * k + 128],
                            identb[:]),
                            waits=[tq0, t_ev.get(g - 2)] if k == 0 else [])
                    t_tts[g] = tts
                    t_ev[g] = p.step("V", lambda v, hs=hs: v.tensor_copy(
                        wstg[:, hs:hs + 512], psP0[:, hs:hs + 512]),
                        waits=[tts, t_dma.get(g - 1)]
                        + (list(t_wstg_start) if g == 0 else []))
                    t_dma[g] = dst_fn(t, h, hs, t_ev[g])
            return t_ev[g]

        def w1_out(t, h, hs, tok):
            return p.dma("Y", lambda y, t=t, h=h, hs=hs: y.dma_start(
                out=wq1s_d[t, :, 512 * h:512 * h + 512],
                in_=wstg[:, hs:hs + 512]), "ds", waits=[tok])

        def w2_out(t, h, hs, tok):
            src = wstg[:, hs:hs + 512].rearrange("p (c j) -> p c j", c=4)
            dst = (wq2s_d[4 * h:4 * h + 4, :, 128 * t:128 * t + 128]
                   .rearrange("c p j -> p c j"))
            return p.dma("Y", lambda y, s=src, d=dst: y.dma_start(out=d, in_=s),
                         "ds", waits=[tok])

        with nc.psum_tensor("psP0", [128, 1024], BF16) as psP0:
            tq1 = quant_w(w1_d, 32, D, sw1, w1_out, [t_s1, t_s2w], psP0)
            t_w1dma = ("ds", p.cnt["ds"])
            tq2 = quant_w(w2_d, 8, HID, sw2, w2_out, [tq1], psP0,
                          t_wstg_start=[t_w1dma, tq1])
        t_wq_dma = ("ds", p.cnt["ds"])

        t_wk = None
        for dch in range(8):
            tg = p.dma("G", lambda g, d=dch: g.dma_start(
                out=wkstg[:, :], in_=wqkv_d[128 * d:128 * d + 128, :]), "dg",
                waits=[t_wk] if t_wk else [])
            srcqk = wkstg[:, 0:2048].rearrange("p (c j) -> p c j", c=16)
            dstqk = wqks_d[:, :, 128 * dch:128 * dch + 128].rearrange("c p j -> p c j")
            p.dma("G", lambda g, s=srcqk, d2=dstqk: g.dma_start(out=d2, in_=s), "dg",
                  waits=[tg])
            srcv = wkstg[:, 2048:3072].rearrange("p (c j) -> p c j", c=8)
            dstv = wvs_d[:, :, 128 * dch:128 * dch + 128].rearrange("c p j -> p c j")
            t_wk = p.dma("G", lambda g, s=srcv, d2=dstv: g.dma_start(out=d2, in_=s),
                         "dg", waits=[tg])
        for b in range(8):
            tg = p.dma("G", lambda g, b=b: g.dma_start(
                out=wkstg[:, 0:1024], in_=wout_d[128 * b:128 * b + 128, :]), "dg",
                waits=[t_wk])
            t_wk = p.dma("G", lambda g, b=b: g.dma_start(
                out=wos_d[b, :, :], in_=wkstg[:, 0:1024]), "dg", waits=[tg])
        t_wk_done = ("dg", p.cnt["dg"])

        def emit_stats(ln_only, t_pre):
            tS = None
            tclr0 = p.step("V", lambda v: v.memset(prmt[:], 0.0),
                           waits=[t_pre])
            for i in range(NCH):
                ch = CH(i)
                c0 = D * i
                p.step("V", lambda v, c0=c0, ch=ch, i=i: v.reduce_sum(
                    prm["ssum"][0:ch, i:i + 1], xb[0:ch, c0:c0 + 1024], AX.X),
                    waits=[t_pre, tclr0] if i == 0 else [])
                tS = p.step("S", lambda s, c0=c0, ch=ch, i=i: s.activation(
                    sqv[0:ch, 0:1024], xb[0:ch, c0:c0 + 1024], AF.Square,
                    accum_out=prm["ssq"][0:ch, i:i + 1]),
                    waits=[t_pre, tclr0] if i == 0 else [])
                if not ln_only:
                    p.step("V", lambda v, c0=c0, ch=ch, i=i: v.reduce_max(
                        prm["smax"][0:ch, i:i + 1], xb[0:ch, c0:c0 + 1024], AX.X))
                    p.step("V", lambda v, c0=c0, ch=ch, i=i: v.tensor_reduce(
                        prm["smin"][0:ch, i:i + 1], xb[0:ch, c0:c0 + 1024],
                        AX.X, ALU.min))
            return tS

        def emit_params(quant, t_stats):
            P_ = prm
            p.step("V", lambda v: v.tensor_scalar(
                P_["pm"][:], P_["ssum"][:], 1.0 / D, None, ALU.mult),
                waits=[t_stats])
            p.step("V", lambda v: v.tensor_scalar(
                P_["pe2"][:], P_["ssq"][:], 1.0 / D, None, ALU.mult))
            p.step("V", lambda v: v.tensor_tensor(
                P_["pmm"][:], P_["pm"][:], P_["pm"][:], ALU.mult))
            p.step("V", lambda v: v.tensor_tensor(
                P_["ssq"][:], P_["pe2"][:], P_["pmm"][:], ALU.subtract))
            p.step("V", lambda v: v.tensor_scalar(
                P_["pe2"][:], P_["ssq"][:], 1e-5, None, ALU.add))
            tu = p.step("V", lambda v: v.reciprocal(P_["pmm"][:], P_["pe2"][:]))
            tr = p.step("S", lambda s: s.activation(
                P_["prstd"][:], P_["pmm"][:], AF.Sqrt), waits=[tu])
            if not quant:
                p.step("V", lambda v: v.tensor_tensor(
                    P_["pe2"][:], P_["pm"][:], P_["prstd"][:], ALU.mult),
                    waits=[tr])
                tb2 = p.step("V", lambda v: v.tensor_scalar(
                    P_["aB"][:], P_["pe2"][:], -1.0, None, ALU.mult))
                return tb2, tr
            p.step("V", lambda v: v.tensor_tensor(
                P_["ssum"][:], P_["smax"][:], P_["pm"][:], ALU.subtract))
            p.step("V", lambda v: v.tensor_tensor(
                P_["pe2"][:], P_["pm"][:], P_["smin"][:], ALU.subtract))
            p.step("V", lambda v: v.tensor_tensor(
                P_["pmabs"][:], P_["ssum"][:], P_["pe2"][:], ALU.max))
            p.step("V", lambda v: v.tensor_tensor(
                P_["ssq"][:], P_["prstd"][:], P_["pmabs"][:], ALU.mult), waits=[tr])
            p.step("V", lambda v: v.tensor_scalar(
                P_["pmyc"][:], P_["ssq"][:], 1e-5, None, ALU.max))
            p.step("V", lambda v: v.reciprocal(P_["ssum"][:], P_["pmyc"][:]))
            p.step("V", lambda v: v.tensor_scalar(
                P_["psa"][:], P_["ssum"][:], 127.0, None, ALU.mult))
            p.step("V", lambda v: v.tensor_tensor(
                P_["pA"][:], P_["prstd"][:], P_["psa"][:], ALU.mult))
            p.step("V", lambda v: v.tensor_tensor(
                P_["pe2"][:], P_["pm"][:], P_["pA"][:], ALU.mult))
            p.step("V", lambda v: v.tensor_scalar(
                P_["pB"][:], P_["pe2"][:], -1.0, 256.0, ALU.mult, ALU.add))
            tpr = p.step("V", lambda v: v.tensor_scalar(
                P_["pr1"][:], P_["pmyc"][:], mw1o127[:, 0:1], None, ALU.mult),
                waits=[t_m1b])
            return tpr, tr

        def emit_F(tag, t_pre):
            t_stats = emit_stats(False, t_pre)
            t_pp, t_rstd = emit_params(True, t_stats)

            def stage_A(ti_, T0, L, slot, t_free, t_psA_free):
                toks = []
                tpz = p.step("V", lambda v, slot=slot: v.memset(
                    xqT[:, 8 * 516 * slot:8 * 516 * (slot + 1)], 0.0),
                    waits=t_free + [t_pp])
                lo = max(0, T0 - 1)
                hi = min(NTOK, T0 + L + 1)
                ci0, ci1 = lo // 128, (hi - 1) // 128
                t_chain = None
                t_ev = {}
                for ci in range(ci0, ci1 + 1):
                    ch = CH(ci)
                    idx = ci - ci0
                    t_dg = p.step("V", lambda v, ci=ci, idx=idx: v.tensor_scalar(
                        diagr[:, (idx % 2) * 128:(idx % 2) * 128 + 128],
                        identb[:], prm["pr1"][:, ci:ci + 1], None, ALU.mult),
                        waits=[t_pp, t_ev.get(2 * idx - 4)])
                    for h in range(2):
                        hx = 2 * idx + h
                        zs = (hx % 2) * 512
                        tz = p.step("S", lambda s, ci=ci, h=h, zs=zs, ch=ch:
                            s.activation(
                                zrA[0:ch, zs:zs + 512],
                                xb[0:ch, D * ci + 512 * h:D * ci + 512 * h + 512],
                                AF.Identity, scale=prm["pA"][0:ch, ci:ci + 1],
                                bias=prm["pB"][0:ch, ci:ci + 1]),
                            waits=[t_pp, t_rstd, t_chain, tpz])
                        p.step("V", lambda v, zs=zs, ch=ch: v.tensor_copy(
                            i16A[0:ch, zs:zs + 512], zrA[0:ch, zs:zs + 512]),
                            waits=[tz])
                        p.step("V", lambda v, zs=zs, ch=ch:
                            v.scalar_tensor_tensor(
                                arbA[0:ch, zs:zs + 512], zrA[0:ch, zs:zs + 512],
                                -0.5, i16A[0:ch, zs:zs + 512],
                                ALU.add, ALU.is_ge))
                        t_chain = p.step("V", lambda v, zs=zs, ch=ch:
                            v.scalar_tensor_tensor(
                                xqrr[0:ch, zs:zs + 512],
                                i16A[0:ch, zs:zs + 512], -256.0,
                                arbA[0:ch, zs:zs + 512], ALU.add, ALU.add))
                        tts = None
                        for kk in range(4):
                            k = 4 * h + kk
                            tts = p.step("T", lambda t, idx=idx, k=k, kk=kk,
                                         ch=ch, zs=zs:
                                t.matmul(psA[:, 128 * k:128 * k + ch],
                                         xqrr[0:ch, zs + 128 * kk:zs + 128 * kk + 128],
                                         diagr[0:ch, (idx % 2) * 128:
                                               (idx % 2) * 128 + ch],
                                         start=True, stop=True),
                                waits=[t_chain, t_dg, t_ev.get(hx - 2)]
                                + t_psA_free if kk == 0 else [])
                        glo = max(lo, 128 * ci)
                        ghi = min(hi, 128 * ci + ch)
                        if ghi <= glo:
                            continue
                        rc0 = glo - (T0 - 1)
                        pc0 = glo - 128 * ci
                        n = ghi - glo
                        t_ev[hx] = p.step("V", lambda v, h=h, rc0=rc0, pc0=pc0,
                                          n=n, slot=slot:
                            v.tensor_copy(
                                xqT3[slot][:, 4 * h:4 * h + 4, rc0:rc0 + n],
                                psA[:, 0:1024]
                                .rearrange("p (k c) -> p k c", k=8)
                                [:, 4 * h:4 * h + 4, pc0:pc0 + n]),
                            waits=[tts, tpz])
                        toks.append(t_ev[hx])
                return toks

            t_tapC = {}
            t_mm1 = {}
            t_sA = {}
            t_sA[0] = stage_A(0, TILES[0][0], TILES[0][1], 0,
                              [t_load if tag == "f1" else t_wq_dma], [])
            t_hqm = None
            t_mm2_l = None
            t_ev2 = []
            for ti_, (T0, L) in enumerate(TILES):
                slot = ti_ % 2
                tcs = chunks_of(T0, L)
                if ti_ + 1 < len(TILES):
                    t_sA[ti_ + 1] = stage_A(
                        ti_ + 1, TILES[ti_ + 1][0], TILES[ti_ + 1][1],
                        1 - slot,
                        [t_wq_dma, t_mm1.get((ti_ - 1, 31))],
                        list(t_ev2))
                t_sil = {}
                t_mx = None
                t_z2 = {}
                t_hq = {}
                for c in range(32):
                    cslot = c % 2
                    ps1 = ps1a if cslot == 0 else ps1b
                    t_w1 = p.dma("Y", lambda y, c=c, cslot=cslot: y.dma_start(
                        out=w1r[:, 1024 * cslot:1024 * cslot + 1024],
                        in_=wq1s_d[c, :, :]),
                        "da" if cslot == 0 else "db",
                        waits=[t_wq_dma if (ti_ == 0 and c < 2 and tag == "f1")
                               else None,
                               t_mm1.get((ti_, c - 2)),
                               t_mm1.get((ti_ - 1, 30 + c)) if c < 2 else None])
                    tml = None
                    for k in range(8):
                        tml = p.step("T", lambda t, k=k, cslot=cslot, ps1=ps1,
                                     slot=slot:
                            t.matmul(ps1[:, 0:512],
                                     w1r[:, 1024 * cslot + 128 * k:
                                         1024 * cslot + 128 * k + 128],
                                     xqT3[slot][:, k, 0:512],
                                     start=(k == 0), stop=(k == 7)),
                            waits=[t_w1, t_tapC.get((ti_, c - 2)),
                                   (t_ev2[-1] if (c < 2 and t_ev2) else None)]
                            + (t_sA[ti_] if c == 0 else []) if k == 0 else [])
                    for k in range(8):
                        tml = p.step("T", lambda t, k=k, cslot=cslot, L=L,
                                     ps1=ps1, slot=slot:
                            t.matmul(ps1[:, 512:L + 2],
                                     w1r[:, 1024 * cslot + 128 * k:
                                         1024 * cslot + 128 * k + 128],
                                     xqT3[slot][:, k, 512:L + 2],
                                     start=(k == 0), stop=(k == 7)))
                    t_mm1[(ti_, c)] = tml
                    tA_ = p.step("S", lambda s, c=c, L=L, ps1=ps1: s.activation(
                        accr[:, 0:L], ps1[:, 0:L], AF.Identity,
                        scale=cw0[:, c:c + 1], bias=cb[:, c:c + 1]),
                        waits=[tml])
                    tB_ = p.step("V", lambda v, c=c, L=L, ps1=ps1:
                        v.scalar_tensor_tensor(
                            accr[:, 0:L], ps1[:, 1:L + 1], cw1[:, c:c + 1],
                            accr[:, 0:L], ALU.mult, ALU.add), waits=[tA_])
                    t_tapC[(ti_, c)] = p.step("V", lambda v, c=c, L=L, ps1=ps1:
                        v.scalar_tensor_tensor(
                            accr[:, 0:L], ps1[:, 2:L + 2], cw2[:, c:c + 1],
                            accr[:, 0:L], ALU.mult, ALU.add))
                    t_sil[c] = p.step("S", lambda s, c=c, L=L: s.activation(
                        ybuf[:, 514 * c:514 * c + L], accr[:, 0:L], AF.Silu),
                        waits=[t_tapC[(ti_, c)]])
                    if c == 0:
                        t_mx = p.step("V", lambda v, L=L: v.tensor_copy(
                            mxv[:, 0:L], ybuf[:, 0:L]), waits=[t_sil[0]])
                    else:
                        t_mx = p.step("V", lambda v, c=c, L=L: v.tensor_tensor(
                            mxv[:, 516 * (c % 2):516 * (c % 2) + L],
                            ybuf[:, 514 * c:514 * c + L],
                            mxv[:, 516 * ((c - 1) % 2):516 * ((c - 1) % 2) + L],
                            ALU.max), waits=[t_sil[c]])
                mxo = 516
                t_red = []
                for q, (ci, chq) in enumerate(tcs):
                    t_tr = p.step("T", lambda t, q=q, chq=chq: t.transpose(
                        psA[0:chq, 128 * q:128 * q + 128],
                        mxv[:, mxo + 128 * q:mxo + 128 * q + chq], identf[:]),
                        waits=([t_mx, (t_ev2[-1] if t_ev2 else None)]
                               + t_sA.get(ti_ + 1, []))
                        if q == 0 else [t_red[q - 1]])
                    t_red.append(p.step("V", lambda v, q=q, chq=chq: v.reduce_max(
                        s2m[0:chq, q:q + 1], psA[0:chq, 128 * q:128 * q + 128],
                        AX.X), waits=[t_tr]))
                tM = p.step("V", lambda v: v.tensor_scalar(
                    s2M[:], s2m[:], SILU_MIN, None, ALU.max), waits=[t_red[-1]])
                p.step("V", lambda v: v.reciprocal(s2vr[:], s2M[:]))
                tsv = p.step("V", lambda v: v.tensor_scalar(
                    s2v[:], s2vr[:], 127.0, None, ALU.mult))
                th2 = p.step("V", lambda v: v.tensor_scalar(
                    hr2[:], s2M[:], mw2o127h[:, 0:1], None, ALU.mult),
                    waits=[tM, t_m2b])
                t_s2b = None
                for q, (ci, chq) in enumerate(tcs):
                    t1 = p.step("V", lambda v, q=q: v.tensor_scalar(
                        srep[:], allones[:], s2v[:, q:q + 1], None, ALU.mult),
                        waits=[tsv, t_s2b])
                    t2 = p.step("T", lambda t, q=q: t.transpose(
                        psA[:, 128 * q:128 * q + 128], srep[:], identf[:]),
                        waits=[t1, t_red[q]])
                    t_s2b = p.step("V", lambda v, q=q, chq=chq: v.tensor_copy(
                        s2b[:, 128 * q:128 * q + chq],
                        psA[:, 128 * q:128 * q + chq]), waits=[t2])
                if ti_ == 0 and tag == "f1":
                    pass
                t_w2p = {}
                t_mm2 = {}
                for c in range(32):
                    zs = 516 * (c % 2)
                    hs = 514 * (c % 2)
                    tz2 = p.step("G", lambda g, c=c, L=L, zs=zs: g.tensor_tensor(
                        zr[:, zs:zs + L], ybuf[:, 514 * c:514 * c + L],
                        s2b[:, 0:L], ALU.mult),
                        waits=[t_s2b, t_sil[c], t_hq.get(c - 2)])
                    ti2 = p.step("V", lambda v, L=L, zs=zs: v.tensor_scalar(
                        i16r[:, zs:zs + L], zr[:, zs:zs + L], 256.0, None,
                        ALU.add), waits=[tz2, t_hq.get(c - 2)])
                    ta2 = p.step("V", lambda v, L=L, zs=zs:
                        v.scalar_tensor_tensor(
                            arb[:, zs:zs + L], zr[:, zs:zs + L], 255.5,
                            i16r[:, zs:zs + L], ALU.add, ALU.is_ge))
                    t_hq[c] = p.step("V", lambda v, c=c, L=L, zs=zs, hs=hs:
                        v.scalar_tensor_tensor(
                            hqr[:, hs:hs + L], i16r[:, zs:zs + L], -256.0,
                            arb[:, zs:zs + L], ALU.add, ALU.add),
                        waits=[ta2, t_mm2.get(c - 2)])
                    if L > 512:
                        t_hqm = p.step("V", lambda v, c=c, hs=hs: v.tensor_copy(
                            hqmini[:, 2 * c:2 * c + 2], hqr[:, hs + 512:hs + 514]),
                            waits=[t_hq[c]])
                    cslot = c % 2
                    t_w2p[c] = p.dma("Y", lambda y, c=c, cslot=cslot: y.dma_start(
                        out=w2r[:, 1024 * cslot:1024 * cslot + 1024],
                        in_=wq2s_d[c, :, :]),
                        "dc" if cslot == 0 else "dd",
                        waits=[t_mm2.get(c - 2),
                               t_wq_dma if (ti_ == 0 and c < 2 and tag == "f1")
                               else None,
                               (t_mm2_l if (c < 2 and ti_ > 0) else None)])
                    tml = None
                    for j in range(min(4, len(tcs))):
                        psv = [ps1a, ps1b, psA, ps2x][j]
                        for nh in range(2):
                            tml = p.step("T", lambda t, c=c, j=j, cslot=cslot,
                                         psv=psv, hs=hs, nh=nh:
                                t.matmul(psv[:, 512 * nh:512 * nh + 512],
                                         hqr[:, hs + 128 * j:hs + 128 * j + 128],
                                         w2r[:, 1024 * cslot + 512 * nh:
                                             1024 * cslot + 512 * nh + 512],
                                         start=(c == 0), stop=(c == 31)),
                                waits=([t_hq[c], t_w2p[c]]
                                       + ([t_tapC[(ti_, 31)], t_s2b]
                                          + t_ev2 if c == 0 else []))
                                if (j == 0 and nh == 0) else [])
                    t_mm2[c] = tml
                t_mm2_l = t_mm2[31]
                t_ev2 = []
                for j in range(min(4, len(tcs))):
                    psv = [ps1a, ps1b, psA, ps2x][j]
                    ci = T0 // 128 + j
                    t_ev2.append(p.step("V", lambda v, ci=ci, psv=psv, j=j:
                        v.scalar_tensor_tensor(
                            xb[:, D * ci:D * ci + 1024], psv[:, 0:1024],
                            hr2[:, j:j + 1], xb[:, D * ci:D * ci + 1024],
                            ALU.mult, ALU.add),
                        waits=[t_mm2[31], th2]))
            t_mm2m = {}
            t_w2m = {}
            for c in range(32):
                cslot = c % 2
                t_w2m[c] = p.dma("Y", lambda y, c=c, cslot=cslot: y.dma_start(
                    out=w2r[:, 1024 * cslot:1024 * cslot + 1024],
                    in_=wq2s_d[c, :, :]),
                    "dc" if cslot == 0 else "dd",
                    waits=[t_mm2m.get(c - 2), t_mm2_l if c < 2 else None])
                for nh in range(2):
                    t_mm2m[c] = p.step("T", lambda t, c=c, cslot=cslot, nh=nh:
                        t.matmul(
                            ps2x[0:2, 512 * nh:512 * nh + 512],
                            hqmini[:, 2 * c:2 * c + 2],
                            w2r[:, 1024 * cslot + 512 * nh:
                                1024 * cslot + 512 * nh + 512],
                            start=(c == 0), stop=(c == 31)),
                        waits=[t_w2m[c], t_hqm] + (t_ev2 if c == 0 else [])
                        if nh == 0 else [])
            t_out = p.step("V", lambda v: v.scalar_tensor_tensor(
                xb[0:2, 16 * D:17 * D], ps2x[0:2, 0:1024], hr2[0:2, 4:5],
                xb[0:2, 16 * D:17 * D], ALU.mult, ALU.add),
                waits=[t_mm2m[31], th2])
            return [t_out] + t_ev2

        with nc.psum_tensor("ps2x_f1", [128, 1024], F32) as ps2x:
            tF1 = emit_F("f1", t_load)
        p.barrier()
        if "x1" in dbg:
            for i in range(NCH):
                p.dma("Y", lambda y, i=i: y.dma_start(
                    out=dbg["x1"][128 * i:128 * i + CH(i), :],
                    in_=xb[0:CH(i), D * i:D * i + D]), "ds")
            p.barrier()

        with ExitStack() as ast:
            psB = ast.enter_context(nc.psum_tensor("psB", [128, 1024], BF16))
            psKV = ast.enter_context(nc.psum_tensor("psKV", [128, 512], F32))

            t_p = None
            for hpair in range(8):
                for h2 in range(2):
                    h = 2 * hpair + h2
                    tl1 = p.dma("Y", lambda y, h=h: y.dma_start(
                        out=srep[:, 0:64], in_=proj_d[h, 0:128, :]), "dc",
                        waits=[t_p])
                    tl2 = p.dma("Y", lambda y, h=h: y.dma_start(
                        out=srep[:, 64:128], in_=proj_d[h, 128:256, :]), "dc",
                        waits=[t_p])
                    tc1 = p.step("V", lambda v: v.tensor_copy(
                        frb[:, 0:128], srep[:, 0:128]), waits=[tl1, tl2, t_p])
                    tt1 = p.step("T", lambda t: t.transpose(
                        psB[0:64, 0:128], frb[:, 0:64], identb[:]), waits=[tc1])
                    tt2 = p.step("T", lambda t: t.transpose(
                        psB[0:64, 128:256], frb[:, 64:128], identb[:]))
                    t_p = p.step("V", lambda v, hpair=hpair, h2=h2: v.tensor_copy(
                        projTc[64 * h2:64 * h2 + 64,
                               256 * hpair:256 * hpair + 256],
                        psB[0:64, 0:256]), waits=[tt2, ("T", tt2[1])])

            t_bv = p.dma("Y", lambda y: y.dma_start(out=bvrow[:], in_=bv_d[:, :]),
                         "dk", waits=[t_p])
            t_st2 = emit_stats(True, None)
            t_pp2, t_rstd2 = emit_params(False, t_st2)

            tkvz = p.step("V", lambda v: v.memset(kvloc[:], 0.0))
            tvz = p.step("V", lambda v: v.memset(vtm[:], 0.0),
                         waits=[t_wk_done, t_wq_dma])
            tvon = p.step("V", lambda v: v.memset(
                vtm[:, :].rearrange("p (tc hh c) -> p tc hh c", tc=5, c=65)
                [:, :, :, 64:65], 1.0), waits=[tvz])

            def y2T_build(si, T0, L, slot, t_free):
                tcs = chunks_of(T0, L)
                toks = []
                tprev = None
                for j, (ci, ch) in enumerate(tcs):
                    tz = p.step("S", lambda s, ci=ci, ch=ch: s.activation(
                        frb[0:ch, 0:1024], xb[0:ch, D * ci:D * ci + 1024],
                        AF.Identity, scale=prm["prstd"][0:ch, ci:ci + 1],
                        bias=prm["aB"][0:ch, ci:ci + 1]),
                        waits=[t_pp2, t_rstd2, tprev] + (t_free if j == 0 else []))
                    tts = None
                    for k in range(8):
                        tts = p.step("T", lambda t, k=k, ch=ch: t.transpose(
                            psB[:, 128 * k:128 * k + ch],
                            frb[0:ch, 128 * k:128 * k + 128],
                            identb[0:ch, 0:ch]),
                            waits=[tz, tprev] if k == 0 else [])
                    tprev = p.step("V", lambda v, j=j, ch=ch, slot=slot:
                        v.tensor_copy(
                            xqT3[slot][:, :, 128 * j:128 * j + ch],
                            psB[:, 0:1024].rearrange("p (k c) -> p k c", k=8)
                            [:, :, 0:ch]),
                        waits=[tts])
                    toks.append(tprev)
                return toks

            t_kv_ev = [tkvz, tkvz]
            t_qk = {}
            t_xk = {}
            t_y2 = {0: y2T_build(0, TILES[0][0], TILES[0][1], 0,
                                 [("T", p.cnt["T"])])}
            t_xq_dma = []
            for si, (T0, L) in enumerate(TILES):
                slot = si % 2
                tcs = chunks_of(T0, L)
                if si + 1 < len(TILES):
                    t_y2[si + 1] = y2T_build(si + 1, TILES[si + 1][0],
                                             TILES[si + 1][1], 1 - slot,
                                             [t_xk.get((si - 1, 15)),
                                              t_y2[si][-1]])
                t_vev = {}
                t_wv = {}
                t_vcp = {}
                t_dml = {}
                t_ve_last = None
                for vc in range(8):
                    wslot = vc % 2
                    t_wv[vc] = p.dma("Y", lambda y, vc=vc, wslot=wslot:
                        y.dma_start(out=wvr[:, 1024 * wslot:1024 * wslot + 1024],
                                    in_=wvs_d[vc, :, :]),
                        "da" if wslot == 0 else "db",
                        waits=[t_wk_done if (si == 0 and vc < 2) else None,
                               t_dml.get(vc - 1) if vc >= 2 else
                               t_vev.get(vc - 2),
                               (t_xk.get((si - 1, 14 + vc)) if vc < 2 else None)])
                    tml = None
                    for d in range(8):
                        tml = p.step("T", lambda t, d=d, wslot=wslot, slot=slot:
                            t.matmul(ps1b[:, 0:512],
                                     wvr[:, 1024 * wslot + 128 * d:
                                         1024 * wslot + 128 * d + 128],
                                     xqT3[slot][:, d, 0:512],
                                     start=(d == 0), stop=False),
                            waits=([t_wv[vc], t_vcp.get(vc - 1)
                                    if vc >= 1 else t_vev.get(vc - 1)]
                                   + t_y2[si])
                            if d == 0 else [])
                        if L > 512:
                            tml = p.step("T", lambda t, d=d, wslot=wslot,
                                         slot=slot, L=L:
                                t.matmul(ps1b[:, 512:L],
                                         wvr[:, 1024 * wslot + 128 * d:
                                             1024 * wslot + 128 * d + 128],
                                         xqT3[slot][:, d, 512:L],
                                         start=(d == 0), stop=False))
                    b0 = 64 * (vc // 4)
                    bc = 128 * (vc % 4)
                    tml = p.step("T", lambda t, b0=b0, bc=bc: t.matmul(
                        ps1b[:, 0:512], bvrow[b0:b0 + 1, bc:bc + 128],
                        onesw[b0:b0 + 1, 0:512], start=False, stop=True),
                        waits=[t_bv])
                    if L > 512:
                        tml = p.step("T", lambda t, b0=b0, bc=bc, L=L: t.matmul(
                            ps1b[:, 512:L],
                            bvrow[b0:b0 + 1, bc:bc + 128],
                            onesw[b0:b0 + 1, 0:L - 512], start=False, stop=True))
                    tvs = p.step("V", lambda v, L=L: v.tensor_copy(
                        vstg[:, 0:L], ps1b[:, 0:L]), waits=[tml])
                    t_vcp[vc] = tvs
                    t_dml[vc] = tml
                    for j, (ci, ch) in enumerate(tcs):
                        ttv = p.step("T", lambda t, j=j, ch=ch: t.transpose(
                            psB[0:ch, 0:128], vstg[:, 128 * j:128 * j + ch],
                            identb[:]), waits=[tvs, t_ve_last])
                        t_ve_last = p.step("V", lambda v, j=j, ch=ch, vc=vc:
                            v.tensor_copy(
                                vtm[0:ch, 1040 * j + 130 * vc:
                                    1040 * j + 130 * vc + 130]
                                .rearrange("p (hh c) -> p hh c", c=65)
                                [:, :, 0:64],
                                psB[0:ch, 0:128]
                                .rearrange("p (hh c) -> p hh c", c=64)),
                            waits=[ttv, tvon])
                    t_vev[vc] = t_ve_last
                for qkc in range(16):
                    wslot = qkc % 2
                    t_qk[(si, qkc)] = p.dma("Y", lambda y, qkc=qkc, wslot=wslot:
                        y.dma_start(
                            out=wqkr[:, 1024 * wslot:1024 * wslot + 1024],
                            in_=wqks_d[qkc, :, :]),
                        "da" if wslot == 0 else "db",
                        waits=[(t_vev.get(6 + qkc) if qkc < 2 else None),
                               t_xk.get((si, qkc - 2))])
                    tml = None
                    for d in range(8):
                        tml = p.step("T", lambda t, d=d, wslot=wslot, slot=slot, L=L:
                            t.matmul(ps1a[:, 0:512],
                                     wqkr[:, 1024 * wslot + 128 * d:
                                          1024 * wslot + 128 * d + 128],
                                     xqT3[slot][:, d, 0:512],
                                     start=(d == 0), stop=(d == 7)),
                            waits=[t_qk[(si, qkc)], t_xk.get((si, qkc - 1))]
                            if d == 0 else [])
                        if L > 512:
                            tml = p.step("T", lambda t, d=d, wslot=wslot,
                                         slot=slot, L=L:
                                t.matmul(ps1a[:, 512:L],
                                         wqkr[:, 1024 * wslot + 128 * d:
                                              1024 * wslot + 128 * d + 128],
                                         xqT3[slot][:, d, 512:L],
                                         start=(d == 0), stop=(d == 7)))
                    if qkc < 8:
                        ss_ = (qkc % 2) * 516
                        txe = p.step("S", lambda s, qkc=qkc, L=L, ss_=ss_:
                            s.activation(
                                xstg[:, ss_:ss_ + L],
                                ps1a[:, 0:L], AF.Identity, scale=DN,
                                bias=bqkvDN[:, qkc:qkc + 1]),
                            waits=[tml, t_bqdn,
                                   (t_xq_dma[-2] if len(t_xq_dma) >= 2
                                    else None)])
                        t_xk[(si, qkc)] = txe
                        t_xq_dma.append(p.dma("Y", lambda y, qkc=qkc, T0=T0,
                                              L=L, ss_=ss_:
                            y.dma_start(out=xqs_d[qkc, :, T0:T0 + L],
                                        in_=xstg[:, ss_:ss_ + L]),
                            "dq" if qkc % 2 == 0 else "dp",
                            waits=[txe]))
                    else:
                        kc = qkc - 8
                        t_xkev = p.step("S", lambda s, L=L, qkc=qkc: s.activation(
                            xkr[:, 0:L], ps1a[:, 0:L], AF.Identity,
                            scale=DN, bias=bqkvDN[:, qkc:qkc + 1]),
                            waits=[tml, t_bqdn])
                        t_xk[(si, qkc)] = t_xkev
                        kvp_by_par = {0: {}, 1: {}}
                        fmtk = {}

                        def emit_fmk(j, kc=kc, tcs=tcs, t_xkev=t_xkev,
                                     t_v7=t_vev.get(7)):
                            chq = tcs[j][1]
                            psf = psA if j % 2 == 0 else ps1b
                            p.step("T", lambda t, kc=kc, j=j, chq=chq, psf=psf:
                                t.matmul(psf[0:chq, 0:256],
                                         xkr[0:64, 128 * j:128 * j + chq],
                                         projTc[0:64, 256 * kc:256 * kc + 256],
                                         start=True, stop=True),
                                waits=[t_xkev, t_v7])
                            fmtk[j] = p.step("T", lambda t, kc=kc, j=j, chq=chq,
                                             psf=psf:
                                t.matmul(psf[0:chq, 256:512],
                                         xkr[64:128, 128 * j:128 * j + chq],
                                         projTc[64:128, 256 * kc:256 * kc + 256],
                                         start=True, stop=True))
                        emit_fmk(0)
                        if len(tcs) > 1:
                            emit_fmk(1)
                        for j, (ci, chq) in enumerate(tcs):
                            psf = psA if j % 2 == 0 else ps1b
                            kprv = kpr if j % 2 == 0 else w2r[:, 1536:2048]
                            trd = p.step("V", lambda v, chq=chq, psf=psf:
                                v.tensor_reduce(
                                    negmx[0:chq, 0:2],
                                    psf[0:chq, 0:512]
                                    .rearrange("p (h m) -> p h m", h=2),
                                    AX.X, ALU.max, negate=True),
                                waits=[fmtk[j]])
                            tb2 = p.step("V", lambda v, chq=chq: v.tensor_scalar(
                                negmx2[0:chq, :], negmx[0:chq, :], -LN16, None,
                                ALU.add))
                            if ci == 0:
                                tb2 = p.step("V", lambda v, chq=chq:
                                    v.tensor_tensor(
                                        negmx2[0:chq, :], negmx2[0:chq, :],
                                        emf[0:chq, :], ALU.add))
                            if ci == 16:
                                tb2 = p.step("V", lambda v, chq=chq:
                                    v.tensor_tensor(
                                        negmx2[0:chq, :], negmx2[0:chq, :],
                                        eml[0:chq, :], ALU.add))
                            te1 = p.step("S", lambda s, chq=chq, psf=psf,
                                         kprv=kprv: s.activation(
                                kprv[0:chq, 0:256], psf[0:chq, 0:256],
                                AF.Exp, bias=negmx2[0:chq, 0:1]),
                                waits=[tb2] + list(kvp_by_par[j % 2].values()))
                            te2 = p.step("S", lambda s, chq=chq, psf=psf,
                                         kprv=kprv: s.activation(
                                kprv[0:chq, 256:512], psf[0:chq, 256:512],
                                AF.Exp, bias=negmx2[0:chq, 1:2]))
                            tep = p.step("V", lambda v, chq=chq, kprv=kprv:
                                v.tensor_scalar(
                                    kprv[0:chq, 0:512], kprv[0:chq, 0:512],
                                    EPS_PHI, None, ALU.add), waits=[te1, te2])
                            kvp_by_par[j % 2] = {}
                            for h2 in range(2):
                                h = 2 * kc + h2
                                psv = psKV if h2 == 0 else psA
                                off = 0 if h2 == 0 else 512
                                tkv = p.step("T", lambda t, j=j, h=h, h2=h2,
                                             chq=chq, psv=psv, off=off,
                                             kprv=kprv,
                                             last=(j == len(tcs) - 1):
                                    t.matmul(psv[0:65, off:off + 256],
                                             vtm[0:chq, 1040 * j + 65 * h:
                                                 1040 * j + 65 * h + 65],
                                             kprv[0:chq, 256 * h2:
                                                  256 * h2 + 256],
                                             start=(j == 0),
                                             stop=last),
                                    waits=[tep, t_vev.get(7)]
                                    + ([t_kv_ev[-2], t_kv_ev[-1]]
                                       if j == 0 else []))
                                kvp_by_par[j % 2][h2] = tkv
                                if j == len(tcs) - 1:
                                    t_kv_ev.append(p.step("V",
                                        lambda v, h=h, psv=psv, off=off:
                                        v.scalar_tensor_tensor(
                                            kvloc[:, 256 * h:256 * h + 256],
                                            psv[0:65, off:off + 256], 1.0,
                                            kvloc[:, 256 * h:256 * h + 256],
                                            ALU.mult, ALU.add),
                                        waits=[tkv, tkvz]))
                            if j + 2 < len(tcs):
                                emit_fmk(j + 2)

            t_ci = p.dma("Y", lambda y: y.dma_start(out=ccin_d[:, :], in_=kvloc[:]),
                         "dk", waits=[t_kv_ev[-1], t_kv_ev[-2]])
            t_cc = p.step("G", lambda g: g.collective_compute(
                "AllReduce", ALU.add,
                replica_groups=[[0, 1], [2, 3], [4, 5], [6, 7]],
                ins=[ccin_d[:, :]], outs=[ccout_d[:, :]]),
                waits=[t_ci, t_wk_done])
            t_co = p.dma("Y", lambda y: y.dma_start(out=kvloc[:], in_=ccout_d[:, :]),
                         "dk", waits=[t_cc])

            t_kvf = {}
            tprev = None
            for h in range(16):
                tc1 = p.step("V", lambda v, h=h: v.tensor_copy(
                    kvb[0:65, 0:256], kvloc[:, 256 * h:256 * h + 256]),
                    waits=[t_co, tprev])
                tt1 = p.step("T", lambda t: t.transpose(
                    psB[:, 0:65], kvb[0:65, 0:128], identb[0:65, 0:65]),
                    waits=[tc1, tprev])
                tt2 = p.step("T", lambda t: t.transpose(
                    psB[:, 128:193], kvb[0:65, 128:256], identb[0:65, 0:65]))
                p.step("V", lambda v, h=h: v.tensor_copy(
                    kvfm0[:, 65 * h:65 * h + 65], psB[:, 0:65]),
                    waits=[tt2, ("T", tt2[1])])
                tprev = p.step("V", lambda v, h=h: v.tensor_copy(
                    kvfm1[:, 65 * h:65 * h + 65], psB[:, 128:193]))
                t_kvf[h] = tprev
            t_kv_ready = tprev

            t_wo = None
            for b in range(8):
                t_wo = p.dma("Y", lambda y, b=b: y.dma_start(
                    out=woutp[b][:, 0:1024], in_=wos_d[b, :, :]), "dw",
                    waits=[t_kv_ev[-1], t_kv_ev[-2], t_wk_done, t_kv_ready])
            t_bo = p.dma("Y", lambda y: y.dma_start(out=bvrow[:], in_=bout_d[:, :]),
                         "dw", waits=[t_kv_ev[-1], t_kv_ev[-2], t_kv_ready])
            t_wo_done = ("dw", p.cnt["dw"])

            t_res = None
            t_at_prev = None
            t_xql = {}
            t_fml = {}
            for i in range(NCH):
                ch = CH(i)
                lslot = i % 2
                t_xql[i] = p.dma("Y", lambda y, i=i, ch=ch, lslot=lslot:
                    y.dma_start(
                        out=xqld[:, 1024 * lslot:1024 * lslot + 8 * ch]
                        .rearrange("p (qc j) -> p qc j", qc=8),
                        in_=xqs_d[:, :, 128 * i:128 * i + ch]
                        .rearrange("qc p j -> p qc j")),
                    "dq" if i % 2 == 0 else "dp",
                    waits=[("dq", p.cnt["dq"]) if i == 0 else None,
                           ("dp", p.cnt["dp"]) if i == 0 else None,
                           t_fml.get(i - 2)])
                t_div = []
                fmtok = {}

                def emit_fm(qc, i=i, ch=ch, lslot=lslot):
                    psfm = psA if qc % 2 == 0 else ps1a
                    p.step("T", lambda t, qc=qc, i=i, ch=ch, lslot=lslot,
                           psfm=psfm:
                        t.matmul(
                            psfm[0:ch, 0:256],
                            xqld[0:64, 1024 * lslot + 128 * qc:
                                 1024 * lslot + 128 * qc + ch],
                            projTc[0:64, 256 * qc:256 * qc + 256],
                            start=True, stop=True),
                        waits=[t_kv_ready, t_at_prev, t_xql[i]]
                        + ([("S", p.cnt["S"])] if qc < 2 and i == 0 else []))
                    fmtok[qc] = p.step("T", lambda t, qc=qc, ch=ch, lslot=lslot,
                                       psfm=psfm:
                        t.matmul(
                            psfm[0:ch, 256:512],
                            xqld[64:128, 1024 * lslot + 128 * qc:
                                 1024 * lslot + 128 * qc + ch],
                            projTc[64:128, 256 * qc:256 * qc + 256],
                            start=True, stop=True))
                emit_fm(0)
                emit_fm(1)
                for qc in range(8):
                    psfm = psA if qc % 2 == 0 else ps1a
                    qprv = qpr if qc % 2 == 0 else xqT[:, 3104:3616]
                    qfmv = qfm if qc % 2 == 0 else xqT[:, 3616:4128]
                    trd = p.step("V", lambda v, ch=ch, psfm=psfm: v.tensor_reduce(
                        negmx[0:ch, 0:2],
                        psfm[0:ch, 0:512].rearrange("p (h m) -> p h m", h=2),
                        AX.X, ALU.max, negate=True), waits=[fmtok[qc]])
                    tb2 = p.step("V", lambda v, ch=ch: v.tensor_scalar(
                        negmx2[0:ch, :], negmx[0:ch, :], -LN16, None, ALU.add))
                    te1 = p.step("S", lambda s, ch=ch, psfm=psfm, qprv=qprv:
                        s.activation(
                            qprv[0:ch, 0:256], psfm[0:ch, 0:256], AF.Exp,
                            bias=negmx2[0:ch, 0:1]), waits=[tb2])
                    te2 = p.step("S", lambda s, ch=ch, psfm=psfm, qprv=qprv:
                        s.activation(
                            qprv[0:ch, 256:512], psfm[0:ch, 256:512], AF.Exp,
                            bias=negmx2[0:ch, 1:2]))
                    tep = p.step("V", lambda v, ch=ch, qprv=qprv: v.tensor_scalar(
                        qprv[0:ch, 0:512], qprv[0:ch, 0:512], EPS_PHI, None,
                        ALU.add), waits=[te1, te2])
                    tts = None
                    for b in range(4):
                        tts = p.step("T", lambda t, b=b, ch=ch, qprv=qprv:
                            t.transpose(
                                psB[:, 128 * b:128 * b + ch],
                                qprv[0:ch, 128 * b:128 * b + 128],
                                identb[0:ch, 0:ch]),
                            waits=[tep] if b == 0 else [])
                    tqf = p.step("V", lambda v, ch=ch, qfmv=qfmv: v.tensor_copy(
                        qfmv[:, 0:512].rearrange("p (b c) -> p b c", b=4)
                        [:, :, 0:ch],
                        psB[:, 0:512].rearrange("p (b c) -> p b c", b=4)
                        [:, :, 0:ch]),
                        waits=[tts])
                    for h2 in range(2):
                        h = 2 * qc + h2
                        psv = psKV if h2 == 0 else psA
                        off = 0 if h2 == 0 else 512
                        tnm = None
                        for fc in range(2):
                            kvf = kvfm0 if fc == 0 else kvfm1
                            tnm = p.step("T", lambda t, h=h, h2=h2, fc=fc, ch=ch,
                                         psv=psv, off=off, kvf=kvf, qfmv=qfmv:
                                t.matmul(psv[0:ch, off:off + 65],
                                         qfmv[:, 128 * (2 * h2 + fc):
                                              128 * (2 * h2 + fc) + ch],
                                         kvf[:, 65 * h:65 * h + 65],
                                         start=(fc == 0), stop=(fc == 1)),
                                waits=[tqf, t_kvf[h],
                                       t_div[-1] if t_div else None]
                                if fc == 0 else [])
                        p.step("V", lambda v, ch=ch, psv=psv, off=off:
                            v.tensor_scalar(dent[0:ch, :],
                                            psv[0:ch, off + 64:off + 65],
                                            1e-6, None, ALU.add), waits=[tnm])
                        p.step("V", lambda v, ch=ch: v.reciprocal(
                            recr[0:ch, :], dent[0:ch, :]))
                        t_div.append(p.step("V", lambda v, h=h, ch=ch, psv=psv,
                                            off=off:
                            v.tensor_scalar(
                                atile[0:ch, 64 * h:64 * h + 64],
                                psv[0:ch, off:off + 64],
                                recr[0:ch, 0:1], None, ALU.mult)))
                    if qc + 2 < 8:
                        emit_fm(qc + 2)
                t_fml[i] = fmtok[7]
                tts = None
                for b in range(8):
                    tts = p.step("T", lambda t, b=b, ch=ch: t.transpose(
                        psB[:, 128 * b:128 * b + ch],
                        atile[0:ch, 128 * b:128 * b + 128], identb[0:ch, 0:ch]),
                        waits=([t_div[-1], t_div[-2]] if b == 0 else []))
                tat = p.step("V", lambda v, ch=ch: v.tensor_copy(
                    aTl[:, 0:1024].rearrange("p (b c) -> p b c", b=8)[:, :, 0:ch],
                    psB[:, 0:1024].rearrange("p (b c) -> p b c", b=8)[:, :, 0:ch]),
                    waits=[tts])
                t_at_prev = tat
                tml = None
                for b in range(8):
                    for nh in range(2):
                        tml = p.step("T", lambda t, b=b, ch=ch, nh=nh: t.matmul(
                            ps1b[0:ch, 512 * nh:512 * nh + 512],
                            aTl[:, 128 * b:128 * b + ch],
                            woutp[b][:, 512 * nh:512 * nh + 512],
                            start=(b == 0), stop=False),
                            waits=[tat, t_wo_done, t_res]
                            if (b == 0 and nh == 0) else [])
                for q4 in range(2):
                    tml = p.step("T", lambda t, ch=ch, q4=q4: t.matmul(
                        ps1b[0:ch, 512 * q4:512 * q4 + 512],
                        allones[64 * q4:64 * q4 + 1, 0:ch],
                        bvrow[64 * q4:64 * q4 + 1, 0:512],
                        start=False, stop=True),
                        waits=[t_bo] if q4 == 0 else [])
                t_res = p.step("V", lambda v, i=i, ch=ch: v.scalar_tensor_tensor(
                    xb[0:ch, D * i:D * i + 1024], ps1b[0:ch, 0:1024], 1.0,
                    xb[0:ch, D * i:D * i + 1024], ALU.mult, ALU.add),
                    waits=[tml])

        p.barrier()
        if "x2" in dbg:
            for i in range(NCH):
                p.dma("Y", lambda y, i=i: y.dma_start(
                    out=dbg["x2"][128 * i:128 * i + CH(i), :],
                    in_=xb[0:CH(i), D * i:D * i + D]), "ds")
            p.barrier()

        with nc.psum_tensor("ps2x_f2", [128, 1024], F32) as ps2x:
            emit_F("f2", None)
        p.barrier()
        for i in range(NCH):
            p.dma("Y", lambda y, i=i: y.dma_start(
                out=out_d[128 * i:128 * i + CH(i), :],
                in_=xb[0:CH(i), D * i:D * i + D]), "ds")
        p.add("Y", lambda y: y.nop(), waits=[("ds", p.cnt["ds"])])

        blk = st.enter_context(nc.Block())
        blk.tensor(lambda eng: [f(eng) for f in p.q["T"]])
        blk.vector(lambda eng: [f(eng) for f in p.q["V"]])
        blk.scalar(lambda eng: [f(eng) for f in p.q["S"]])
        blk.gpsimd(lambda eng: [f(eng) for f in p.q["G"]])
        blk.sync(lambda eng: [f(eng) for f in p.q["Y"]])

    return nc


def kernel(**inputs):
    x = np.ascontiguousarray(np.asarray(inputs["L0_attn_x"], np.float32))
    w_qkv = np.ascontiguousarray(np.asarray(inputs["L0_attn_w_qkv"], np.float32))
    b_qkv = np.asarray(inputs["L0_attn_b_qkv"], np.float32)
    w_out = np.ascontiguousarray(np.asarray(inputs["L0_attn_w_out"], np.float32))
    b_out = np.asarray(inputs["L0_attn_b_out"], np.float32)
    proj = np.ascontiguousarray(np.asarray(inputs["L0_attn_proj_matrix"], np.float32))
    w1 = np.ascontiguousarray(np.asarray(inputs["L0_ffn_w1"], np.float32))
    conv_w = np.asarray(inputs["L0_ffn_conv_w"], np.float32).reshape(HID, 3)
    conv_b = np.asarray(inputs["L0_ffn_conv_b"], np.float32)
    w2 = np.ascontiguousarray(np.asarray(inputs["L0_ffn_w2"], np.float32))

    cw = [np.ascontiguousarray(conv_w[:, j].reshape(32, 128).T) for j in range(3)]
    cbt = np.ascontiguousarray(conv_b.reshape(32, 128).T)
    bqkv_t = np.ascontiguousarray(b_qkv.reshape(24, 128).T)
    bv_row = np.zeros((128, 512), np.float32)
    bv_row[0, :] = b_qkv[2048:2560]
    bv_row[64, :] = b_qkv[2560:3072]
    bout_row = np.zeros((128, 512), np.float32)
    bout_row[0, :] = b_out[0:512]
    bout_row[64, :] = b_out[512:1024]

    in_maps = []
    for core in range(8):
        b = core // 2
        half = core % 2
        lo = half * 2046
        xs = np.ascontiguousarray(x[b, lo:lo + NTOK])
        emaskf = np.zeros((128, 2), np.float32)
        emaskl = np.zeros((128, 2), np.float32)
        if half == 1:
            emaskf[0:2, :] = -1e30
        else:
            emaskl[0:2, :] = -1e30
        in_maps.append({
            "x0": xs, "w1": w1, "w2": w2,
            "cw0": cw[0], "cw1": cw[1], "cw2": cw[2], "cb": cbt,
            "wqkv": w_qkv, "bqkv": bqkv_t, "bv": bv_row,
            "wout": w_out, "bout": bout_row, "proj": proj,
            "emaskf": emaskf, "emaskl": emaskl,
        })

    nc = build_program(tuple(DEBUG))
    res = run_bass_kernel_spmd(nc, in_maps, core_ids=list(range(8)))

    out = np.empty((4, 4096, D), np.float32)
    for core in range(8):
        b = core // 2
        half = core % 2
        full = np.asarray(res.results[core]["xout"], np.float32)
        own = full[0:2048] if half == 0 else full[2:2050]
        out[b, half * 2048:(half + 1) * 2048] = own
    if DEBUG:
        kernel._dbg = [{k: np.asarray(v) for k, v in r.items()}
                       for r in res.results]
    return out
